# version 16
# baseline (speedup 1.0000x reference)
import contextlib
import numpy as np

import concourse.bass as bass
import concourse.mybir as mybir
import concourse.library_config as libcfg
from concourse.bass_utils import run_bass_kernel_spmd
from concourse.library_overlay import lower_extended_insts

F32 = mybir.dt.float32
I32 = mybir.dt.int32
I16 = mybir.dt.int16
U32 = mybir.dt.uint32
Alu = mybir.AluOpType
Act = mybir.ActivationFunctionType
AX = mybir.AxisListType.X

TOK = 2048
NB = 17
TA = NB * 128
D = 1024
KC = 8
EXP = 8
DFF = 4096
CAP = 384
CAPF = 336
CAPW = CAP // 16
NIT = 32
WINM = 136
INV2T = 1.0 / 0.56
EPS_LN = 1e-5
LOOSE = 1.0 - 1e-4
SENT = 2048
NTILE = 128
XGW = 1088

_CACHE = {}


def build_program(debug_taps=()):
    nc = bass.Bass("TRN2", target_bir_lowering=False)

    x_att = nc.dram_tensor("x_att", [TA, D], F32, kind="ExternalInput").ap()
    x_full_T = nc.dram_tensor("x_full_T", [D, 16384], F32, kind="ExternalInput").ap()
    Wq_d = nc.dram_tensor("Wq", [D, D], F32, kind="ExternalInput").ap()
    Wk_d = nc.dram_tensor("Wk", [D, D], F32, kind="ExternalInput").ap()
    Wv_d = nc.dram_tensor("Wv", [D, D], F32, kind="ExternalInput").ap()
    Wo_d = nc.dram_tensor("Wo", [D, D], F32, kind="ExternalInput").ap()
    rw_d = nc.dram_tensor("rw", [D, EXP], F32, kind="ExternalInput").ap()
    rb_d = nc.dram_tensor("rbias", [1, EXP], F32, kind="ExternalInput").ap()
    w1_d = nc.dram_tensor("w1", [EXP, D, DFF], F32, kind="ExternalInput").ap()
    b1_d = nc.dram_tensor("b1", [EXP, DFF], F32, kind="ExternalInput").ap()
    w2_d = nc.dram_tensor("w2", [EXP, DFF, D], F32, kind="ExternalInput").ap()
    b2_d = nc.dram_tensor("b2", [EXP, D], F32, kind="ExternalInput").ap()
    ident_d = nc.dram_tensor("ident", [128, 128], F32, kind="ExternalInput").ap()
    maskA_d = nc.dram_tensor("maskA", [128, 128], F32, kind="ExternalInput").ap()
    mask0_d = nc.dram_tensor("mask0", [128, 128], F32, kind="ExternalInput").ap()
    iota_d = nc.dram_tensor("iota16", [128, 16], F32, kind="ExternalInput").ap()
    ioslot_d = nc.dram_tensor("ioslot", [16, CAPW], F32, kind="ExternalInput").ap()
    bd8_d = nc.dram_tensor("bd8", [128, EXP], F32, kind="ExternalInput").ap()
    bdT_d = nc.dram_tensor("bdT", [8, 128], F32, kind="ExternalInput").ap()
    ones_d = nc.dram_tensor("onesrow", [1, 128], F32, kind="ExternalInput").ap()

    y_out = nc.dram_tensor("y", [TOK, D], F32, kind="ExternalOutput").ap()
    taps = {}
    tapdefs = {"t_xn": [TA, D], "t_tau": [128, 2 * EXP], "t_scores": [TOK, EXP],
               "t_s": [TOK, D], "t_g": [TOK, D], "t_idx": [16, EXP * CAPW],
               "t_srch": [128, 1024]}
    for name in debug_taps:
        taps[name] = nc.dram_tensor(name, tapdefs[name], F32, kind="ExternalOutput").ap()

    xg_stage = nc.dram_tensor("xg_stage", [SENT + 1, XGW], F32, kind="Internal").ap()
    g_stage = nc.dram_tensor("g_stage", [SENT + 1, D], F32, kind="Internal").ap()
    s_stage = nc.dram_tensor("s_stage", [TOK, D], F32, kind="Internal").ap()

    ctx = contextlib.ExitStack()
    with ctx:
        AR = 51300
        arena = ctx.enter_context(nc.sbuf_tensor("arena", [128, AR], F32)).ap()

        def A(off, n):
            assert off + n <= AR, (off, n)
            return arena[:, off:off + n]

        def A3(off, dims, pcnt=128):
            return bass.AP(arena.tensor, off, [[AR, pcnt]] + dims)

        XA, XT, QKV, WB, SM, PS = 0, 17408, 34816, 41344, 47488, 49536

        xa = A(XA, NB * D)
        xnT = A(XT, KC * TA)

        xT = [A(QKV, 2048), A(QKV + 2048, 2048)]
        srch = A(XA + 6 * TA, 1024)
        gescrD = A(XA + 6 * TA + 1024, 1024)
        lgt = A(XA + 6 * TA + 2048, 96)
        sgP = A(XA + 7 * TA, 1024)
        lgTst = A(XA + 7 * TA + 1024, 1024)
        gescr = A(XA + 7 * TA + 2048, 64)
        fixs = A(XA + 7 * TA + 2112, 64)

        qT = A(QKV, 2048)
        kT = A(QKV + 2048, TA)
        vC = A(QKV + 2048 + TA, 18 * 128)
        wqb = [A(WB, 1024), A(WB + 3072, 1024)]
        wkb = [A(WB + 1024, 1024), A(WB + 4096, 1024)]
        wvb = [A(WB + 2048, 1024), A(WB + 5120, 1024)]

        woSM = A(SM, 2 * D)
        woWB = A(WB, 6 * D)
        stile = [A(QKV, D), A(QKV + D, D)]

        def woap(kc):
            return woSM[:, kc * D:(kc + 1) * D] if kc < 2 else \
                woWB[:, (kc - 2) * D:(kc - 1) * D]

        h1t = A(XA, 32 * CAP)
        toksT = A(XA + 32 * CAP, KC * CAP)
        b1pp = A(XA + 32 * CAP + KC * CAP, 32)
        w1r = [A(XT, 4096), A(XT + 4096, 4096)]
        w2r = [A(XT + 8192, 2048), A(XT + 8192 + 2048, 2048)]
        tg = [A(QKV, 3 * XGW), A(QKV + 3 * XGW, 3 * XGW)]
        tg0a = A(XT + 12288, 3 * XGW)

        def tgbuf(e):
            return tg0a if e == 0 else tg[e % 2]
        eob = A(WB, 3 * 1024)
        b2b = A(WB + 3 * 1024, D)
        zero = A(SM, XGW)

        fg = [A(QKV, D), A(QKV + D, D), A(XT + 12288, D)]
        fx = [A(QKV + 2 * D, D), A(QKV + 3 * D, D), A(XT + 12288 + D, D)]
        fs = [A(QKV + 4 * D, D), A(QKV + 5 * D, D), A(XT + 12288 + 2 * D, D)]
        fy = A(SM, D)
        fsc = A(SM + D, D)

        _p = [PS]

        def P_(n):
            r = A(_p[0], n)
            _p[0] += n
            return r

        identt = P_(128)
        maskAt = P_(128)
        mask0t = P_(128)
        iota16t = P_(16)
        bd8t = P_(EXP)
        bdTt = P_(128)
        onesrt = P_(128)
        rw9 = P_(KC * 9 + 8)
        rwsumb = P_(EXP)
        rbb = P_(EXP)
        taub = P_(EXP)
        taulb = P_(EXP)
        gm_all = P_(128)
        ssum = P_(NB + 1)
        sqsum = P_(NB + 1)
        mnegv = P_(NB + 1)
        varv = P_(NB + 1)
        rstdv = P_(NB + 1)
        tA_ = P_(NB + 1)
        tB_ = P_(NB + 1)
        tC_ = P_(NB + 1)
        g_sq = P_(EXP)
        g_mneg = P_(EXP)
        g_var = P_(EXP)
        g_rstd = P_(EXP)
        g_nm = P_(EXP)
        g_rs = P_(EXP)
        g_ri = P_(EXP)
        g_sA = P_(EXP)
        g_sB = P_(EXP)
        g_sC = P_(EXP)
        lo8 = [P_(1), P_(1)]
        hi8 = [P_(1), P_(1)]
        mid8 = P_(1)
        midpp = P_(1)
        cntp = P_(1)
        cnt8s = P_(1)
        taur = P_(8)
        selvt = P_(16)
        nmx2 = [P_(1), P_(1)]
        rsum2 = [P_(1), P_(1)]
        rinv2 = [P_(1), P_(1)]
        wtsb2 = [P_(128), P_(128)]
        wtsTa2 = [P_(128), P_(128)]
        attnT = [A(XA + kc * TA, TA) for kc in range(KC)]
        assert _p[0] <= AR, _p[0]

        selv_sb = ctx.enter_context(nc.sbuf_tensor("selv_sb", [16, 128], F32)).ap()
        idxraw = ctx.enter_context(nc.sbuf_tensor("idxraw", [16, EXP * CAPW], F32)).ap()
        idxsel = ctx.enter_context(nc.sbuf_tensor("idxsel", [16, EXP * CAPW], F32)).ap()
        c2048 = ctx.enter_context(nc.sbuf_tensor("c2048", [16, CAPW], F32)).ap()
        ioslt = ctx.enter_context(nc.sbuf_tensor("ioslt", [16, CAPW], F32)).ap()
        vmask = ctx.enter_context(nc.sbuf_tensor("vmask", [16, CAPW], U32)).ap()
        pred8 = ctx.enter_context(nc.sbuf_tensor("pred8", [8, 2], U32)).ap()
        idx16 = ctx.enter_context(nc.sbuf_tensor("idx16", [128, EXP * CAPW], I16)).ap()
        nfu = ctx.enter_context(nc.sbuf_tensor("nfu", [16, 2], U32)).ap()
        nfb = ctx.enter_context(nc.sbuf_tensor("nfb", [16, 2], F32)).ap()
        rbrow = ctx.enter_context(nc.sbuf_tensor("rbrow", [1, EXP], F32)).ap()
        b2row = ctx.enter_context(nc.sbuf_tensor("b2row", [1, D], F32)).ap()
        b1tmp = ctx.enter_context(nc.sbuf_tensor("b1tmp", [32, 128], F32)).ap()

        PBk = [ctx.enter_context(nc.psum_tensor(f"ps{i}", [128, 512], F32)).ap()
               for i in range(8)]

        class SemCtr:
            def __init__(self, name):
                self.h = ctx.enter_context(nc.semaphore(name))
                self.n = 0

            def inc(self, instr, k=1):
                instr.then_inc(self.h, k)
                self.n += k
                return self.n

            def dma(self, instr):
                return self.inc(instr, 16)

        sIn = SemCtr("sIn")
        sXa = SemCtr("sXa")
        sXb = SemCtr("sXb")
        sXc = SemCtr("sXc")
        sWc = SemCtr("sWc")
        sVc = SemCtr("sVc")
        sOutB = SemCtr("sOutB")
        sOutC = SemCtr("sOutC")
        sW0 = SemCtr("sW0")
        sW1 = SemCtr("sW1")
        sS0 = SemCtr("sS0")
        sS1 = SemCtr("sS1")
        sV0 = SemCtr("sV0")
        sV1 = SemCtr("sV1")
        sSt = SemCtr("sSt")
        sSt2 = SemCtr("sSt2")
        sOut = SemCtr("sOut")
        sBm = SemCtr("sBm")
        sGd = SemCtr("sGd")
        sGd1 = SemCtr("sGd1")
        sGs = SemCtr("sGs")
        sMM = SemCtr("sMM")
        sV = SemCtr("sV")
        sVg = SemCtr("sVg")
        sA = SemCtr("sA")
        sAe = SemCtr("sAe")
        sGp = SemCtr("sGp")
        XS = [sXa, sXb]
        WS = [sW0, sW1]
        SS = [sS0, sS1]
        VS = [sV0, sV1]

        blk = ctx.enter_context(nc.Block())

        def vchain(v, instr, ctr):
            ctr.inc(instr)
            v.wait_ge(ctr.h, ctr.n)

        RSQ_OPS = 3 + 4 * 4 + 1

        def emit_rsqrt(v, ctr, x_ap, out_ap, ta, tb, tc):
            xi = x_ap.bitcast(I32)
            yi = ta.bitcast(I32)
            vchain(v, v.tensor_scalar(yi, xi, 1, None, op0=Alu.logical_shift_right), ctr)
            vchain(v, v.tensor_scalar(yi, yi, 0, None, op0=Alu.bitwise_not), ctr)
            vchain(v, v.tensor_scalar(yi, yi, 0x5f3759e0, None, op0=Alu.add), ctr)
            for _ in range(4):
                vchain(v, v.tensor_tensor(tb, ta, ta, Alu.mult), ctr)
                vchain(v, v.tensor_tensor(tc, tb, x_ap, Alu.mult), ctr)
                vchain(v, v.tensor_scalar(tc, tc, -0.5, 1.5, op0=Alu.mult, op1=Alu.add), ctr)
                vchain(v, v.tensor_tensor(ta, ta, tc, Alu.mult), ctr)
            vchain(v, v.tensor_copy(out_ap, ta), ctr)

        if True:
            @blk.vector
            def _(v):
                v.memset(zero, 0.0)
                v.memset(c2048, float(SENT))
                v.memset(idx16, 0)
                v.memset(idxraw, -1.0)
                v.wait_ge(sIn.h, 16 * 15)
                sV.inc(v.memset(A3(rw9.offset + EXP, [[9, KC], [1, 1]]), 1.0))

            @blk.sync
            def _(sy):
                sIn.dma(sy.dma_start(out=identt, in_=ident_d))
                sIn.dma(sy.dma_start(out=maskAt, in_=maskA_d))
                sIn.dma(sy.dma_start(out=mask0t, in_=mask0_d))
                sIn.dma(sy.dma_start(out=iota16t, in_=iota_d))
                sIn.dma(sy.dma_start(out=bd8t, in_=bd8_d))
                sIn.dma(sy.dma_start(out=bdTt[0:8, :], in_=bdT_d))
                sIn.dma(sy.dma_start(out=onesrt[0:1, :], in_=ones_d))
                sIn.dma(sy.dma_start(out=ioslt, in_=ioslot_d))
                sIn.dma(sy.dma_start(out=rbrow, in_=rb_d))
                sIn.dma(sy.dma_start(
                    out=A3(rw9.offset, [[9, KC], [1, EXP]]),
                    in_=rw_d.rearrange("(kc p) e -> p kc e", p=128)))
                for q in range(4):
                    nb_q = [4, 4, 4, 5][q]
                    ob_q = [0, 4, 8, 12][q]
                    sIn.dma(sy.dma_start(
                        out=A3(XA + ob_q * D, [[D, nb_q], [1, D]]),
                        in_=bass.AP(x_att.tensor, ob_q * 128 * D,
                                    [[D, 128], [128 * D, nb_q], [1, D]])))
                sIn.dma(sy.dma_start(out=b2row[0:1, 0:EXP], in_=rb_d))
                assert sIn.n == 16 * 15
        p0_v = sV.n

        ssb8 = [sS0.n, sS1.n]
        if True:
            @blk.sync
            def _(sy):
                for c in range(2):
                    sl = c % 2
                    SS[sl].dma(sy.dma_start(
                        out=A3(wqb[sl].offset, [[128, KC], [1, 128]]),
                        in_=Wq_d[:, c * 128:(c + 1) * 128].rearrange(
                            "(kc p) n -> p kc n", p=128)))
                    SS[sl].dma(sy.dma_start(
                        out=A3(wkb[sl].offset, [[128, KC], [1, 128]]),
                        in_=Wk_d[:, c * 128:(c + 1) * 128].rearrange(
                            "(kc p) n -> p kc n", p=128)))
                    SS[sl].dma(sy.dma_start(
                        out=A3(wvb[sl].offset, [[128, KC], [1, 128]]),
                        in_=Wv_d[:, c * 128:(c + 1) * 128].rearrange(
                            "(kc p) n -> p kc n", p=128)))
                for hx in range(2):
                    WS[hx % 2].dma(sy.dma_start(
                        out=A3(xT[hx % 2].offset, [[256, KC], [1, 256]]),
                        in_=bass.AP(x_full_T.tensor, hx * 256,
                                    [[16384, 128], [128 * 16384, KC], [1, 256]])))

        if True:
            @blk.vector
            def _(v):
                v.wait_ge(sV.h, p0_v)
                for c in range(NB):
                    v.tensor_reduce(ssum[:, c:c + 1], xa[:, c * D:(c + 1) * D], AX, Alu.add)
                sV.inc(v.engine_nop())
                v.wait_ge(sV.h, sV.n)
                v.wait_ge(sA.h, sA.n + NB)
                S, Q = ssum[:, 0:NB], sqsum[:, 0:NB]
                M, VV, R = mnegv[:, 0:NB], varv[:, 0:NB], rstdv[:, 0:NB]
                Ta, Tb, Tc = tA_[:, 0:NB], tB_[:, 0:NB], tC_[:, 0:NB]
                vchain(v, v.tensor_scalar(M, S, -1.0 / D, None, op0=Alu.mult), sV)
                vchain(v, v.tensor_tensor(Ta, M, M, Alu.mult), sV)
                vchain(v, v.tensor_scalar(Tb, Q, 1.0 / D, None, op0=Alu.mult), sV)
                vchain(v, v.tensor_tensor(VV, Tb, Ta, Alu.subtract), sV)
                vchain(v, v.tensor_scalar(VV, VV, 0.0, float(EPS_LN),
                                          op0=Alu.max, op1=Alu.add), sV)
                emit_rsqrt(v, sV, VV, R, Ta, Tb, Tc)
                for c in range(NB):
                    sV.inc(v.tensor_scalar(
                        xa[:, c * D:(c + 1) * D], xa[:, c * D:(c + 1) * D],
                        mnegv[:, c:c + 1], rstdv[:, c:c + 1],
                        op0=Alu.add, op1=Alu.mult))

            @blk.scalar
            def _(sc):
                sc.wait_ge(sV.h, p0_v)
                for c in range(NB):
                    sc.wait_ge(sA.h, sA.n)
                    sA.inc(sc.activation(vC[:, 0:1024], xa[:, c * D:(c + 1) * D],
                                         Act.Square,
                                         accum_out=sqsum[:, c:c + 1]))
        ln_done = sV.n

        _p2xn = [0]
        _p2sta = [0]
        if True:
            @blk.sync
            def _(sy):
                sy.wait_ge(sV.h, ln_done)
                if "t_xn" in taps:
                    sSt2.dma(sy.dma_start(
                        out=taps["t_xn"].rearrange("(b p) d -> p b d", p=128),
                        in_=A3(XA, [[D, NB], [1, D]])))
                for b in list(range(11, 16)) + list(range(0, 11)):
                    sb_ = sSt2 if b >= 11 else sSt
                    sb_.dma(sy.dma_start(
                        out=xg_stage[b * 128: b * 128 + 120, 0:D],
                        in_=xa[8:128, b * D:(b + 1) * D]))
                    sb_.dma(sy.dma_start(
                        out=xg_stage[b * 128 + 120: b * 128 + 128, 0:D],
                        in_=xa[0:8, (b + 1) * D:(b + 2) * D]))
                    if b == 15:
                        _p2xn[0] = sSt2.n
                sSt.dma(sy.dma_start(out=xg_stage[SENT:SENT + 1, :], in_=zero[0:1, :]))
                for b in range(16):
                    sSt.dma(sy.dma_start(
                        out=xg_stage[b * 128:(b + 1) * 128, 1024 + EXP:XGW],
                        in_=zero[:, 0:XGW - 1024 - EXP]))
                for b in range(16):
                    sSt.dma(sy.dma_start(out=g_stage[b * 128:(b + 1) * 128, :],
                                         in_=zero[:, 0:D]))
                sSt.dma(sy.dma_start(out=g_stage[SENT:SENT + 1, :], in_=zero[0:1, 0:D]))
                _p2sta[0] = sSt.n


        p3_mm = sMM.n
        p3_v = sV.n
        if True:
            @blk.vector
            def _(v):
                for i in range(NB * 2):
                    c, q = i // 2, i % 2
                    v.wait_ge(sMM.h, p3_mm + i + 1)
                    sV.inc(v.tensor_copy(
                        A3(XT + (4 * q) * TA + c * 128, [[TA, 4], [1, 128]]),
                        bass.AP(PBk[q].tensor, 0, [[512, 128], [128, 4], [1, 128]])))

            @blk.tensor
            def _(te):
                te.wait_ge(sV.h, ln_done)
                for i in range(NB * 2):
                    c, q = i // 2, i % 2
                    if i >= 2:
                        te.wait_ge(sV.h, p3_v + (i - 2) + 1)
                    for k4 in range(4):
                        kc = 4 * q + k4
                        mi = te.matmul(PBk[q][:, k4 * 128:(k4 + 1) * 128],
                                  xa[:, c * D + kc * 128: c * D + (kc + 1) * 128],
                                  identt, start=(k4 == 0), stop=(k4 == 3),
                                  is_transpose=True)
                    sMM.inc(mi)
        p3_done_v = sV.n

        p4i_mm = sMM.n
        p4i_v = sV.n
        if True:
            @blk.vector
            def _(v):
                v.wait_ge(sMM.h, p4i_mm + 1)
                sV.inc(v.tensor_copy(taur[0:8, 0:1], PBk[2][0:8, 0:1]))
                v.wait_ge(sMM.h, p4i_mm + 2)
                sV.inc(v.tensor_copy(taur[0:1, 0:8], PBk[3][0:1, 0:8]))
                v.wait_ge(sMM.h, p4i_mm + 3)
                v.tensor_copy(rwsumb, PBk[4][:, 0:8])
                sV.inc(v.tensor_copy(rbb, PBk[5][:, 0:8]))

            @blk.tensor
            def _(te):
                te.wait_ge(sV.h, p0_v)
                for kc in range(KC):
                    mi = te.matmul(PBk[2][0:8, 0:1],
                              A3(rw9.offset + kc * 9, [[1, EXP]]),
                              A3(rw9.offset + kc * 9 + EXP, [[1, 1]]),
                              start=(kc == 0), stop=(kc == KC - 1))
                sMM.inc(mi)
                te.wait_ge(sV.h, p4i_v + 1)
                mi = te.matmul(PBk[3][0:1, 0:8], taur[0:8, 0:1], identt[0:8, 0:8],
                          start=True, stop=True, is_transpose=True)
                sMM.inc(mi)
                te.wait_ge(sV.h, p4i_v + 2)
                mi = te.matmul(PBk[4][:, 0:8], onesrt[0:1, :], taur[0:1, 0:8],
                          start=True, stop=True)
                mi = te.matmul(PBk[5][:, 0:8], onesrt[0:1, :], rbrow, start=True, stop=True)
                sMM.inc(mi)
        p4i_done = sV.n

        p4_mm = sMM.n
        p4_v = sV.n
        p4_ae = sAe.n
        p4_vg = sVg.n
        GPOPS = 5 + RSQ_OPS + 4 + 2 + 3
        grp_scores_done = []
        grp_exp_wait = []
        p4_rel_evac = []
        NGRP = 16

        gtmp = srch[:, 0:128]

        if True:
            @blk.vector
            def _(v):
                v.wait_ge(sSt2.h, _p2xn[0])
                for g in range(NGRP):
                    gb_mm = p4_mm + g * 20
                    for q4 in range(4):
                        v.wait_ge(sMM.h, gb_mm + 3 * q4 + 1)
                        sV.inc(v.tensor_copy(lgTst[0:9, q4 * 256:(q4 + 1) * 256],
                                             PBk[2 + q4 % 2][0:9, 0:256]))
                        for tid in range(2):
                            v.wait_ge(sMM.h, gb_mm + 3 * q4 + 2 + tid)
                            vchain(v, v.tensor_tensor(
                                gtmp, PBk[4 + tid][:, 0:128], identt, Alu.mult), sV)
                            vchain(v, v.tensor_reduce(
                                g_sq[:, 2 * q4 + tid:2 * q4 + tid + 1],
                                gtmp, AX, Alu.add), sV)
                    for jj in range(8):
                        v.wait_ge(sMM.h, gb_mm + 12 + jj + 1)
                        sV.inc(v.tensor_copy(lgt[:, jj * 9:(jj + 1) * 9],
                                             PBk[jj % 2][:, 0:9]))
                    h, gp = g // 2, g % 2
                    L8 = A3(lgt.offset, [[9, 8], [1, 8]])
                    SX8 = A3(lgt.offset + 8, [[9, 8]])
                    FS_ = A3(fixs.offset, [[8, 8], [1, 8]])
                    EXG = A3(gescr.offset, [[8, 8], [1, 8]])
                    rw_bc = A3(rwsumb.offset, [[0, 8], [1, 8]])
                    rb_bc = A3(rbb.offset, [[0, 8], [1, 8]])
                    mn_bc = A3(g_mneg.offset, [[1, 8], [0, 8]])
                    rstd_bc = A3(g_rstd.offset, [[1, 8], [0, 8]])
                    nm_bc = A3(g_nm.offset, [[1, 8], [0, 8]])
                    ri_bc = A3(g_ri.offset, [[1, 8], [0, 8]])
                    v.wait_ge(sV.h, sV.n)
                    base_vg = sVg.n
                    vchain(v, v.tensor_scalar(g_mneg, SX8, -1.0 / D, None,
                                              op0=Alu.mult), sVg)
                    vchain(v, v.tensor_tensor(g_sA, g_mneg, g_mneg, Alu.mult), sVg)
                    vchain(v, v.tensor_scalar(g_sB, g_sq, 1.0 / D, None,
                                              op0=Alu.mult), sVg)
                    vchain(v, v.tensor_tensor(g_var, g_sB, g_sA, Alu.subtract), sVg)
                    vchain(v, v.tensor_scalar(g_var, g_var, 0.0, float(EPS_LN),
                                              op0=Alu.max, op1=Alu.add), sVg)
                    emit_rsqrt(v, sVg, g_var, g_rstd, g_sA, g_sB, g_sC)
                    vchain(v, v.tensor_tensor(FS_, rw_bc, mn_bc, Alu.mult), sVg)
                    vchain(v, v.tensor_tensor(FS_, FS_, L8, Alu.add), sVg)
                    vchain(v, v.tensor_tensor(FS_, FS_, rstd_bc, Alu.mult), sVg)
                    vchain(v, v.tensor_tensor(FS_, FS_, rb_bc, Alu.add), sVg)
                    vchain(v, v.tensor_reduce(g_nm, FS_, AX, Alu.max, negate=True), sVg)
                    vchain(v, v.tensor_tensor(FS_, FS_, nm_bc, Alu.add), sVg)
                    grp_exp_wait.append(sVg.n)
                    v.wait_ge(sAe.h, p4_ae + g + 1)
                    vchain(v, v.tensor_reduce(g_rs, EXG, AX, Alu.add), sVg)
                    vchain(v, v.reciprocal(g_ri, g_rs), sVg)
                    vchain(v, v.tensor_tensor(
                        A3(sgP.offset + h * 128 + gp * 64, [[8, 8], [1, 8]]),
                        EXG, ri_bc, Alu.mult), sVg)
                    grp_scores_done.append(sVg.n)
                    assert sVg.n - base_vg == GPOPS
                for h in range(8):
                    v.wait_ge(sMM.h, p4_mm + NGRP * 20 + h + 1)
                    p4_rel_evac.append(sV.inc(v.tensor_copy(
                        srch[:, h * 128:(h + 1) * 128], PBk[h % 2][:, 0:128])))

            @blk.scalar
            def _(sc):
                sc.wait_ge(sSt2.h, _p2xn[0])
                for g in range(NGRP):
                    sc.wait_ge(sVg.h, grp_exp_wait[g])
                    sAe.inc(sc.activation(A3(gescr.offset, [[8, 8], [1, 8]]),
                                          A3(fixs.offset, [[8, 8], [1, 8]]),
                                          Act.Exp))

            @blk.tensor
            def _(te):
                te.wait_ge(sV.h, p3_done_v)
                te.wait_ge(sV.h, p4i_done)
                for g in range(NGRP):
                    gb_v = p4_v + g * 28
                    for q4 in range(4):
                        hx = g * 4 + q4
                        sl = hx % 2
                        te.wait_ge(WS[sl].h, 16 * (hx // 2 + 1))
                        if hx >= 2:
                            pg, pq = (hx - 2) // 4, (hx - 2) % 4
                            te.wait_ge(sV.h, p4_v + pg * 28 + 5 * pq + 1)
                        for kc in range(KC):
                            mi = te.matmul(
                                PBk[2 + q4 % 2][0:9, 0:256],
                                A3(rw9.offset + kc * 9, [[1, 9]]),
                                xT[sl][:, kc * 256:(kc + 1) * 256],
                                start=(kc == 0), stop=(kc == KC - 1))
                        sMM.inc(mi)
                        for tid in range(2):
                            if hx >= 1:
                                pg, pq = (hx - 1) // 4, (hx - 1) % 4
                                te.wait_ge(sV.h,
                                           p4_v + pg * 28 + 5 * pq + 2 + 2 * tid)
                            for kc in range(KC):
                                xtile = xT[sl][:, kc * 256 + tid * 128:
                                               kc * 256 + (tid + 1) * 128]
                                mi = te.matmul(PBk[4 + tid][:, 0:128],
                                               xtile, xtile,
                                               start=(kc == 0), stop=(kc == KC - 1))
                            sMM.inc(mi)
                    te.wait_ge(sV.h, gb_v + 16)
                    for jj in range(8):
                        if jj >= 2:
                            te.wait_ge(sV.h, gb_v + 20 + (jj - 2) + 1)
                        elif g >= 1:
                            te.wait_ge(sV.h, p4_v + (g - 1) * 28 + 20 + (jj + 6) + 1)
                        mi = te.matmul(PBk[jj % 2][:, 0:9],
                                       lgTst[0:9, jj * 128:(jj + 1) * 128],
                                       identt[0:9, 0:9],
                                       start=True, stop=True, is_transpose=True)
                        sMM.inc(mi)
                te.wait_ge(sV.h, p4_v + NGRP * 28)
                for h in range(8):
                    te.wait_ge(sVg.h, grp_scores_done[2 * h + 1])
                    if h >= 2:
                        te.wait_ge(sV.h, p4_rel_evac[h - 2])
                    mi = te.matmul(PBk[h % 2][:, 0:128], sgP[:, h * 128:(h + 1) * 128],
                                   identt, start=True, stop=True, is_transpose=True)
                    sMM.inc(mi)

            @blk.sync
            def _(sy):
                for hx in range(2, 64):
                    sl = hx % 2
                    if True:
                        pg, pq = (hx - 2) // 4, (hx - 2) % 4
                        sy.wait_ge(sMM.h, p4_mm + pg * 20 + 3 * pq + 3)
                    WS[sl].dma(sy.dma_start(
                        out=A3(xT[sl].offset, [[256, KC], [1, 256]]),
                        in_=bass.AP(x_full_T.tensor, hx * 256,
                                    [[16384, 128], [128 * 16384, KC], [1, 256]])))

        p4_done_v = sV.n
        p4_done_vg = sVg.n
        p4_done_mm = sMM.n

        p5_vg = sVg.n
        p5_mid_ready = []
        p5_midpp_evac = []
        p5_cnt_ready = []
        p5_cnt_evac = []
        p5_fin = []
        p7_selv_evac = []
        p7_nfready = []
        p7_nfb_evac = []
        p7_selvt_done = []
        p7_gp = sGp.n
        fe_pairs = []

        def _p5v_init(v):
            v.wait_ge(sV.h, p4_done_v)
            v.wait_ge(sVg.h, p4_done_vg)
            vchain(v, v.memset(lo8[0][0:8, :], 0.0), sVg)
            vchain(v, v.memset(hi8[0][0:8, :], 2.0), sVg)

        def _p5t_init(te):
            te.wait_ge(sV.h, p4_done_v)

        fe_pairs.append((_p5v_init, _p5t_init))

        def _p5v_iter(it):
            def f(v):
                pa, pb = it % 2, 1 - it % 2
                vchain(v, v.tensor_tensor(mid8[0:8, :], lo8[pa][0:8, :],
                                          hi8[pa][0:8, :], Alu.add), sVg)
                vchain(v, v.tensor_scalar(mid8[0:8, :], mid8[0:8, :], 0.5, None,
                                          op0=Alu.mult), sVg)
                p5_mid_ready.append(sVg.n)
                v.wait_ge(sBm.h, it * 2 + 1)
                vchain(v, v.tensor_copy(midpp, PBk[3][:, 0:1]), sVg)
                p5_midpp_evac.append(sVg.n)
                vchain(v, v.tensor_scalar(gescrD, srch, midpp, None,
                                          op0=Alu.is_ge, op1=Alu.add,
                                          accum_out=cntp), sVg)
                p5_cnt_ready.append(sVg.n)
                v.wait_ge(sBm.h, it * 2 + 2)
                vchain(v, v.tensor_copy(cnt8s[0:8, :], PBk[3][0:8, 0:1]), sVg)
                p5_cnt_evac.append(sVg.n)
                vchain(v, v.tensor_scalar(pred8[:, 0:1], cnt8s[0:8, :], 2048.0,
                                          None, op0=Alu.is_ge), sVg)
                sVg.inc(v.select(lo8[pb][0:8, :], pred8[:, 0:1], mid8[0:8, :],
                                 lo8[pa][0:8, :], add_drain=True))
                v.wait_ge(sVg.h, sVg.n)
                sVg.inc(v.select(hi8[pb][0:8, :], pred8[:, 0:1], hi8[pa][0:8, :],
                                 mid8[0:8, :], add_drain=True))
                v.wait_ge(sVg.h, sVg.n)
            return f

        def _p5t_iter(it):
            def f(te):
                te.wait_ge(sVg.h, p5_mid_ready[it])
                if it >= 1:
                    te.wait_ge(sVg.h, p5_midpp_evac[it - 1])
                    te.wait_ge(sVg.h, p5_cnt_evac[it - 1])
                sBm.inc(te.matmul(PBk[3][:, 0:1], bdTt[0:8, :], mid8[0:8, 0:1],
                                  start=True, stop=True))
                te.wait_ge(sVg.h, p5_cnt_ready[it])
                sBm.inc(te.matmul(PBk[3][0:8, 0:1], bd8t, cntp,
                                  start=True, stop=True))
            return f

        for _it in range(NIT):
            fe_pairs.append((_p5v_iter(_it), _p5t_iter(_it)))

        def _p5v_fin(v):
            p5_fin.append(sVg.n)
            v.wait_ge(sBm.h, NIT * 2 + 1)
            vchain(v, v.tensor_copy(taur[0:1, 0:8], PBk[3][0:1, 0:8]), sVg)
            p5_fin.append(sVg.n)
            v.wait_ge(sBm.h, NIT * 2 + 2)
            vchain(v, v.tensor_copy(taub, PBk[3][:, 0:8]), sVg)
            vchain(v, v.tensor_scalar(taulb, taub, float(LOOSE), None,
                                      op0=Alu.mult), sVg)
            for e in range(EXP):
                ssl = A3(sgP.offset + e, [[8, 16]])
                sVg.inc(v.scalar_tensor_tensor(
                    A3(gm_all.offset + e, [[8, 16]]),
                    ssl, taub[:, e:e + 1], ssl, op0=Alu.is_ge, op1=Alu.mult))
            v.wait_ge(sVg.h, sVg.n)
            sVg.inc(v.engine_nop())

        def _p5t_fin(te):
            te.wait_ge(sVg.h, p5_fin[0])
            sBm.inc(te.matmul(PBk[3][0:1, 0:8], lo8[NIT % 2][0:8, 0:1],
                              identt[0:8, 0:8],
                              start=True, stop=True, is_transpose=True))
            te.wait_ge(sVg.h, p5_fin[1])
            sBm.inc(te.matmul(PBk[3][:, 0:8], onesrt[0:1, :], taur[0:1, 0:8],
                              start=True, stop=True))

        fe_pairs.append((_p5v_fin, _p5t_fin))
        p6_vg_box = [0]

        def _p6v_mark(v):
            p6_vg_box[0] = sVg.n

        fe_pairs.append((_p6v_mark, None))

        def _p7v(e):
            def f(v):
                ssl = A3(sgP.offset + e, [[8, 16]])
                if e >= 1:
                    v.wait_ge(sBm.h, 2 * NIT + 2 + 2 * e)
                vchain(v, v.scalar_tensor_tensor(
                    selvt, ssl, taulb[:, e:e + 1], iota16t,
                    op0=Alu.is_ge, op1=Alu.mult), sVg)
                vchain(v, v.tensor_scalar(selvt, selvt, -1.0, None,
                                          op0=Alu.add), sVg)
                p7_selvt_done.append(sVg.n)
                v.wait_ge(sBm.h, 2 * NIT + 2 + 2 * e + 1)
                vchain(v, v.tensor_copy(selv_sb, PBk[3][0:16, 0:128]), sVg)
                p7_selv_evac.append(sVg.n)
                v.wait_ge(sGp.h, p7_gp + e + 1)
                vchain(v, v.tensor_copy(nfb[0:1, 0:1], nfu[0:1, 0:1]), sVg)
                p7_nfready.append(sVg.n)
                v.wait_ge(sBm.h, 2 * NIT + 2 + 2 * e + 2)
                vchain(v, v.tensor_copy(nfb[0:16, 1:2], PBk[3][0:16, 0:1]), sVg)
                p7_nfb_evac.append(sVg.n)
                vchain(v, v.tensor_scalar(vmask, ioslt, nfb[0:16, 1:2], None,
                                          op0=Alu.is_lt), sVg)
                sVg.inc(v.select(idxsel[:, e * CAPW:(e + 1) * CAPW], vmask,
                                 idxraw[:, e * CAPW:(e + 1) * CAPW], c2048,
                                 add_drain=True))
                v.wait_ge(sVg.h, sVg.n)
                vchain(v, v.tensor_copy(idx16[0:16, e * CAPW:(e + 1) * CAPW],
                                        idxsel[:, e * CAPW:(e + 1) * CAPW]), sVg)
            return f

        def _p7t(e):
            def f(te):
                te.wait_ge(sVg.h, p7_selvt_done[e])
                if e >= 1:
                    te.wait_ge(sVg.h, p7_selv_evac[e - 1])
                sBm.inc(te.matmul(PBk[3][0:16, 0:128], selvt, identt,
                                  start=True, stop=True, is_transpose=True))
                te.wait_ge(sVg.h, p7_nfready[e])
                if e >= 1:
                    te.wait_ge(sVg.h, p7_nfb_evac[e - 1])
                sBm.inc(te.matmul(PBk[3][0:16, 0:1], onesrt[0:1, 0:16],
                                  nfb[0:1, 0:1], start=True, stop=True))
            return f

        for _e in range(EXP):
            fe_pairs.append((_p7v(_e), _p7t(_e)))
        p7dv_box = [0]

        def _p7v_mark(v):
            p7dv_box[0] = sVg.n

        fe_pairs.append((_p7v_mark, None))


        fe_vi = [0, 0]
        fe_ti = [0, 0]

        def weave_v(v):
            fe_vi[0] += 1
            if fe_vi[0] % 4 == 0 and fe_vi[1] < len(fe_pairs):
                fn = fe_pairs[fe_vi[1]][0]
                fe_vi[1] += 1
                if fn is not None:
                    fn(v)

        def weave_t(te):
            fe_ti[0] += 1
            if fe_ti[0] % 4 == 0 and fe_ti[1] < len(fe_pairs):
                fn = fe_pairs[fe_ti[1]][1]
                fe_ti[1] += 1
                if fn is not None:
                    fn(te)

        p8_mm = sMM.n
        p8_v = sV.n
        p8_a = sA.n
        NJT = 18
        NHT = 2 * NJT
        PE_PER_C = 27 + 3 * NHT
        V_PER_C = 27 + 5 * NHT
        A_PER_C = 2 * NHT
        QP, KP = 4, 5

        def mmv(k):
            return 27 + 3 * k

        p8_ev = {}
        p8_sc = {}
        p8_nm = {}
        p8_qkv = {}
        pe_scores = {}
        pe_T = {}
        pe_attn = {}
        _n = 27
        for _k in range(NHT + 2):
            if _k < NHT:
                _n += 1; pe_scores[_k] = _n
            if 1 <= _k <= NHT:
                _n += 1; pe_T[_k - 1] = _n
            if _k >= 2:
                _n += 1; pe_attn[_k - 2] = _n
        assert _n == PE_PER_C, _n

        if True:
            @blk.vector
            def _(v):
                v.wait_ge(sSt.h, _p2sta[0])
                for c in range(KC):
                    vb = p8_v + c * V_PER_C
                    mb = p8_mm + c * PE_PER_C
                    ab = p8_a + c * A_PER_C
                    for i in range(QP):
                        v.wait_ge(sMM.h, mb + i + 1)
                        sV.inc(v.tensor_scalar(qT[:, i * 512:(i + 1) * 512],
                                               PBk[i % 2][:, 0:512],
                                               float(INV2T), None, op0=Alu.mult))
                        weave_v(v)
                    for i in range(KP):
                        npz = 512 if i < 4 else 128
                        v.wait_ge(sMM.h, mb + QP + i + 1)
                        sV.inc(v.tensor_copy(kT[:, i * 512:i * 512 + npz],
                                             PBk[i % 2][:, 0:npz]))
                        weave_v(v)
                    for b in range(NJT):
                        v.wait_ge(sMM.h, mb + QP + KP + b + 1)
                        sV.inc(v.tensor_copy(vC[:, b * 128:(b + 1) * 128],
                                             PBk[b % 2][:, 0:128]))
                        weave_v(v)
                    p8_qkv[c] = sV.n
                    def qwof(jt):
                        return 120 if jt < NJT - 1 else TOK - 120 * (NJT - 1)
                    for k in range(NHT + 2):
                        if k < NHT:
                            h2, jt = k // NJT, k % NJT
                            qw = qwof(jt)
                            sb = [4, 5, 6][k % 3]
                            mt = mask0t if jt == 0 else maskAt
                            v.wait_ge(sMM.h, mb + pe_scores[k])
                            sV.inc(v.tensor_tensor(PBk[sb][0:qw, 0:128],
                                                   PBk[sb][0:qw, 0:128],
                                                   mt[0:qw, 0:128], Alu.add))
                            v.wait_ge(sV.h, sV.n)
                            sV.inc(v.tensor_reduce(nmx2[k % 2][0:qw, :],
                                                   PBk[sb][0:qw, 0:128], AX,
                                                   Alu.max, negate=True))
                            p8_nm[(c, k)] = sV.n
                        if 1 <= k <= NHT:
                            kp = k - 1
                            qwp = qwof(kp % NJT)
                            v.wait_ge(sA.h, ab + 2 * kp + 1)
                            sV.inc(v.reciprocal(rinv2[kp % 2][0:qwp, :],
                                                rsum2[kp % 2][0:qwp, :]))
                            v.wait_ge(sV.h, sV.n)
                            sV.inc(v.tensor_scalar(wtsb2[kp % 2][0:qwp, :],
                                                   wtsb2[kp % 2][0:qwp, :],
                                                   rinv2[kp % 2][0:qwp, :], None,
                                                   op0=Alu.mult))
                            p8_sc[(c, kp)] = sV.n
                        if k >= 2:
                            kp = k - 2
                            h2p, jtp = kp // NJT, kp % NJT
                            qwp = qwof(jtp)
                            v.wait_ge(sMM.h, mb + pe_attn[kp])
                            sV.inc(v.tensor_copy(
                                attnT[c][64 * h2p:64 * h2p + 64,
                                         jtp * 120:jtp * 120 + qwp],
                                PBk[[7, 2][kp % 2]][64 * h2p:64 * h2p + 64, 0:qwp]))
                            p8_ev[(c, kp)] = sV.n
                        weave_v(v)

            @blk.scalar
            def _(sc):
                for c in range(KC):
                    mb = p8_mm + c * PE_PER_C
                    for k in range(NHT + 1):
                        if k < NHT:
                            jt = k % NJT
                            qw = 120 if jt < NJT - 1 else TOK - 120 * (NJT - 1)
                            sb = [4, 5, 6][k % 3]
                            sc.wait_ge(sV.h, p8_nm[(c, k)])
                            if k >= 2:
                                sc.wait_ge(sMM.h, mb + pe_T[k - 2])
                            elif c >= 1:
                                sc.wait_ge(sMM.h, p8_mm + (c - 1) * PE_PER_C
                                           + pe_T[NHT - 2 + k])
                            sA.inc(sc.activation(wtsb2[k % 2][0:qw, :],
                                                 PBk[sb][0:qw, 0:128],
                                                 Act.Exp, bias=nmx2[k % 2][0:qw, :],
                                                 scale=1.0,
                                                 accum_out=rsum2[k % 2][0:qw, :]))
                        if k >= 1:
                            kp = k - 1
                            qwp = 120 if kp % NJT < NJT - 1 else TOK - 120 * (NJT - 1)
                            sc.wait_ge(sMM.h, mb + pe_T[kp])
                            if kp >= 2:
                                sc.wait_ge(sMM.h, mb + pe_attn[kp - 2])
                            elif c >= 1:
                                sc.wait_ge(sMM.h, p8_mm + (c - 1) * PE_PER_C
                                           + pe_attn[NHT - 2 + kp])
                            sA.inc(sc.copy(wtsTa2[kp % 2][:, 0:qwp],
                                           PBk[kp % 2][:, 0:qwp]))

            @blk.tensor
            def _(te):
                te.wait_ge(sV.h, p4_done_v)
                for c in range(KC):
                    vb = p8_v + c * V_PER_C
                    ab = p8_a + c * A_PER_C
                    mb = p8_mm + c * PE_PER_C
                    sl = c % 2
                    te.wait_ge(SS[sl].h, ssb8[sl] + 48 * (c // 2 + 1))
                    if c == 0:
                        te.wait_ge(sV.h, p3_done_v)
                    for i in range(QP):
                        bk = i % 2
                        if i >= 2:
                            te.wait_ge(sV.h, vb + (i - 2) + 1)
                        elif c >= 1:
                            te.wait_ge(sA.h, p8_a + c * A_PER_C)
                            te.wait_ge(sV.h, p8_v + c * V_PER_C)
                        for kc in range(KC):
                            mi = te.matmul(PBk[bk][:, 0:512],
                                           wqb[sl][:, kc * 128:(kc + 1) * 128],
                                           xnT[:, kc * TA + 8 + i * 512:
                                               kc * TA + 8 + (i + 1) * 512],
                                           start=(kc == 0), stop=(kc == KC - 1))
                        sMM.inc(mi)
                        weave_t(te)
                    for i in range(KP):
                        npz = 512 if i < 4 else 128
                        bk = i % 2
                        te.wait_ge(sV.h, vb + QP + (i - 2) + 1 if i >= 2 else vb + QP)
                        for kc in range(KC):
                            mi = te.matmul(PBk[bk][:, 0:npz],
                                           wkb[sl][:, kc * 128:(kc + 1) * 128],
                                           xnT[:, kc * TA + i * 512: kc * TA + i * 512 + npz],
                                           start=(kc == 0), stop=(kc == KC - 1))
                        sMM.inc(mi)
                        weave_t(te)
                    for b in range(NJT):
                        bk = b % 2
                        te.wait_ge(sV.h, vb + QP + KP + (b - 2) + 1 if b >= 2
                                   else vb + QP + KP)
                        for kc in range(KC):
                            mi = te.matmul(PBk[bk][:, 0:128],
                                           xnT[:, kc * TA + b * 120: kc * TA + b * 120 + 128],
                                           wvb[sl][:, kc * 128:(kc + 1) * 128],
                                           start=(kc == 0), stop=(kc == KC - 1))
                        sMM.inc(mi)
                        weave_t(te)
                    for k in range(NHT + 2):
                        if k < NHT:
                            h2, jt = k // NJT, k % NJT
                            qw = 120 if jt < NJT - 1 else TOK - 120 * (NJT - 1)
                            sb = [4, 5, 6][k % 3]
                            te.wait_ge(sV.h, p8_qkv[c])
                            if k >= 3:
                                te.wait_ge(sA.h, ab + 2 * (k - 3) + 1)
                            elif c >= 1:
                                te.wait_ge(sA.h, p8_a + (c - 1) * A_PER_C
                                           + 2 * (NHT - 3 + k) + 1)
                            mi = te.matmul(PBk[sb][0:qw, 0:128],
                                           qT[64 * h2:64 * h2 + 64,
                                              jt * 120: jt * 120 + qw],
                                           kT[64 * h2:64 * h2 + 64,
                                              jt * 120: jt * 120 + 128],
                                           start=True, stop=True)
                            sMM.inc(mi)
                        if 1 <= k <= NHT:
                            kp = k - 1
                            qwp = 120 if kp % NJT < NJT - 1 else TOK - 120 * (NJT - 1)
                            te.wait_ge(sV.h, p8_sc[(c, kp)])
                            mi = te.matmul(PBk[kp % 2][:, 0:qwp],
                                           wtsb2[kp % 2][0:qwp, 0:128],
                                           identt[0:qwp, 0:qwp],
                                           start=True, stop=True, is_transpose=True)
                            sMM.inc(mi)
                        if k >= 2:
                            kp = k - 2
                            h2p, jtp = kp // NJT, kp % NJT
                            qwp = 120 if jtp < NJT - 1 else TOK - 120 * (NJT - 1)
                            te.wait_ge(sA.h, ab + 2 * kp + 2)
                            if kp >= 2:
                                te.wait_ge(sV.h, p8_ev[(c, kp - 2)])
                            elif c >= 1:
                                te.wait_ge(sV.h, p8_ev[(c - 1, NHT - 2 + kp)])
                            tp = {"tile_position": (0, 64)} if h2p else {}
                            mi = te.matmul(PBk[[7, 2][kp % 2]][64 * h2p:64 * h2p + 64,
                                                               0:qwp],
                                           vC[0:128, jtp * 128 + 64 * h2p:
                                              jtp * 128 + 64 * h2p + 64],
                                           wtsTa2[kp % 2][:, 0:qwp],
                                           start=True, stop=True, **tp)
                            sMM.inc(mi)
                        weave_t(te)

            @blk.sync
            def _(sy):
                for c in range(2, KC):
                    sl = c % 2
                    sy.wait_ge(sMM.h, p8_mm + (c - 1) * PE_PER_C + 27)
                    SS[sl].dma(sy.dma_start(
                        out=A3(wqb[sl].offset, [[128, KC], [1, 128]]),
                        in_=Wq_d[:, c * 128:(c + 1) * 128].rearrange(
                            "(kc p) n -> p kc n", p=128)))
                    SS[sl].dma(sy.dma_start(
                        out=A3(wkb[sl].offset, [[128, KC], [1, 128]]),
                        in_=Wk_d[:, c * 128:(c + 1) * 128].rearrange(
                            "(kc p) n -> p kc n", p=128)))
                    SS[sl].dma(sy.dma_start(
                        out=A3(wvb[sl].offset, [[128, KC], [1, 128]]),
                        in_=Wv_d[:, c * 128:(c + 1) * 128].rearrange(
                            "(kc p) n -> p kc n", p=128)))
        p8_done_v = sV.n
        p8_done_mm = sMM.n
        p8_done_a = sA.n
        p7_done_vg = p7dv_box[0]
        p6_vg = p6_vg_box[0]
        assert fe_vi[1] == len(fe_pairs) and fe_ti[1] == len(fe_pairs)

        if True:
            @blk.gpsimd
            def _(gp):
                gp.load_library(libcfg.sparse_gather)
                for e in range(EXP):
                    gp.wait_ge(sVg.h, p7_selv_evac[e])
                    sGp.inc(gp.sparse_gather(idxraw[:, e * CAPW:(e + 1) * CAPW],
                                             selv_sb, num_found=nfu[0:1, 0:1]))
                gp.wait_ge(sVg.h, p7_done_vg)
                sGd.dma(gp.dma_start(out=idx16[16:32, :], in_=idx16[0:16, :]))

            @blk.sync
            def _(sy):
                sy.wait_ge(sVg.h, p6_vg)
                sSt.dma(sy.dma_start(
                    out=bass.AP(xg_stage.tensor, 1024,
                                [[XGW, 128], [128 * XGW, 16], [1, 8]]),
                    in_=A3(gm_all.offset, [[8, 16], [1, 8]])))
                if "t_tau" in taps:
                    sSt.dma(sy.dma_start(out=taps["t_tau"][:, 0:EXP], in_=taub))
                    sSt.dma(sy.dma_start(out=taps["t_tau"][:, EXP:2 * EXP],
                                         in_=taulb))
                if "t_idx" in taps:
                    sy.wait_ge(sVg.h, p7_done_vg)
                    sSt.dma(sy.dma_start(out=taps["t_idx"], in_=idxsel))


        p9_mm = sMM.n
        p9_v = sV.n
        w0base = sW0.n
        p9_evac = []
        if True:
            @blk.vector
            def _(v):
                S_PER = 32 if "t_s" in taps else 16
                ssb = [sS0.n, sS1.n]
                for tt in range(16):
                    sl = tt % 2
                    for half in range(2):
                        v.wait_ge(sMM.h, p9_mm + tt * 2 + half + 1)
                        if tt >= 2:
                            v.wait_ge(SS[sl].h, ssb[sl] + S_PER * (tt // 2))
                        p9_evac.append(sV.inc(v.tensor_copy(
                            stile[sl][:, half * 512:(half + 1) * 512],
                            PBk[4 + half][:, 0:512])))

            @blk.tensor
            def _(te):
                te.wait_ge(sV.h, p8_done_v)
                te.wait_ge(sA.h, p8_done_a)
                te.wait_ge(sW0.h, w0base + 16 * 8)
                for tt in range(16):
                    for half in range(2):
                        if tt >= 1:
                            te.wait_ge(sV.h, p9_evac[(tt - 1) * 2 + half])
                        for kc in range(KC):
                            mi = te.matmul(PBk[4 + half][:, 0:512],
                                      attnT[kc][:, tt * 128:(tt + 1) * 128],
                                      woap(kc)[:, half * 512:half * 512 + 512],
                                      start=(kc == 0), stop=(kc == KC - 1))
                        sMM.inc(mi)

            @blk.sync
            def _(sy):
                sy.wait_ge(sSt.h, sSt.n)
                for kc in range(2):
                    sW0.dma(sy.dma_start(
                        out=A3(woSM.offset + kc * D, [[1, D]]),
                        in_=Wo_d[kc * 128:(kc + 1) * 128, :]))
                sy.wait_ge(sMM.h, p8_mm + (KC - 1) * PE_PER_C + 27)
                for kc in range(2, KC):
                    sW0.dma(sy.dma_start(
                        out=A3(woWB.offset + (kc - 2) * D, [[1, D]]),
                        in_=Wo_d[kc * 128:(kc + 1) * 128, :]))
                for tt in range(16):
                    sl = tt % 2
                    sy.wait_ge(sV.h, p9_v + tt * 2 + 2)
                    SS[sl].dma(sy.dma_start(
                        out=s_stage[tt * 128:(tt + 1) * 128, :], in_=stile[sl]))
                    if "t_s" in taps:
                        SS[sl].dma(sy.dma_start(
                            out=taps["t_s"][tt * 128:(tt + 1) * 128, :], in_=stile[sl]))
        p9_done_v = sV.n
        p9_done_mm = sMM.n
        s_dma_done = (sS0.n, sS1.n)

        p10_mm = sMM.n
        p10_v = sV.n
        p10_a = sA.n
        p10_gd = sGd.n
        p10_gd1 = sGd1.n
        GD = [sGd, sGd1]
        gdb = [p10_gd, p10_gd1]

        def gdone(e):
            return GD[e % 2].h, gdb[e % 2] + 16 * (e // 2 + 1)
        p10_gs = sGs.n
        inbase = sIn.n
        w1base = [sW0.n, sW1.n]
        w2base = [sV0.n, sV1.n]
        PE_E = 48
        V_E = 38
        A_E = 16

        def mmb(e):
            return p10_mm + e * PE_E

        def vbb(e):
            return p10_v + e * V_E

        def abb(e):
            return p10_a + e * A_E

        if True:
            @blk.vector
            def _(v):
                for e in range(EXP):
                    v.wait_ge(sMM.h, mmb(e) + 1)
                    sV.inc(v.tensor_copy(b1pp, PBk[0][:, 0:32]))
                    v.wait_ge(sMM.h, mmb(e) + 2)
                    sV.inc(v.tensor_copy(b2b[:, 0:512], PBk[2][:, 0:512]))
                    sV.inc(v.tensor_copy(b2b[:, 512:1024], PBk[3][:, 0:512]))
                    for qd in range(6):
                        v.wait_ge(sMM.h, mmb(e) + 3 + qd)
                        blk_i, q = qd // 2, qd % 2
                        sV.inc(v.tensor_copy(
                            A3(toksT.offset + (4 * q) * CAP + blk_i * 128,
                               [[CAP, 4], [1, 128]]),
                            bass.AP(PBk[qd % 2].tensor, 0, [[512, 128], [128, 4], [1, 128]])))
                    for g in range(8):
                        bk = (4 if g % 2 == 0 else 0)
                        v.wait_ge(sMM.h, mmb(e) + 9 + g)
                        for m4 in (0, 2):
                            m = g * 4 + m4
                            sV.inc(v.tensor_scalar(
                                h1t[:, m * CAP:m * CAP + CAPF], PBk[bk + m4][:, 0:CAPF],
                                b1pp[:, m:m + 1], 0.0, op0=Alu.add, op1=Alu.max))
                    v.wait_ge(sMM.h, mmb(e) + PE_E)
                    v.wait_ge(*gdone(e))
                    if e >= 1:
                        v.wait_ge(sGs.h, p10_gs + 16 * e)
                    sV.inc(v.memset(eob[64:128, 2 * 1024:3 * 1024], 0.0))
                    v.wait_ge(sV.h, sV.n)
                    for blk_i in range(3):
                        tw = 128 if blk_i < 2 else CAPF - 256
                        for half in range(2):
                            bk = 2 + blk_i * 2 + half
                            dst = eob[0:tw, blk_i * 1024 + half * 512:
                                      blk_i * 1024 + half * 512 + 512]
                            sV.inc(v.tensor_tensor(dst, PBk[bk][0:tw, 0:512],
                                                   b2b[0:tw, half * 512:(half + 1) * 512],
                                                   Alu.add))
                    v.wait_ge(sV.h, sV.n)
                    for blk_i in range(3):
                        tw = 128 if blk_i < 2 else CAPF - 256
                        for half in range(2):
                            dst = eob[0:tw, blk_i * 1024 + half * 512:
                                      blk_i * 1024 + half * 512 + 512]
                            gmcol = tgbuf(e)[0:tw, blk_i * XGW + 1024 + e:
                                             blk_i * XGW + 1024 + e + 1]
                            sV.inc(v.tensor_scalar(dst, dst, gmcol, None, op0=Alu.mult))
                    v.wait_ge(sV.h, sV.n)

            @blk.scalar
            def _(sc):
                for e in range(EXP):
                    for g in range(8):
                        bk = (4 if g % 2 == 0 else 0)
                        sc.wait_ge(sMM.h, mmb(e) + 9 + g)
                        for m4 in (1, 3):
                            m = g * 4 + m4
                            sA.inc(sc.activation(
                                h1t[:, m * CAP:m * CAP + CAPF], PBk[bk + m4][:, 0:CAPF],
                                Act.Relu, bias=b1pp[:, m:m + 1]))

            @blk.gpsimd
            def _(gp):
                gp.load_library(libcfg.mlp)
                gp.wait_ge(sSt.h, sSt.n)
                gp.wait_ge(sSt2.h, sSt2.n)
                gp.wait_ge(sGd.h, p10_gd)
                for k in range(EXP + 2):
                    if k >= 2:
                        ep = k - 2
                        gp.wait_ge(sV.h, vbb(ep) + V_E)
                        sGs.dma(gp.dma_scatter_add(
                            g_stage, eob.rearrange("p (b w) -> p b w", w=D),
                            idx16[:, ep * CAPW:(ep + 1) * CAPW],
                            num_idxs=CAP, num_idxs_reg=CAP, elem_size=D))
                    if k < EXP:
                        if k == 0:
                            gp.wait_ge(sMM.h, p8_mm + (KC - 1) * PE_PER_C + 27)
                        elif k == 1:
                            gp.wait_ge(sMM.h, p8_done_mm)
                        else:
                            if k == 2:
                                gp.wait_ge(sS0.h, s_dma_done[0])
                                gp.wait_ge(sS1.h, s_dma_done[1])
                            gp.wait_ge(sMM.h, mmb(k - 2) + 8)
                        GD[k % 2].dma(gp.dma_gather(
                            tgbuf(k).rearrange("p (b w) -> p b w", w=XGW),
                            xg_stage, idx16[:, k * CAPW:(k + 1) * CAPW],
                            num_idxs=CAP, num_idxs_reg=CAP, elem_size=XGW))

            @blk.tensor
            def _(te):
                te.wait_ge(sMM.h, p9_done_mm)
                te.wait_ge(sA.h, p8_done_a)
                for e in range(EXP):
                    te.wait_ge(sIn.h, inbase + 32 * e + 32)
                    if e >= 1:
                        te.wait_ge(sV.h, vbb(e - 1) + 25)
                        te.wait_ge(sA.h, abb(e - 1) + A_E)
                    mi = te.matmul(PBk[0][:, 0:32], b1tmp, identt[0:32, 0:32],
                              start=True, stop=True, is_transpose=True)
                    sMM.inc(mi)
                    if e >= 1:
                        te.wait_ge(sV.h, vbb(e - 1) + 28)
                    mi = te.matmul(PBk[2][:, 0:512], onesrt[0:1, :], b2row[0:1, 0:512],
                              start=True, stop=True)
                    mi = te.matmul(PBk[3][:, 0:512], onesrt[0:1, :], b2row[0:1, 512:1024],
                              start=True, stop=True)
                    sMM.inc(mi)
                    te.wait_ge(*gdone(e))
                    te.wait_ge(sV.h, vbb(e) + 1)
                    for qd in range(6):
                        blk_i, q = qd // 2, qd % 2
                        if qd >= 2:
                            te.wait_ge(sV.h, vbb(e) + 3 + (qd - 2) + 1)
                        for k4 in range(4):
                            kc = 4 * q + k4
                            mi = te.matmul(PBk[qd % 2][:, k4 * 128:(k4 + 1) * 128],
                                      tgbuf(e)[:, blk_i * XGW + kc * 128:
                                               blk_i * XGW + (kc + 1) * 128],
                                      identt, start=(k4 == 0), stop=(k4 == 3),
                                      is_transpose=True)
                        sMM.inc(mi)
                    for g in range(8):
                        sl = g % 2
                        bk = (4 if g % 2 == 0 else 0)
                        te.wait_ge(WS[sl].h, w1base[sl] + 16 * (g // 2 + 1 + (4 * e)))
                        if g >= 2:
                            te.wait_ge(sV.h, vbb(e) + 9 + 2 * (g - 2) + 2)
                            te.wait_ge(sA.h, abb(e) + 2 * (g - 2) + 2)
                        elif g == 0 and e >= 1:
                            te.wait_ge(sV.h, vbb(e - 1) + 32)
                        elif g == 0:
                            te.wait_ge(sV.h, p9_done_v)
                        if g < 2:
                            te.wait_ge(sV.h, vbb(e) + 9)
                        for m4 in range(4):
                            for kc in range(KC):
                                mi = te.matmul(PBk[bk + m4][:, 0:CAPF],
                                          w1r[sl][:, kc * 512 + m4 * 128:
                                                  kc * 512 + (m4 + 1) * 128],
                                          toksT[:, kc * CAP:kc * CAP + CAPF],
                                          start=(kc == 0), stop=(kc == KC - 1))
                        sMM.inc(mi)
                    te.wait_ge(sV.h, vbb(e) + 25)
                    te.wait_ge(sA.h, abb(e) + A_E)
                    for m in range(32):
                        sl = (m // 2) % 2
                        te.wait_ge(VS[sl].h, w2base[sl] + 16 * (m // 4 + 1 + 8 * e))
                        for blk_i in range(3):
                            tw = 128 if blk_i < 2 else CAPF - 256
                            for half in range(2):
                                mi = te.matmul(PBk[2 + blk_i * 2 + half][0:tw, 0:512],
                                          h1t[:, m * CAP + blk_i * 128:
                                              m * CAP + blk_i * 128 + tw],
                                          w2r[sl][:, (m % 2) * 1024 + half * 512:
                                                  (m % 2) * 1024 + (half + 1) * 512],
                                          start=(m == 0), stop=(m == 31))
                        sMM.inc(mi)

            @blk.sync
            def _(sy):
                sy.wait_ge(sMM.h, p8_done_mm)
                for e in range(EXP):
                    if e >= 1:
                        sy.wait_ge(sMM.h, mmb(e - 1) + 2)
                    sIn.dma(sy.dma_start(out=b1tmp,
                                         in_=b1_d[e:e + 1, :].rearrange(
                                             "o (m n) -> (o m) n", n=128)))
                    sIn.dma(sy.dma_start(out=b2row, in_=b2_d[e:e + 1, :]))
                    for g in range(8):
                        sl = g % 2
                        gu = e * 8 + g
                        if gu >= 2:
                            ep, gp_ = (gu - 2) // 8, (gu - 2) % 8
                            sy.wait_ge(sMM.h, mmb(ep) + 9 + gp_)
                        sW = WS[sl]
                        sW.dma(sy.dma_start(
                            out=A3(w1r[sl].offset, [[512, KC], [1, 512]]),
                            in_=bass.AP(w1_d.tensor, e * D * DFF + g * 512,
                                        [[DFF, 128], [128 * DFF, KC], [1, 512]])))
                    for mp in range(16):
                        sl = mp % 2
                        mu = e * 16 + mp
                        if mu >= 2:
                            ep, mp_ = (mu - 2) // 16, (mu - 2) % 16
                            te_i = 16 + (2 * mp_ + 1) + 1
                            sy.wait_ge(sMM.h, mmb(ep) + te_i)
                        sy2 = VS[sl]
                        sy2.dma(sy.dma_start(
                            out=A3(w2r[sl].offset, [[1024, 2], [1, 1024]]),
                            in_=bass.AP(w2_d.tensor, e * DFF * D + mp * 2 * 128 * D,
                                        [[D, 128], [128 * D, 2], [1, D]])))
        p10_done_v = sV.n
        p10_done_gd = sGd.n
        p10_done_gs = sGs.n

        p11_v = sV.n
        p11_a = sA.n
        FG3 = [sXa, sXb, sXc]
        FS3 = [sW0, sW1, sWc]
        FX3 = [sV0, sV1, sVc]
        g3base = [sXa.n, sXb.n, sXc.n]
        s3base = [sW0.n, sW1.n, sWc.n]
        x3base = [sV0.n, sV1.n, sVc.n]
        OUT3 = [sOut, sOutB, sOutC]
        outbase3 = [sOut.n, sOutB.n, sOutC.n]
        if True:
            @blk.vector
            def _(v):
                for tt in range(16):
                    sl = tt % 3
                    v.wait_ge(sA.h, p11_a + tt + 1)
                    v.wait_ge(FS3[sl].h, s3base[sl] + 16 * (tt // 3 + 1))
                    v.wait_ge(FX3[sl].h, x3base[sl] + 16 * (tt // 3 + 1))
                    sV.inc(v.tensor_tensor(fs[sl], fs[sl], fsc, Alu.mult))
                    v.wait_ge(sV.h, sV.n)
                    sV.inc(v.tensor_tensor(fx[sl], fx[sl], fs[sl], Alu.add))

            @blk.scalar
            def _(sc):
                sc.wait_ge(sS1.h, s_dma_done[1])
                for tt in range(16):
                    sl = tt % 3
                    sc.wait_ge(FG3[sl].h, g3base[sl] + 16 * (tt // 3 + 1))
                    if tt >= 1:
                        sc.wait_ge(sV.h, p11_v + 2 * (tt - 1) + 1)
                    sA.inc(sc.activation(fsc, fg[sl], Act.Silu))

            @blk.sync
            def _(sy):
                sy.wait_ge(sGd.h, p10_done_gd)
                sy.wait_ge(sGd1.h, sGd1.n)
                sy.wait_ge(sGs.h, p10_done_gs)
                sy.wait_ge(sS0.h, s_dma_done[0])
                sy.wait_ge(sS1.h, s_dma_done[1])
                for tt in range(18):
                    if tt >= 2:
                        tp = tt - 2
                        sy.wait_ge(sV.h, p11_v + 2 * tp + 2)
                        OUT3[tp % 3].dma(sy.dma_start(
                            out=y_out[tp * 128:(tp + 1) * 128, :], in_=fx[tp % 3]))
                    if tt < 16:
                        sl = tt % 3
                        if tt >= 3:
                            sy.wait_ge(sA.h, p11_a + (tt - 3) + 1)
                            sy.wait_ge(sV.h, p11_v + 2 * (tt - 3) + 2)
                            sy.wait_ge(OUT3[(tt - 3) % 3].h,
                                       outbase3[(tt - 3) % 3]
                                       + 16 * ((tt - 3) // 3 + 1))
                        FG3[sl].dma(sy.dma_start(
                            out=fg[sl], in_=g_stage[tt * 128:(tt + 1) * 128, :]))
                        FS3[sl].dma(sy.dma_start(
                            out=fs[sl], in_=s_stage[tt * 128:(tt + 1) * 128, :]))
                        FX3[sl].dma(sy.dma_start(
                            out=fx[sl], in_=x_att[8 + tt * 128: 8 + (tt + 1) * 128, :]))
                sy.wait_ge(sOut.h, sOut.n)
                sy.wait_ge(sOutB.h, sOutB.n)
                sy.wait_ge(sOutC.h, sOutC.n)

    lower_extended_insts(nc)
    return nc


def _consts():
    c = {}
    c["ident"] = np.eye(128, dtype=np.float32)
    band = np.zeros((128, 128), dtype=np.float32)
    for p_ in range(128):
        for cc in range(128):
            if not (0 <= cc - p_ <= 8):
                band[p_, cc] = -1e30
    c["maskA"] = band
    m0 = band.copy()
    m0[:, 0:8] = -1e30
    c["mask0"] = m0
    io = np.zeros((128, 16), dtype=np.float32)
    for p_ in range(128):
        for jj in range(16):
            io[p_, jj] = p_ + 128 * jj + 1
    c["iota16"] = io
    isl = np.zeros((16, CAPW), dtype=np.float32)
    for j in range(CAP):
        isl[j % 16, j // 16] = float(j)
    c["ioslot"] = isl
    bd = np.zeros((128, EXP), dtype=np.float32)
    for q in range(128):
        bd[q, q % EXP] = 1.0
    c["bd8"] = bd
    c["bdT"] = np.ascontiguousarray(bd.T)
    c["onesrow"] = np.ones((1, 128), dtype=np.float32)
    return c


def kernel(L0_attn_x, L0_attn_Wq, L0_attn_Wk, L0_attn_Wv, L0_attn_Wo,
           L0_ffn_router_w, L0_ffn_router_b, L0_ffn_w1, L0_ffn_b1,
           L0_ffn_w2, L0_ffn_b2, _debug_taps=()):
    key = tuple(sorted(_debug_taps))
    if key not in _CACHE:
        _CACHE[key] = build_program(key)
    nc = _CACHE[key]

    x = np.ascontiguousarray(np.asarray(L0_attn_x, dtype=np.float32)).reshape(16384, D)
    consts = _consts()
    shared = {
        "Wq": np.ascontiguousarray(L0_attn_Wq, dtype=np.float32),
        "Wk": np.ascontiguousarray(L0_attn_Wk, dtype=np.float32),
        "Wv": np.ascontiguousarray(L0_attn_Wv, dtype=np.float32),
        "Wo": np.ascontiguousarray(L0_attn_Wo, dtype=np.float32),
        "rw": np.ascontiguousarray(L0_ffn_router_w, dtype=np.float32),
        "rbias": np.ascontiguousarray(L0_ffn_router_b, dtype=np.float32).reshape(1, EXP),
        "w1": np.ascontiguousarray(L0_ffn_w1, dtype=np.float32),
        "b1": np.ascontiguousarray(L0_ffn_b1, dtype=np.float32),
        "w2": np.ascontiguousarray(L0_ffn_w2, dtype=np.float32),
        "b2": np.ascontiguousarray(L0_ffn_b2, dtype=np.float32),
        "ident": consts["ident"], "maskA": consts["maskA"],
        "iota16": consts["iota16"], "ioslot": consts["ioslot"],
        "bd8": consts["bd8"], "bdT": consts["bdT"], "onesrow": consts["onesrow"],
    }
    in_maps = []
    for c in range(8):
        start = c * TOK
        xatt = np.zeros((TA, D), dtype=np.float32)
        s0 = max(0, start - 8)
        nh = start - s0
        xatt[8 - nh:8] = x[s0:start]
        xatt[8:8 + TOK] = x[start:start + TOK]
        m = dict(shared)
        m["x_att"] = xatt
        xr = np.roll(x, -start, axis=0)
        m["x_full_T"] = np.ascontiguousarray(xr.T)
        m["mask0"] = consts["mask0"] if c in (0, 4) else consts["maskA"]
        in_maps.append(m)
    res = run_bass_kernel_spmd(nc, in_maps, core_ids=list(range(8)))
    outs = [res.results[c]["y"] for c in range(8)]
    y = np.concatenate(outs, axis=0).reshape(2, 8192, D).astype(np.float32)
    if _debug_taps:
        kernel._tap_results = res.results
    return y


# version 19
# speedup vs baseline: 1.0527x; 1.0527x over previous
import contextlib
import numpy as np

import concourse.bass as bass
import concourse.mybir as mybir
import concourse.library_config as libcfg
from concourse.bass_utils import run_bass_kernel_spmd
from concourse.library_overlay import lower_extended_insts

F32 = mybir.dt.float32
I32 = mybir.dt.int32
I16 = mybir.dt.int16
U32 = mybir.dt.uint32
Alu = mybir.AluOpType
Act = mybir.ActivationFunctionType
AX = mybir.AxisListType.X

TOK = 2048
NB = 17
TA = NB * 128
D = 1024
KC = 8
EXP = 8
DFF = 4096
CAP = 384
CAPF = 336
CAPW = CAP // 16
NIT = 32
WINM = 136
INV2T = 1.0 / 0.56
EPS_LN = 1e-5
LOOSE = 1.0 - 1e-4
SENT = 2048
NTILE = 128
XGW = 1088

_CACHE = {}


def build_program(debug_taps=()):
    nc = bass.Bass("TRN2", target_bir_lowering=False)

    x_att = nc.dram_tensor("x_att", [TA, D], F32, kind="ExternalInput").ap()
    x_full_T = nc.dram_tensor("x_full_T", [D, 16384], F32, kind="ExternalInput").ap()
    Wq_d = nc.dram_tensor("Wq", [D, D], F32, kind="ExternalInput").ap()
    Wk_d = nc.dram_tensor("Wk", [D, D], F32, kind="ExternalInput").ap()
    Wv_d = nc.dram_tensor("Wv", [D, D], F32, kind="ExternalInput").ap()
    Wo_d = nc.dram_tensor("Wo", [D, D], F32, kind="ExternalInput").ap()
    rw_d = nc.dram_tensor("rw", [D, EXP], F32, kind="ExternalInput").ap()
    rb_d = nc.dram_tensor("rbias", [1, EXP], F32, kind="ExternalInput").ap()
    w1_d = nc.dram_tensor("w1", [EXP, D, DFF], F32, kind="ExternalInput").ap()
    b1_d = nc.dram_tensor("b1", [EXP, DFF], F32, kind="ExternalInput").ap()
    w2_d = nc.dram_tensor("w2", [EXP, DFF, D], F32, kind="ExternalInput").ap()
    b2_d = nc.dram_tensor("b2", [EXP, D], F32, kind="ExternalInput").ap()
    ident_d = nc.dram_tensor("ident", [128, 128], F32, kind="ExternalInput").ap()
    maskA_d = nc.dram_tensor("maskA", [128, 128], F32, kind="ExternalInput").ap()
    mask0_d = nc.dram_tensor("mask0", [128, 128], F32, kind="ExternalInput").ap()
    iota_d = nc.dram_tensor("iota16", [128, 16], F32, kind="ExternalInput").ap()
    ioslot_d = nc.dram_tensor("ioslot", [16, CAPW], F32, kind="ExternalInput").ap()
    bd8_d = nc.dram_tensor("bd8", [128, EXP], F32, kind="ExternalInput").ap()
    bdT_d = nc.dram_tensor("bdT", [8, 128], F32, kind="ExternalInput").ap()
    ones_d = nc.dram_tensor("onesrow", [1, 128], F32, kind="ExternalInput").ap()

    y_out = nc.dram_tensor("y", [TOK, D], F32, kind="ExternalOutput").ap()
    taps = {}
    tapdefs = {"t_xn": [TA, D], "t_tau": [128, 2 * EXP], "t_scores": [TOK, EXP],
               "t_s": [TOK, D], "t_g": [TOK, D], "t_idx": [16, EXP * CAPW],
               "t_srch": [128, 1024]}
    for name in debug_taps:
        taps[name] = nc.dram_tensor(name, tapdefs[name], F32, kind="ExternalOutput").ap()

    xg_stage = nc.dram_tensor("xg_stage", [SENT + 1, XGW], F32, kind="Internal").ap()
    g_stage = nc.dram_tensor("g_stage", [SENT + 1, D], F32, kind="Internal").ap()
    s_stage = nc.dram_tensor("s_stage", [TOK, D], F32, kind="Internal").ap()

    ctx = contextlib.ExitStack()
    with ctx:
        AR = 51300
        arena = ctx.enter_context(nc.sbuf_tensor("arena", [128, AR], F32)).ap()

        def A(off, n):
            assert off + n <= AR, (off, n)
            return arena[:, off:off + n]

        def A3(off, dims, pcnt=128):
            return bass.AP(arena.tensor, off, [[AR, pcnt]] + dims)

        XA, XT, QKV, WB, SM, PS = 0, 17408, 34816, 41344, 47488, 49536

        xa = A(XA, NB * D)
        xnT = A(XT, KC * TA)

        xT = [A(QKV, 2048), A(QKV + 2048, 2048)]
        srch = A(XA + 6 * TA, 1024)
        gescrD = A(XA + 6 * TA + 1024, 1024)
        lgt = A(XA + 6 * TA + 2048, 96)
        sgP = A(XA + 7 * TA, 1024)
        lgTst = A(XA + 7 * TA + 1024, 1024)
        gescr = A(XA + 7 * TA + 2048, 64)
        fixs = A(XA + 7 * TA + 2112, 64)

        qT = A(QKV, 2048)
        kT = A(QKV + 2048, TA)
        vC = A(QKV + 2048 + TA, 18 * 128)
        wqb = [A(WB, 1024), A(WB + 3072, 1024)]
        wkb = [A(WB + 1024, 1024), A(WB + 4096, 1024)]
        wvb = [A(WB + 2048, 1024), A(WB + 5120, 1024)]

        woSM = A(SM, 2 * D)
        woWB = A(WB, 6 * D)
        stile = [A(QKV, D), A(QKV + D, D)]

        def woap(kc):
            return woSM[:, kc * D:(kc + 1) * D] if kc < 2 else \
                woWB[:, (kc - 2) * D:(kc - 1) * D]

        h1t = A(XA, 32 * CAP)
        toksT = A(XA + 32 * CAP, KC * CAP)
        b1pp = A(XA + 32 * CAP + KC * CAP, 32)
        w1r = [A(XT, 4096), A(XT + 4096, 4096)]
        w2r = [A(XT + 8192, 2048), A(XT + 8192 + 2048, 2048)]
        tg = [A(QKV, 3 * XGW), A(QKV + 3 * XGW, 3 * XGW)]
        tg0a = A(XT + 12288, 3 * XGW)

        def tgbuf(e):
            return tg0a if e == 0 else tg[e % 2]
        eob = A(WB, 3 * 1024)
        b2b = A(WB + 3 * 1024, D)
        zero = A(SM, XGW)

        fg = [A(QKV, D), A(QKV + D, D), A(XT + 12288, D)]
        fx = [A(QKV + 2 * D, D), A(QKV + 3 * D, D), A(XT + 12288 + D, D)]
        fs = [A(QKV + 4 * D, D), A(QKV + 5 * D, D), A(XT + 12288 + 2 * D, D)]
        fy = A(SM, D)
        fsc = A(SM + D, D)

        _p = [PS]

        def P_(n):
            r = A(_p[0], n)
            _p[0] += n
            return r

        identt = P_(128)
        maskAt = P_(128)
        mask0t = P_(128)
        iota16t = P_(16)
        bd8t = P_(EXP)
        bdTt = P_(128)
        onesrt = P_(128)
        rw9 = P_(KC * 9 + 8)
        rwsumb = P_(EXP)
        rbb = P_(EXP)
        taub = P_(EXP)
        taulb = P_(EXP)
        gm_all = P_(128)
        ssum = P_(NB + 1)
        sqsum = P_(NB + 1)
        mnegv = P_(NB + 1)
        varv = P_(NB + 1)
        rstdv = P_(NB + 1)
        tA_ = P_(NB + 1)
        tB_ = P_(NB + 1)
        tC_ = P_(NB + 1)
        g_sq = P_(EXP)
        g_mneg = P_(EXP)
        g_var = P_(EXP)
        g_rstd = P_(EXP)
        g_nm = P_(EXP)
        g_rs = P_(EXP)
        g_ri = P_(EXP)
        g_sA = P_(EXP)
        g_sB = P_(EXP)
        g_sC = P_(EXP)
        lo8 = [P_(1), P_(1)]
        hi8 = [P_(1), P_(1)]
        mid8 = P_(1)
        midpp = P_(1)
        cntp = P_(1)
        cnt8s = P_(1)
        taur = P_(8)
        selvt = P_(16)
        nmx2 = [P_(1), P_(1)]
        rsum2 = [P_(1), P_(1)]
        rinv2 = [P_(1), P_(1)]
        wtsb2 = [P_(128), P_(128)]
        wtsTa2 = [P_(128), P_(128)]
        attnT = [A(XA + kc * TA, TA) for kc in range(KC)]
        assert _p[0] <= AR, _p[0]

        selv_sb = ctx.enter_context(nc.sbuf_tensor("selv_sb", [16, 128], F32)).ap()
        idxraw = ctx.enter_context(nc.sbuf_tensor("idxraw", [16, EXP * CAPW], F32)).ap()
        idxsel = ctx.enter_context(nc.sbuf_tensor("idxsel", [16, EXP * CAPW], F32)).ap()
        c2048 = ctx.enter_context(nc.sbuf_tensor("c2048", [16, CAPW], F32)).ap()
        ioslt = ctx.enter_context(nc.sbuf_tensor("ioslt", [16, CAPW], F32)).ap()
        vmask = ctx.enter_context(nc.sbuf_tensor("vmask", [16, CAPW], U32)).ap()
        pred8 = ctx.enter_context(nc.sbuf_tensor("pred8", [8, 2], U32)).ap()
        idx16 = ctx.enter_context(nc.sbuf_tensor("idx16", [128, EXP * CAPW], I16)).ap()
        nfu = ctx.enter_context(nc.sbuf_tensor("nfu", [16, 2], U32)).ap()
        nfb = ctx.enter_context(nc.sbuf_tensor("nfb", [16, 2], F32)).ap()
        rbrow = ctx.enter_context(nc.sbuf_tensor("rbrow", [1, EXP], F32)).ap()
        b2row = ctx.enter_context(nc.sbuf_tensor("b2row", [1, D], F32)).ap()
        b1tmp = ctx.enter_context(nc.sbuf_tensor("b1tmp", [32, 128], F32)).ap()

        PBk = [ctx.enter_context(nc.psum_tensor(f"ps{i}", [128, 512], F32)).ap()
               for i in range(8)]

        class SemCtr:
            def __init__(self, name):
                self.h = ctx.enter_context(nc.semaphore(name))
                self.n = 0

            def inc(self, instr, k=1):
                instr.then_inc(self.h, k)
                self.n += k
                return self.n

            def dma(self, instr):
                return self.inc(instr, 16)

        sIn = SemCtr("sIn")
        sXa = SemCtr("sXa")
        sXb = SemCtr("sXb")
        sXc = SemCtr("sXc")
        sWc = SemCtr("sWc")
        sVc = SemCtr("sVc")
        sOutB = SemCtr("sOutB")
        sOutC = SemCtr("sOutC")
        sW0 = SemCtr("sW0")
        sW1 = SemCtr("sW1")
        sS0 = SemCtr("sS0")
        sS1 = SemCtr("sS1")
        sV0 = SemCtr("sV0")
        sV1 = SemCtr("sV1")
        sSt = SemCtr("sSt")
        sSt2 = SemCtr("sSt2")
        sOut = SemCtr("sOut")
        sBm = SemCtr("sBm")
        sGd = SemCtr("sGd")
        sGd1 = SemCtr("sGd1")
        sGs = SemCtr("sGs")
        sMM = SemCtr("sMM")
        sV = SemCtr("sV")
        sVg = SemCtr("sVg")
        sA = SemCtr("sA")
        sAe = SemCtr("sAe")
        sGp = SemCtr("sGp")
        XS = [sXa, sXb]
        WS = [sW0, sW1]
        SS = [sS0, sS1]
        VS = [sV0, sV1]

        blk = ctx.enter_context(nc.Block())

        def vchain(v, instr, ctr):
            ctr.inc(instr)
            v.wait_ge(ctr.h, ctr.n)

        RSQ_OPS = 3 + 4 * 4 + 1

        def emit_rsqrt(v, ctr, x_ap, out_ap, ta, tb, tc):
            xi = x_ap.bitcast(I32)
            yi = ta.bitcast(I32)
            vchain(v, v.tensor_scalar(yi, xi, 1, None, op0=Alu.logical_shift_right), ctr)
            vchain(v, v.tensor_scalar(yi, yi, 0, None, op0=Alu.bitwise_not), ctr)
            vchain(v, v.tensor_scalar(yi, yi, 0x5f3759e0, None, op0=Alu.add), ctr)
            for _ in range(4):
                vchain(v, v.tensor_tensor(tb, ta, ta, Alu.mult), ctr)
                vchain(v, v.tensor_tensor(tc, tb, x_ap, Alu.mult), ctr)
                vchain(v, v.tensor_scalar(tc, tc, -0.5, 1.5, op0=Alu.mult, op1=Alu.add), ctr)
                vchain(v, v.tensor_tensor(ta, ta, tc, Alu.mult), ctr)
            vchain(v, v.tensor_copy(out_ap, ta), ctr)

        if True:
            @blk.vector
            def _(v):
                v.memset(zero, 0.0)
                v.memset(c2048, float(SENT))
                v.memset(idx16, 0)
                v.memset(idxraw, -1.0)
                v.wait_ge(sIn.h, 16 * 10)
                sV.inc(v.memset(A3(rw9.offset + EXP, [[9, KC], [1, 1]]), 1.0))

            @blk.sync
            def _(sy):
                sIn.dma(sy.dma_start(out=identt, in_=ident_d))
                sIn.dma(sy.dma_start(out=maskAt, in_=maskA_d))
                sIn.dma(sy.dma_start(out=mask0t, in_=mask0_d))
                sIn.dma(sy.dma_start(out=iota16t, in_=iota_d))
                sIn.dma(sy.dma_start(out=bd8t, in_=bd8_d))
                sIn.dma(sy.dma_start(out=bdTt[0:8, :], in_=bdT_d))
                sIn.dma(sy.dma_start(out=onesrt[0:1, :], in_=ones_d))
                sIn.dma(sy.dma_start(out=ioslt, in_=ioslot_d))
                sIn.dma(sy.dma_start(out=rbrow, in_=rb_d))
                sIn.dma(sy.dma_start(
                    out=A3(rw9.offset, [[9, KC], [1, EXP]]),
                    in_=rw_d.rearrange("(kc p) e -> p kc e", p=128)))
                XQ = [sXc, sWc, sVc, sOutB]
                for q in range(4):
                    nb_q = [4, 4, 4, 5][q]
                    ob_q = [0, 4, 8, 12][q]
                    XQ[q].dma(sy.dma_start(
                        out=A3(XA + ob_q * D, [[D, nb_q], [1, D]]),
                        in_=bass.AP(x_att.tensor, ob_q * 128 * D,
                                    [[D, 128], [128 * D, nb_q], [1, D]])))
                assert sIn.n == 16 * 10
        p0_v = sV.n

        ssb8 = [sS0.n, sS1.n]
        if True:
            @blk.sync
            def _(sy):
                for hx in range(2):
                    WS[hx % 2].dma(sy.dma_start(
                        out=A3(xT[hx % 2].offset, [[256, KC], [1, 256]]),
                        in_=bass.AP(x_full_T.tensor, hx * 256,
                                    [[16384, 128], [128 * 16384, KC], [1, 256]])))
                for c in range(2):
                    sl = c % 2
                    SS[sl].dma(sy.dma_start(
                        out=A3(wqb[sl].offset, [[128, KC], [1, 128]]),
                        in_=Wq_d[:, c * 128:(c + 1) * 128].rearrange(
                            "(kc p) n -> p kc n", p=128)))
                    SS[sl].dma(sy.dma_start(
                        out=A3(wkb[sl].offset, [[128, KC], [1, 128]]),
                        in_=Wk_d[:, c * 128:(c + 1) * 128].rearrange(
                            "(kc p) n -> p kc n", p=128)))
                    SS[sl].dma(sy.dma_start(
                        out=A3(wvb[sl].offset, [[128, KC], [1, 128]]),
                        in_=Wv_d[:, c * 128:(c + 1) * 128].rearrange(
                            "(kc p) n -> p kc n", p=128)))

        if True:
            @blk.vector
            def _(v):
                v.wait_ge(sV.h, p0_v)
                for c in range(NB):
                    q = min(3, c // 4)
                    v.wait_ge([sXc, sWc, sVc, sOutB][q].h, 16)
                    v.tensor_reduce(ssum[:, c:c + 1], xa[:, c * D:(c + 1) * D], AX, Alu.add)
                sV.inc(v.engine_nop())
                v.wait_ge(sV.h, sV.n)
                v.wait_ge(sA.h, sA.n + NB)
                S, Q = ssum[:, 0:NB], sqsum[:, 0:NB]
                M, VV, R = mnegv[:, 0:NB], varv[:, 0:NB], rstdv[:, 0:NB]
                Ta, Tb, Tc = tA_[:, 0:NB], tB_[:, 0:NB], tC_[:, 0:NB]
                vchain(v, v.tensor_scalar(M, S, -1.0 / D, None, op0=Alu.mult), sV)
                vchain(v, v.tensor_tensor(Ta, M, M, Alu.mult), sV)
                vchain(v, v.tensor_scalar(Tb, Q, 1.0 / D, None, op0=Alu.mult), sV)
                vchain(v, v.tensor_tensor(VV, Tb, Ta, Alu.subtract), sV)
                vchain(v, v.tensor_scalar(VV, VV, 0.0, float(EPS_LN),
                                          op0=Alu.max, op1=Alu.add), sV)
                emit_rsqrt(v, sV, VV, R, Ta, Tb, Tc)
                for c in range(NB):
                    sV.inc(v.tensor_scalar(
                        xa[:, c * D:(c + 1) * D], xa[:, c * D:(c + 1) * D],
                        mnegv[:, c:c + 1], rstdv[:, c:c + 1],
                        op0=Alu.add, op1=Alu.mult))

            @blk.scalar
            def _(sc):
                sc.wait_ge(sV.h, p0_v)
                for c in range(NB):
                    sc.wait_ge([sXc, sWc, sVc, sOutB][min(3, c // 4)].h, 16)
                    sc.wait_ge(sA.h, sA.n)
                    sA.inc(sc.activation(vC[:, 0:1024], xa[:, c * D:(c + 1) * D],
                                         Act.Square,
                                         accum_out=sqsum[:, c:c + 1]))
        ln_done = sV.n

        _p2xn = [0]
        _p2sta = [0]
        if True:
            @blk.sync
            def _(sy):
                sy.wait_ge(sV.h, ln_done)
                if "t_xn" in taps:
                    sSt2.dma(sy.dma_start(
                        out=taps["t_xn"].rearrange("(b p) d -> p b d", p=128),
                        in_=A3(XA, [[D, NB], [1, D]])))
                for b in list(range(11, 16)) + list(range(0, 11)):
                    sb_ = sSt2 if b >= 11 else sSt
                    sb_.dma(sy.dma_start(
                        out=xg_stage[b * 128: b * 128 + 120, 0:D],
                        in_=xa[8:128, b * D:(b + 1) * D]))
                    sb_.dma(sy.dma_start(
                        out=xg_stage[b * 128 + 120: b * 128 + 128, 0:D],
                        in_=xa[0:8, (b + 1) * D:(b + 2) * D]))
                    if b == 15:
                        _p2xn[0] = sSt2.n
                sSt.dma(sy.dma_start(out=xg_stage[SENT:SENT + 1, :], in_=zero[0:1, :]))
                for b in range(16):
                    sSt.dma(sy.dma_start(
                        out=xg_stage[b * 128:(b + 1) * 128, 1024 + EXP:XGW],
                        in_=zero[:, 0:XGW - 1024 - EXP]))
                for b in range(16):
                    sSt.dma(sy.dma_start(out=g_stage[b * 128:(b + 1) * 128, :],
                                         in_=zero[:, 0:D]))
                sSt.dma(sy.dma_start(out=g_stage[SENT:SENT + 1, :], in_=zero[0:1, 0:D]))
                _p2sta[0] = sSt.n


        p3_mm = sMM.n
        p3_v = sV.n
        if True:
            @blk.vector
            def _(v):
                for i in range(NB * 2):
                    c, q = i // 2, i % 2
                    v.wait_ge(sMM.h, p3_mm + i + 1)
                    sV.inc(v.tensor_copy(
                        A3(XT + (4 * q) * TA + c * 128, [[TA, 4], [1, 128]]),
                        bass.AP(PBk[q].tensor, 0, [[512, 128], [128, 4], [1, 128]])))

            @blk.tensor
            def _(te):
                te.wait_ge(sV.h, ln_done)
                for i in range(NB * 2):
                    c, q = i // 2, i % 2
                    if i >= 2:
                        te.wait_ge(sV.h, p3_v + (i - 2) + 1)
                    for k4 in range(4):
                        kc = 4 * q + k4
                        mi = te.matmul(PBk[q][:, k4 * 128:(k4 + 1) * 128],
                                  xa[:, c * D + kc * 128: c * D + (kc + 1) * 128],
                                  identt, start=(k4 == 0), stop=(k4 == 3),
                                  is_transpose=True)
                    sMM.inc(mi)
        p3_done_v = sV.n

        p4i_mm = sMM.n
        p4i_v = sV.n
        if True:
            @blk.vector
            def _(v):
                v.wait_ge(sMM.h, p4i_mm + 1)
                sV.inc(v.tensor_copy(taur[0:8, 0:1], PBk[2][0:8, 0:1]))
                v.wait_ge(sMM.h, p4i_mm + 2)
                sV.inc(v.tensor_copy(taur[0:1, 0:8], PBk[3][0:1, 0:8]))
                v.wait_ge(sMM.h, p4i_mm + 3)
                v.tensor_copy(rwsumb, PBk[4][:, 0:8])
                sV.inc(v.tensor_copy(rbb, PBk[5][:, 0:8]))

            @blk.tensor
            def _(te):
                te.wait_ge(sV.h, p0_v)
                for kc in range(KC):
                    mi = te.matmul(PBk[2][0:8, 0:1],
                              A3(rw9.offset + kc * 9, [[1, EXP]]),
                              A3(rw9.offset + kc * 9 + EXP, [[1, 1]]),
                              start=(kc == 0), stop=(kc == KC - 1))
                sMM.inc(mi)
                te.wait_ge(sV.h, p4i_v + 1)
                mi = te.matmul(PBk[3][0:1, 0:8], taur[0:8, 0:1], identt[0:8, 0:8],
                          start=True, stop=True, is_transpose=True)
                sMM.inc(mi)
                te.wait_ge(sV.h, p4i_v + 2)
                mi = te.matmul(PBk[4][:, 0:8], onesrt[0:1, :], taur[0:1, 0:8],
                          start=True, stop=True)
                mi = te.matmul(PBk[5][:, 0:8], onesrt[0:1, :], rbrow, start=True, stop=True)
                sMM.inc(mi)
        p4i_done = sV.n

        p4_mm = sMM.n
        p4_v = sV.n
        p4_ae = sAe.n
        p4_vg = sVg.n
        GPOPS = 5 + RSQ_OPS + 4 + 2 + 3
        grp_scores_done = []
        grp_exp_wait = []
        p4_rel_evac = []
        NGRP = 16

        gtmp = srch[:, 0:128]

        if True:
            @blk.vector
            def _(v):
                v.wait_ge(sSt2.h, _p2xn[0])
                for g in range(NGRP):
                    gb_mm = p4_mm + g * 20
                    for q4 in range(4):
                        v.wait_ge(sMM.h, gb_mm + 3 * q4 + 1)
                        sV.inc(v.tensor_copy(lgTst[0:9, q4 * 256:(q4 + 1) * 256],
                                             PBk[2 + q4 % 2][0:9, 0:256]))
                        for tid in range(2):
                            v.wait_ge(sMM.h, gb_mm + 3 * q4 + 2 + tid)
                            vchain(v, v.tensor_tensor(
                                gtmp, PBk[4 + tid][:, 0:128], identt, Alu.mult), sV)
                            vchain(v, v.tensor_reduce(
                                g_sq[:, 2 * q4 + tid:2 * q4 + tid + 1],
                                gtmp, AX, Alu.add), sV)
                    for jj in range(8):
                        v.wait_ge(sMM.h, gb_mm + 12 + jj + 1)
                        sV.inc(v.tensor_copy(lgt[:, jj * 9:(jj + 1) * 9],
                                             PBk[jj % 2][:, 0:9]))
                    h, gp = g // 2, g % 2
                    L8 = A3(lgt.offset, [[9, 8], [1, 8]])
                    SX8 = A3(lgt.offset + 8, [[9, 8]])
                    FS_ = A3(fixs.offset, [[8, 8], [1, 8]])
                    EXG = A3(gescr.offset, [[8, 8], [1, 8]])
                    rw_bc = A3(rwsumb.offset, [[0, 8], [1, 8]])
                    rb_bc = A3(rbb.offset, [[0, 8], [1, 8]])
                    mn_bc = A3(g_mneg.offset, [[1, 8], [0, 8]])
                    rstd_bc = A3(g_rstd.offset, [[1, 8], [0, 8]])
                    nm_bc = A3(g_nm.offset, [[1, 8], [0, 8]])
                    ri_bc = A3(g_ri.offset, [[1, 8], [0, 8]])
                    v.wait_ge(sV.h, sV.n)
                    base_vg = sVg.n
                    vchain(v, v.tensor_scalar(g_mneg, SX8, -1.0 / D, None,
                                              op0=Alu.mult), sVg)
                    vchain(v, v.tensor_tensor(g_sA, g_mneg, g_mneg, Alu.mult), sVg)
                    vchain(v, v.tensor_scalar(g_sB, g_sq, 1.0 / D, None,
                                              op0=Alu.mult), sVg)
                    vchain(v, v.tensor_tensor(g_var, g_sB, g_sA, Alu.subtract), sVg)
                    vchain(v, v.tensor_scalar(g_var, g_var, 0.0, float(EPS_LN),
                                              op0=Alu.max, op1=Alu.add), sVg)
                    emit_rsqrt(v, sVg, g_var, g_rstd, g_sA, g_sB, g_sC)
                    vchain(v, v.tensor_tensor(FS_, rw_bc, mn_bc, Alu.mult), sVg)
                    vchain(v, v.tensor_tensor(FS_, FS_, L8, Alu.add), sVg)
                    vchain(v, v.tensor_tensor(FS_, FS_, rstd_bc, Alu.mult), sVg)
                    vchain(v, v.tensor_tensor(FS_, FS_, rb_bc, Alu.add), sVg)
                    vchain(v, v.tensor_reduce(g_nm, FS_, AX, Alu.max, negate=True), sVg)
                    vchain(v, v.tensor_tensor(FS_, FS_, nm_bc, Alu.add), sVg)
                    grp_exp_wait.append(sVg.n)
                    v.wait_ge(sAe.h, p4_ae + g + 1)
                    vchain(v, v.tensor_reduce(g_rs, EXG, AX, Alu.add), sVg)
                    vchain(v, v.reciprocal(g_ri, g_rs), sVg)
                    vchain(v, v.tensor_tensor(
                        A3(sgP.offset + h * 128 + gp * 64, [[8, 8], [1, 8]]),
                        EXG, ri_bc, Alu.mult), sVg)
                    grp_scores_done.append(sVg.n)
                    assert sVg.n - base_vg == GPOPS
                for h in range(8):
                    v.wait_ge(sMM.h, p4_mm + NGRP * 20 + h + 1)
                    p4_rel_evac.append(sV.inc(v.tensor_copy(
                        srch[:, h * 128:(h + 1) * 128], PBk[h % 2][:, 0:128])))

            @blk.scalar
            def _(sc):
                sc.wait_ge(sSt2.h, _p2xn[0])
                for g in range(NGRP):
                    sc.wait_ge(sVg.h, grp_exp_wait[g])
                    sAe.inc(sc.activation(A3(gescr.offset, [[8, 8], [1, 8]]),
                                          A3(fixs.offset, [[8, 8], [1, 8]]),
                                          Act.Exp))

            @blk.tensor
            def _(te):
                te.wait_ge(sV.h, p3_done_v)
                te.wait_ge(sV.h, p4i_done)
                for g in range(NGRP):
                    gb_v = p4_v + g * 28
                    for q4 in range(4):
                        hx = g * 4 + q4
                        sl = hx % 2
                        te.wait_ge(WS[sl].h, 16 * (hx // 2 + 1))
                        if hx >= 2:
                            pg, pq = (hx - 2) // 4, (hx - 2) % 4
                            te.wait_ge(sV.h, p4_v + pg * 28 + 5 * pq + 1)
                        for kc in range(KC):
                            mi = te.matmul(
                                PBk[2 + q4 % 2][0:9, 0:256],
                                A3(rw9.offset + kc * 9, [[1, 9]]),
                                xT[sl][:, kc * 256:(kc + 1) * 256],
                                start=(kc == 0), stop=(kc == KC - 1))
                        sMM.inc(mi)
                        for tid in range(2):
                            if hx >= 1:
                                pg, pq = (hx - 1) // 4, (hx - 1) % 4
                                te.wait_ge(sV.h,
                                           p4_v + pg * 28 + 5 * pq + 2 + 2 * tid)
                            for kc in range(KC):
                                xtile = xT[sl][:, kc * 256 + tid * 128:
                                               kc * 256 + (tid + 1) * 128]
                                mi = te.matmul(PBk[4 + tid][:, 0:128],
                                               xtile, xtile,
                                               start=(kc == 0), stop=(kc == KC - 1))
                            sMM.inc(mi)
                    te.wait_ge(sV.h, gb_v + 16)
                    for jj in range(8):
                        if jj >= 2:
                            te.wait_ge(sV.h, gb_v + 20 + (jj - 2) + 1)
                        elif g >= 1:
                            te.wait_ge(sV.h, p4_v + (g - 1) * 28 + 20 + (jj + 6) + 1)
                        mi = te.matmul(PBk[jj % 2][:, 0:9],
                                       lgTst[0:9, jj * 128:(jj + 1) * 128],
                                       identt[0:9, 0:9],
                                       start=True, stop=True, is_transpose=True)
                        sMM.inc(mi)
                te.wait_ge(sV.h, p4_v + NGRP * 28)
                for h in range(8):
                    te.wait_ge(sVg.h, grp_scores_done[2 * h + 1])
                    if h >= 2:
                        te.wait_ge(sV.h, p4_rel_evac[h - 2])
                    mi = te.matmul(PBk[h % 2][:, 0:128], sgP[:, h * 128:(h + 1) * 128],
                                   identt, start=True, stop=True, is_transpose=True)
                    sMM.inc(mi)

            @blk.sync
            def _(sy):
                for hx in range(2, 64):
                    sl = hx % 2
                    if True:
                        pg, pq = (hx - 2) // 4, (hx - 2) % 4
                        sy.wait_ge(sMM.h, p4_mm + pg * 20 + 3 * pq + 3)
                    WS[sl].dma(sy.dma_start(
                        out=A3(xT[sl].offset, [[256, KC], [1, 256]]),
                        in_=bass.AP(x_full_T.tensor, hx * 256,
                                    [[16384, 128], [128 * 16384, KC], [1, 256]])))

        p4_done_v = sV.n
        p4_done_vg = sVg.n
        p4_done_mm = sMM.n

        p5_vg = sVg.n
        p5_mid_ready = []
        p5_midpp_evac = []
        p5_cnt_ready = []
        p5_cnt_evac = []
        p5_fin = []
        p7_selv_evac = []
        p7_nfready = []
        p7_nfb_evac = []
        p7_selvt_done = []
        p7_gp = sGp.n
        fe_pairs = []

        def _p5v_init(v):
            v.wait_ge(sV.h, p4_done_v)
            v.wait_ge(sVg.h, p4_done_vg)
            vchain(v, v.memset(lo8[0][0:8, :], 0.0), sVg)
            vchain(v, v.memset(hi8[0][0:8, :], 2.0), sVg)

        def _p5t_init(te):
            te.wait_ge(sV.h, p4_done_v)

        fe_pairs.append((_p5v_init, _p5t_init))

        def _p5v_iter(it):
            def f(v):
                pa, pb = it % 2, 1 - it % 2
                vchain(v, v.tensor_tensor(mid8[0:8, :], lo8[pa][0:8, :],
                                          hi8[pa][0:8, :], Alu.add), sVg)
                vchain(v, v.tensor_scalar(mid8[0:8, :], mid8[0:8, :], 0.5, None,
                                          op0=Alu.mult), sVg)
                p5_mid_ready.append(sVg.n)
                v.wait_ge(sBm.h, it * 2 + 1)
                vchain(v, v.tensor_copy(midpp, PBk[3][:, 0:1]), sVg)
                p5_midpp_evac.append(sVg.n)
                vchain(v, v.tensor_scalar(gescrD, srch, midpp, None,
                                          op0=Alu.is_ge, op1=Alu.add,
                                          accum_out=cntp), sVg)
                p5_cnt_ready.append(sVg.n)
                v.wait_ge(sBm.h, it * 2 + 2)
                vchain(v, v.tensor_copy(cnt8s[0:8, :], PBk[3][0:8, 0:1]), sVg)
                p5_cnt_evac.append(sVg.n)
                vchain(v, v.tensor_scalar(pred8[:, 0:1], cnt8s[0:8, :], 2048.0,
                                          None, op0=Alu.is_ge), sVg)
                sVg.inc(v.select(lo8[pb][0:8, :], pred8[:, 0:1], mid8[0:8, :],
                                 lo8[pa][0:8, :], add_drain=True))
                v.wait_ge(sVg.h, sVg.n)
                sVg.inc(v.select(hi8[pb][0:8, :], pred8[:, 0:1], hi8[pa][0:8, :],
                                 mid8[0:8, :], add_drain=True))
                v.wait_ge(sVg.h, sVg.n)
            return f

        def _p5t_iter(it):
            def f(te):
                te.wait_ge(sVg.h, p5_mid_ready[it])
                if it >= 1:
                    te.wait_ge(sVg.h, p5_midpp_evac[it - 1])
                    te.wait_ge(sVg.h, p5_cnt_evac[it - 1])
                sBm.inc(te.matmul(PBk[3][:, 0:1], bdTt[0:8, :], mid8[0:8, 0:1],
                                  start=True, stop=True))
                te.wait_ge(sVg.h, p5_cnt_ready[it])
                sBm.inc(te.matmul(PBk[3][0:8, 0:1], bd8t, cntp,
                                  start=True, stop=True))
            return f

        for _it in range(NIT):
            fe_pairs.append((_p5v_iter(_it), _p5t_iter(_it)))

        def _p5v_fin(v):
            p5_fin.append(sVg.n)
            v.wait_ge(sBm.h, NIT * 2 + 1)
            vchain(v, v.tensor_copy(taur[0:1, 0:8], PBk[3][0:1, 0:8]), sVg)
            p5_fin.append(sVg.n)
            v.wait_ge(sBm.h, NIT * 2 + 2)
            vchain(v, v.tensor_copy(taub, PBk[3][:, 0:8]), sVg)
            vchain(v, v.tensor_scalar(taulb, taub, float(LOOSE), None,
                                      op0=Alu.mult), sVg)
            for e in range(EXP):
                ssl = A3(sgP.offset + e, [[8, 16]])
                sVg.inc(v.scalar_tensor_tensor(
                    A3(gm_all.offset + e, [[8, 16]]),
                    ssl, taub[:, e:e + 1], ssl, op0=Alu.is_ge, op1=Alu.mult))
            v.wait_ge(sVg.h, sVg.n)
            sVg.inc(v.engine_nop())

        def _p5t_fin(te):
            te.wait_ge(sVg.h, p5_fin[0])
            sBm.inc(te.matmul(PBk[3][0:1, 0:8], lo8[NIT % 2][0:8, 0:1],
                              identt[0:8, 0:8],
                              start=True, stop=True, is_transpose=True))
            te.wait_ge(sVg.h, p5_fin[1])
            sBm.inc(te.matmul(PBk[3][:, 0:8], onesrt[0:1, :], taur[0:1, 0:8],
                              start=True, stop=True))

        fe_pairs.append((_p5v_fin, _p5t_fin))
        p6_vg_box = [0]

        def _p6v_mark(v):
            p6_vg_box[0] = sVg.n

        fe_pairs.append((_p6v_mark, None))

        def _p7v(e):
            def f(v):
                ssl = A3(sgP.offset + e, [[8, 16]])
                if e >= 1:
                    v.wait_ge(sBm.h, 2 * NIT + 2 + 2 * e)
                vchain(v, v.scalar_tensor_tensor(
                    selvt, ssl, taulb[:, e:e + 1], iota16t,
                    op0=Alu.is_ge, op1=Alu.mult), sVg)
                vchain(v, v.tensor_scalar(selvt, selvt, -1.0, None,
                                          op0=Alu.add), sVg)
                p7_selvt_done.append(sVg.n)
                v.wait_ge(sBm.h, 2 * NIT + 2 + 2 * e + 1)
                vchain(v, v.tensor_copy(selv_sb, PBk[3][0:16, 0:128]), sVg)
                p7_selv_evac.append(sVg.n)
                v.wait_ge(sGp.h, p7_gp + e + 1)
                vchain(v, v.tensor_copy(nfb[0:1, 0:1], nfu[0:1, 0:1]), sVg)
                p7_nfready.append(sVg.n)
                v.wait_ge(sBm.h, 2 * NIT + 2 + 2 * e + 2)
                vchain(v, v.tensor_copy(nfb[0:16, 1:2], PBk[3][0:16, 0:1]), sVg)
                p7_nfb_evac.append(sVg.n)
                vchain(v, v.tensor_scalar(vmask, ioslt, nfb[0:16, 1:2], None,
                                          op0=Alu.is_lt), sVg)
                sVg.inc(v.select(idxsel[:, e * CAPW:(e + 1) * CAPW], vmask,
                                 idxraw[:, e * CAPW:(e + 1) * CAPW], c2048,
                                 add_drain=True))
                v.wait_ge(sVg.h, sVg.n)
                vchain(v, v.tensor_copy(idx16[0:16, e * CAPW:(e + 1) * CAPW],
                                        idxsel[:, e * CAPW:(e + 1) * CAPW]), sVg)
            return f

        def _p7t(e):
            def f(te):
                te.wait_ge(sVg.h, p7_selvt_done[e])
                if e >= 1:
                    te.wait_ge(sVg.h, p7_selv_evac[e - 1])
                sBm.inc(te.matmul(PBk[3][0:16, 0:128], selvt, identt,
                                  start=True, stop=True, is_transpose=True))
                te.wait_ge(sVg.h, p7_nfready[e])
                if e >= 1:
                    te.wait_ge(sVg.h, p7_nfb_evac[e - 1])
                sBm.inc(te.matmul(PBk[3][0:16, 0:1], onesrt[0:1, 0:16],
                                  nfb[0:1, 0:1], start=True, stop=True))
            return f

        for _e in range(EXP):
            fe_pairs.append((_p7v(_e), _p7t(_e)))
        p7dv_box = [0]

        def _p7v_mark(v):
            p7dv_box[0] = sVg.n

        fe_pairs.append((_p7v_mark, None))


        fe_vi = [0, 0]
        fe_ti = [0, 0]

        def weave_v(v):
            fe_vi[0] += 1
            if fe_vi[0] % 4 == 0 and fe_vi[1] < len(fe_pairs):
                fn = fe_pairs[fe_vi[1]][0]
                fe_vi[1] += 1
                if fn is not None:
                    fn(v)

        def weave_t(te):
            fe_ti[0] += 1
            if fe_ti[0] % 4 == 0 and fe_ti[1] < len(fe_pairs):
                fn = fe_pairs[fe_ti[1]][1]
                fe_ti[1] += 1
                if fn is not None:
                    fn(te)

        p8_mm = sMM.n
        p8_v = sV.n
        p8_a = sA.n
        NJT = 18
        NHT = 2 * NJT
        PE_PER_C = 27 + 3 * NHT
        V_PER_C = 27 + 5 * NHT
        A_PER_C = 2 * NHT
        QP, KP = 4, 5

        def mmv(k):
            return 27 + 3 * k

        p8_ev = {}
        p8_sc = {}
        p8_nm = {}
        p8_qkv = {}
        pe_scores = {}
        pe_T = {}
        pe_attn = {}
        _n = 27
        for _k in range(NHT + 2):
            if _k < NHT:
                _n += 1; pe_scores[_k] = _n
            if 1 <= _k <= NHT:
                _n += 1; pe_T[_k - 1] = _n
            if _k >= 2:
                _n += 1; pe_attn[_k - 2] = _n
        assert _n == PE_PER_C, _n

        if True:
            @blk.vector
            def _(v):
                v.wait_ge(sSt.h, _p2sta[0])
                for c in range(KC):
                    vb = p8_v + c * V_PER_C
                    mb = p8_mm + c * PE_PER_C
                    ab = p8_a + c * A_PER_C
                    for i in range(QP):
                        v.wait_ge(sMM.h, mb + i + 1)
                        sV.inc(v.tensor_scalar(qT[:, i * 512:(i + 1) * 512],
                                               PBk[i % 2][:, 0:512],
                                               float(INV2T), None, op0=Alu.mult))
                        weave_v(v)
                    for i in range(KP):
                        npz = 512 if i < 4 else 128
                        v.wait_ge(sMM.h, mb + QP + i + 1)
                        sV.inc(v.tensor_copy(kT[:, i * 512:i * 512 + npz],
                                             PBk[i % 2][:, 0:npz]))
                        weave_v(v)
                    for b in range(NJT):
                        v.wait_ge(sMM.h, mb + QP + KP + b + 1)
                        sV.inc(v.tensor_copy(vC[:, b * 128:(b + 1) * 128],
                                             PBk[b % 2][:, 0:128]))
                        weave_v(v)
                    p8_qkv[c] = sV.n
                    def qwof(jt):
                        return 120 if jt < NJT - 1 else TOK - 120 * (NJT - 1)
                    for k in range(NHT + 2):
                        if k < NHT:
                            h2, jt = k // NJT, k % NJT
                            qw = qwof(jt)
                            sb = [4, 5, 6][k % 3]
                            mt = mask0t if jt == 0 else maskAt
                            v.wait_ge(sMM.h, mb + pe_scores[k])
                            sV.inc(v.tensor_tensor(PBk[sb][0:qw, 0:128],
                                                   PBk[sb][0:qw, 0:128],
                                                   mt[0:qw, 0:128], Alu.add))
                            v.wait_ge(sV.h, sV.n)
                            sV.inc(v.tensor_reduce(nmx2[k % 2][0:qw, :],
                                                   PBk[sb][0:qw, 0:128], AX,
                                                   Alu.max, negate=True))
                            p8_nm[(c, k)] = sV.n
                        if 1 <= k <= NHT:
                            kp = k - 1
                            qwp = qwof(kp % NJT)
                            v.wait_ge(sA.h, ab + 2 * kp + 1)
                            sV.inc(v.reciprocal(rinv2[kp % 2][0:qwp, :],
                                                rsum2[kp % 2][0:qwp, :]))
                            v.wait_ge(sV.h, sV.n)
                            sV.inc(v.tensor_scalar(wtsb2[kp % 2][0:qwp, :],
                                                   wtsb2[kp % 2][0:qwp, :],
                                                   rinv2[kp % 2][0:qwp, :], None,
                                                   op0=Alu.mult))
                            p8_sc[(c, kp)] = sV.n
                        if k >= 2:
                            kp = k - 2
                            h2p, jtp = kp // NJT, kp % NJT
                            qwp = qwof(jtp)
                            v.wait_ge(sMM.h, mb + pe_attn[kp])
                            sV.inc(v.tensor_copy(
                                attnT[c][64 * h2p:64 * h2p + 64,
                                         jtp * 120:jtp * 120 + qwp],
                                PBk[[7, 2][kp % 2]][64 * h2p:64 * h2p + 64, 0:qwp]))
                            p8_ev[(c, kp)] = sV.n
                        weave_v(v)

            @blk.scalar
            def _(sc):
                for c in range(KC):
                    mb = p8_mm + c * PE_PER_C
                    for k in range(NHT + 1):
                        if k < NHT:
                            jt = k % NJT
                            qw = 120 if jt < NJT - 1 else TOK - 120 * (NJT - 1)
                            sb = [4, 5, 6][k % 3]
                            sc.wait_ge(sV.h, p8_nm[(c, k)])
                            if k >= 2:
                                sc.wait_ge(sMM.h, mb + pe_T[k - 2])
                            elif c >= 1:
                                sc.wait_ge(sMM.h, p8_mm + (c - 1) * PE_PER_C
                                           + pe_T[NHT - 2 + k])
                            sA.inc(sc.activation(wtsb2[k % 2][0:qw, :],
                                                 PBk[sb][0:qw, 0:128],
                                                 Act.Exp, bias=nmx2[k % 2][0:qw, :],
                                                 scale=1.0,
                                                 accum_out=rsum2[k % 2][0:qw, :]))
                        if k >= 1:
                            kp = k - 1
                            qwp = 120 if kp % NJT < NJT - 1 else TOK - 120 * (NJT - 1)
                            sc.wait_ge(sMM.h, mb + pe_T[kp])
                            if kp >= 2:
                                sc.wait_ge(sMM.h, mb + pe_attn[kp - 2])
                            elif c >= 1:
                                sc.wait_ge(sMM.h, p8_mm + (c - 1) * PE_PER_C
                                           + pe_attn[NHT - 2 + kp])
                            sA.inc(sc.copy(wtsTa2[kp % 2][:, 0:qwp],
                                           PBk[kp % 2][:, 0:qwp]))

            @blk.tensor
            def _(te):
                te.wait_ge(sV.h, p4_done_v)
                for c in range(KC):
                    vb = p8_v + c * V_PER_C
                    ab = p8_a + c * A_PER_C
                    mb = p8_mm + c * PE_PER_C
                    sl = c % 2
                    te.wait_ge(SS[sl].h, ssb8[sl] + 48 * (c // 2 + 1))
                    if c == 0:
                        te.wait_ge(sV.h, p3_done_v)
                    for i in range(QP):
                        bk = i % 2
                        if i >= 2:
                            te.wait_ge(sV.h, vb + (i - 2) + 1)
                        elif c >= 1:
                            te.wait_ge(sA.h, p8_a + c * A_PER_C)
                            te.wait_ge(sV.h, p8_v + c * V_PER_C)
                        for kc in range(KC):
                            mi = te.matmul(PBk[bk][:, 0:512],
                                           wqb[sl][:, kc * 128:(kc + 1) * 128],
                                           xnT[:, kc * TA + 8 + i * 512:
                                               kc * TA + 8 + (i + 1) * 512],
                                           start=(kc == 0), stop=(kc == KC - 1))
                        sMM.inc(mi)
                        weave_t(te)
                    for i in range(KP):
                        npz = 512 if i < 4 else 128
                        bk = i % 2
                        te.wait_ge(sV.h, vb + QP + (i - 2) + 1 if i >= 2 else vb + QP)
                        for kc in range(KC):
                            mi = te.matmul(PBk[bk][:, 0:npz],
                                           wkb[sl][:, kc * 128:(kc + 1) * 128],
                                           xnT[:, kc * TA + i * 512: kc * TA + i * 512 + npz],
                                           start=(kc == 0), stop=(kc == KC - 1))
                        sMM.inc(mi)
                        weave_t(te)
                    for b in range(NJT):
                        bk = b % 2
                        te.wait_ge(sV.h, vb + QP + KP + (b - 2) + 1 if b >= 2
                                   else vb + QP + KP)
                        for kc in range(KC):
                            mi = te.matmul(PBk[bk][:, 0:128],
                                           xnT[:, kc * TA + b * 120: kc * TA + b * 120 + 128],
                                           wvb[sl][:, kc * 128:(kc + 1) * 128],
                                           start=(kc == 0), stop=(kc == KC - 1))
                        sMM.inc(mi)
                        weave_t(te)
                    for k in range(NHT + 2):
                        if k < NHT:
                            h2, jt = k // NJT, k % NJT
                            qw = 120 if jt < NJT - 1 else TOK - 120 * (NJT - 1)
                            sb = [4, 5, 6][k % 3]
                            te.wait_ge(sV.h, p8_qkv[c])
                            if k >= 3:
                                te.wait_ge(sA.h, ab + 2 * (k - 3) + 1)
                            elif c >= 1:
                                te.wait_ge(sA.h, p8_a + (c - 1) * A_PER_C
                                           + 2 * (NHT - 3 + k) + 1)
                            mi = te.matmul(PBk[sb][0:qw, 0:128],
                                           qT[64 * h2:64 * h2 + 64,
                                              jt * 120: jt * 120 + qw],
                                           kT[64 * h2:64 * h2 + 64,
                                              jt * 120: jt * 120 + 128],
                                           start=True, stop=True)
                            sMM.inc(mi)
                        if 1 <= k <= NHT:
                            kp = k - 1
                            qwp = 120 if kp % NJT < NJT - 1 else TOK - 120 * (NJT - 1)
                            te.wait_ge(sV.h, p8_sc[(c, kp)])
                            mi = te.matmul(PBk[kp % 2][:, 0:qwp],
                                           wtsb2[kp % 2][0:qwp, 0:128],
                                           identt[0:qwp, 0:qwp],
                                           start=True, stop=True, is_transpose=True)
                            sMM.inc(mi)
                        if k >= 2:
                            kp = k - 2
                            h2p, jtp = kp // NJT, kp % NJT
                            qwp = 120 if jtp < NJT - 1 else TOK - 120 * (NJT - 1)
                            te.wait_ge(sA.h, ab + 2 * kp + 2)
                            if kp >= 2:
                                te.wait_ge(sV.h, p8_ev[(c, kp - 2)])
                            elif c >= 1:
                                te.wait_ge(sV.h, p8_ev[(c - 1, NHT - 2 + kp)])
                            tp = {"tile_position": (0, 64)} if h2p else {}
                            mi = te.matmul(PBk[[7, 2][kp % 2]][64 * h2p:64 * h2p + 64,
                                                               0:qwp],
                                           vC[0:128, jtp * 128 + 64 * h2p:
                                              jtp * 128 + 64 * h2p + 64],
                                           wtsTa2[kp % 2][:, 0:qwp],
                                           start=True, stop=True, **tp)
                            sMM.inc(mi)
                        weave_t(te)

            @blk.sync
            def _(sy):
                for c in range(2, KC):
                    sl = c % 2
                    sy.wait_ge(sMM.h, p8_mm + (c - 1) * PE_PER_C + 27)
                    SS[sl].dma(sy.dma_start(
                        out=A3(wqb[sl].offset, [[128, KC], [1, 128]]),
                        in_=Wq_d[:, c * 128:(c + 1) * 128].rearrange(
                            "(kc p) n -> p kc n", p=128)))
                    SS[sl].dma(sy.dma_start(
                        out=A3(wkb[sl].offset, [[128, KC], [1, 128]]),
                        in_=Wk_d[:, c * 128:(c + 1) * 128].rearrange(
                            "(kc p) n -> p kc n", p=128)))
                    SS[sl].dma(sy.dma_start(
                        out=A3(wvb[sl].offset, [[128, KC], [1, 128]]),
                        in_=Wv_d[:, c * 128:(c + 1) * 128].rearrange(
                            "(kc p) n -> p kc n", p=128)))
        p8_done_v = sV.n
        p8_done_mm = sMM.n
        p8_done_a = sA.n
        p7_done_vg = p7dv_box[0]
        p6_vg = p6_vg_box[0]
        assert fe_vi[1] == len(fe_pairs) and fe_ti[1] == len(fe_pairs)

        if True:
            @blk.gpsimd
            def _(gp):
                gp.load_library(libcfg.sparse_gather)
                for e in range(EXP):
                    gp.wait_ge(sVg.h, p7_selv_evac[e])
                    sGp.inc(gp.sparse_gather(idxraw[:, e * CAPW:(e + 1) * CAPW],
                                             selv_sb, num_found=nfu[0:1, 0:1]))
                gp.wait_ge(sVg.h, p7_done_vg)
                sGd.dma(gp.dma_start(out=idx16[16:32, :], in_=idx16[0:16, :]))

            @blk.sync
            def _(sy):
                sy.wait_ge(sVg.h, p6_vg)
                sSt.dma(sy.dma_start(
                    out=bass.AP(xg_stage.tensor, 1024,
                                [[XGW, 128], [128 * XGW, 16], [1, 8]]),
                    in_=A3(gm_all.offset, [[8, 16], [1, 8]])))
                if "t_tau" in taps:
                    sSt.dma(sy.dma_start(out=taps["t_tau"][:, 0:EXP], in_=taub))
                    sSt.dma(sy.dma_start(out=taps["t_tau"][:, EXP:2 * EXP],
                                         in_=taulb))
                if "t_idx" in taps:
                    sy.wait_ge(sVg.h, p7_done_vg)
                    sSt.dma(sy.dma_start(out=taps["t_idx"], in_=idxsel))


        p9_mm = sMM.n
        p9_v = sV.n
        w0base = sW0.n
        p9_evac = []
        if True:
            @blk.vector
            def _(v):
                S_PER = 32 if "t_s" in taps else 16
                ssb = [sS0.n, sS1.n]
                for tt in range(16):
                    sl = tt % 2
                    for half in range(2):
                        v.wait_ge(sMM.h, p9_mm + tt * 2 + half + 1)
                        if tt >= 2:
                            v.wait_ge(SS[sl].h, ssb[sl] + S_PER * (tt // 2))
                        p9_evac.append(sV.inc(v.tensor_copy(
                            stile[sl][:, half * 512:(half + 1) * 512],
                            PBk[4 + half][:, 0:512])))

            @blk.tensor
            def _(te):
                te.wait_ge(sV.h, p8_done_v)
                te.wait_ge(sA.h, p8_done_a)
                te.wait_ge(sW0.h, w0base + 16 * 8)
                for tt in range(16):
                    for half in range(2):
                        if tt >= 1:
                            te.wait_ge(sV.h, p9_evac[(tt - 1) * 2 + half])
                        for kc in range(KC):
                            mi = te.matmul(PBk[4 + half][:, 0:512],
                                      attnT[kc][:, tt * 128:(tt + 1) * 128],
                                      woap(kc)[:, half * 512:half * 512 + 512],
                                      start=(kc == 0), stop=(kc == KC - 1))
                        sMM.inc(mi)

            @blk.sync
            def _(sy):
                sy.wait_ge(sSt.h, sSt.n)
                for kc in range(2):
                    sW0.dma(sy.dma_start(
                        out=A3(woSM.offset + kc * D, [[1, D]]),
                        in_=Wo_d[kc * 128:(kc + 1) * 128, :]))
                sy.wait_ge(sMM.h, p8_mm + (KC - 1) * PE_PER_C + 27)
                for kc in range(2, KC):
                    sW0.dma(sy.dma_start(
                        out=A3(woWB.offset + (kc - 2) * D, [[1, D]]),
                        in_=Wo_d[kc * 128:(kc + 1) * 128, :]))
                for tt in range(16):
                    sl = tt % 2
                    sy.wait_ge(sV.h, p9_v + tt * 2 + 2)
                    SS[sl].dma(sy.dma_start(
                        out=s_stage[tt * 128:(tt + 1) * 128, :], in_=stile[sl]))
                    if "t_s" in taps:
                        SS[sl].dma(sy.dma_start(
                            out=taps["t_s"][tt * 128:(tt + 1) * 128, :], in_=stile[sl]))
        p9_done_v = sV.n
        p9_done_mm = sMM.n
        s_dma_done = (sS0.n, sS1.n)

        p10_mm = sMM.n
        p10_v = sV.n
        p10_a = sA.n
        p10_gd = sGd.n
        p10_gd1 = sGd1.n
        GD = [sGd, sGd1]
        gdb = [p10_gd, p10_gd1]

        def gdone(e):
            return GD[e % 2].h, gdb[e % 2] + 16 * (e // 2 + 1)
        p10_gs = sGs.n
        inbase = sIn.n
        w1base = [sW0.n, sW1.n]
        w2base = [sV0.n, sV1.n]
        PE_E = 48
        V_E = 38
        A_E = 16

        def mmb(e):
            return p10_mm + e * PE_E

        def vbb(e):
            return p10_v + e * V_E

        def abb(e):
            return p10_a + e * A_E

        if True:
            @blk.vector
            def _(v):
                for e in range(EXP):
                    v.wait_ge(sMM.h, mmb(e) + 1)
                    sV.inc(v.tensor_copy(b1pp, PBk[0][:, 0:32]))
                    v.wait_ge(sMM.h, mmb(e) + 2)
                    sV.inc(v.tensor_copy(b2b[:, 0:512], PBk[2][:, 0:512]))
                    sV.inc(v.tensor_copy(b2b[:, 512:1024], PBk[3][:, 0:512]))
                    for qd in range(6):
                        v.wait_ge(sMM.h, mmb(e) + 3 + qd)
                        blk_i, q = qd // 2, qd % 2
                        sV.inc(v.tensor_copy(
                            A3(toksT.offset + (4 * q) * CAP + blk_i * 128,
                               [[CAP, 4], [1, 128]]),
                            bass.AP(PBk[qd % 2].tensor, 0, [[512, 128], [128, 4], [1, 128]])))
                    for g in range(8):
                        bk = (4 if g % 2 == 0 else 0)
                        v.wait_ge(sMM.h, mmb(e) + 9 + g)
                        for m4 in (0, 2):
                            m = g * 4 + m4
                            sV.inc(v.tensor_scalar(
                                h1t[:, m * CAP:m * CAP + CAPF], PBk[bk + m4][:, 0:CAPF],
                                b1pp[:, m:m + 1], 0.0, op0=Alu.add, op1=Alu.max))
                    v.wait_ge(sMM.h, mmb(e) + PE_E)
                    v.wait_ge(*gdone(e))
                    if e >= 1:
                        v.wait_ge(sGs.h, p10_gs + 16 * e)
                    sV.inc(v.memset(eob[64:128, 2 * 1024:3 * 1024], 0.0))
                    v.wait_ge(sV.h, sV.n)
                    for blk_i in range(3):
                        tw = 128 if blk_i < 2 else CAPF - 256
                        for half in range(2):
                            bk = 2 + blk_i * 2 + half
                            dst = eob[0:tw, blk_i * 1024 + half * 512:
                                      blk_i * 1024 + half * 512 + 512]
                            sV.inc(v.tensor_tensor(dst, PBk[bk][0:tw, 0:512],
                                                   b2b[0:tw, half * 512:(half + 1) * 512],
                                                   Alu.add))
                    v.wait_ge(sV.h, sV.n)
                    for blk_i in range(3):
                        tw = 128 if blk_i < 2 else CAPF - 256
                        for half in range(2):
                            dst = eob[0:tw, blk_i * 1024 + half * 512:
                                      blk_i * 1024 + half * 512 + 512]
                            gmcol = tgbuf(e)[0:tw, blk_i * XGW + 1024 + e:
                                             blk_i * XGW + 1024 + e + 1]
                            sV.inc(v.tensor_scalar(dst, dst, gmcol, None, op0=Alu.mult))
                    v.wait_ge(sV.h, sV.n)

            @blk.scalar
            def _(sc):
                for e in range(EXP):
                    for g in range(8):
                        bk = (4 if g % 2 == 0 else 0)
                        sc.wait_ge(sMM.h, mmb(e) + 9 + g)
                        for m4 in (1, 3):
                            m = g * 4 + m4
                            sA.inc(sc.activation(
                                h1t[:, m * CAP:m * CAP + CAPF], PBk[bk + m4][:, 0:CAPF],
                                Act.Relu, bias=b1pp[:, m:m + 1]))

            @blk.gpsimd
            def _(gp):
                gp.load_library(libcfg.mlp)
                gp.wait_ge(sSt.h, sSt.n)
                gp.wait_ge(sSt2.h, sSt2.n)
                gp.wait_ge(sGd.h, p10_gd)
                for k in range(EXP + 2):
                    if k >= 2:
                        ep = k - 2
                        gp.wait_ge(sV.h, vbb(ep) + V_E)
                        sGs.dma(gp.dma_scatter_add(
                            g_stage, eob.rearrange("p (b w) -> p b w", w=D),
                            idx16[:, ep * CAPW:(ep + 1) * CAPW],
                            num_idxs=CAP, num_idxs_reg=CAP, elem_size=D))
                    if k < EXP:
                        if k == 0:
                            gp.wait_ge(sMM.h, p8_mm + (KC - 1) * PE_PER_C + 27)
                        elif k == 1:
                            gp.wait_ge(sMM.h, p8_done_mm)
                        else:
                            if k == 2:
                                gp.wait_ge(sS0.h, s_dma_done[0])
                                gp.wait_ge(sS1.h, s_dma_done[1])
                            gp.wait_ge(sMM.h, mmb(k - 2) + 8)
                        GD[k % 2].dma(gp.dma_gather(
                            tgbuf(k).rearrange("p (b w) -> p b w", w=XGW),
                            xg_stage, idx16[:, k * CAPW:(k + 1) * CAPW],
                            num_idxs=CAP, num_idxs_reg=CAP, elem_size=XGW))

            @blk.tensor
            def _(te):
                te.wait_ge(sMM.h, p9_done_mm)
                te.wait_ge(sA.h, p8_done_a)
                for e in range(EXP):
                    te.wait_ge(sIn.h, inbase + 32 * e + 32)
                    if e >= 1:
                        te.wait_ge(sV.h, vbb(e - 1) + 25)
                        te.wait_ge(sA.h, abb(e - 1) + A_E)
                    mi = te.matmul(PBk[0][:, 0:32], b1tmp, identt[0:32, 0:32],
                              start=True, stop=True, is_transpose=True)
                    sMM.inc(mi)
                    if e >= 1:
                        te.wait_ge(sV.h, vbb(e - 1) + 28)
                    mi = te.matmul(PBk[2][:, 0:512], onesrt[0:1, :], b2row[0:1, 0:512],
                              start=True, stop=True)
                    mi = te.matmul(PBk[3][:, 0:512], onesrt[0:1, :], b2row[0:1, 512:1024],
                              start=True, stop=True)
                    sMM.inc(mi)
                    te.wait_ge(*gdone(e))
                    te.wait_ge(sV.h, vbb(e) + 1)
                    for qd in range(6):
                        blk_i, q = qd // 2, qd % 2
                        if qd >= 2:
                            te.wait_ge(sV.h, vbb(e) + 3 + (qd - 2) + 1)
                        for k4 in range(4):
                            kc = 4 * q + k4
                            mi = te.matmul(PBk[qd % 2][:, k4 * 128:(k4 + 1) * 128],
                                      tgbuf(e)[:, blk_i * XGW + kc * 128:
                                               blk_i * XGW + (kc + 1) * 128],
                                      identt, start=(k4 == 0), stop=(k4 == 3),
                                      is_transpose=True)
                        sMM.inc(mi)
                    for g in range(8):
                        sl = g % 2
                        bk = (4 if g % 2 == 0 else 0)
                        te.wait_ge(WS[sl].h, w1base[sl] + 16 * (g // 2 + 1 + (4 * e)))
                        if g >= 2:
                            te.wait_ge(sV.h, vbb(e) + 9 + 2 * (g - 2) + 2)
                            te.wait_ge(sA.h, abb(e) + 2 * (g - 2) + 2)
                        elif g == 0 and e >= 1:
                            te.wait_ge(sV.h, vbb(e - 1) + 32)
                        elif g == 0:
                            te.wait_ge(sV.h, p9_done_v)
                        if g < 2:
                            te.wait_ge(sV.h, vbb(e) + 9)
                        for m4 in range(4):
                            for kc in range(KC):
                                mi = te.matmul(PBk[bk + m4][:, 0:CAPF],
                                          w1r[sl][:, kc * 512 + m4 * 128:
                                                  kc * 512 + (m4 + 1) * 128],
                                          toksT[:, kc * CAP:kc * CAP + CAPF],
                                          start=(kc == 0), stop=(kc == KC - 1))
                        sMM.inc(mi)
                    te.wait_ge(sV.h, vbb(e) + 25)
                    te.wait_ge(sA.h, abb(e) + A_E)
                    for m in range(32):
                        sl = (m // 2) % 2
                        te.wait_ge(VS[sl].h, w2base[sl] + 16 * (m // 4 + 1 + 8 * e))
                        for blk_i in range(3):
                            tw = 128 if blk_i < 2 else CAPF - 256
                            for half in range(2):
                                mi = te.matmul(PBk[2 + blk_i * 2 + half][0:tw, 0:512],
                                          h1t[:, m * CAP + blk_i * 128:
                                              m * CAP + blk_i * 128 + tw],
                                          w2r[sl][:, (m % 2) * 1024 + half * 512:
                                                  (m % 2) * 1024 + (half + 1) * 512],
                                          start=(m == 0), stop=(m == 31))
                        sMM.inc(mi)

            @blk.sync
            def _(sy):
                sy.wait_ge(sMM.h, p8_done_mm)
                for e in range(EXP):
                    if e >= 1:
                        sy.wait_ge(sMM.h, mmb(e - 1) + 2)
                    sIn.dma(sy.dma_start(out=b1tmp,
                                         in_=b1_d[e:e + 1, :].rearrange(
                                             "o (m n) -> (o m) n", n=128)))
                    sIn.dma(sy.dma_start(out=b2row, in_=b2_d[e:e + 1, :]))
                    for g in range(8):
                        sl = g % 2
                        gu = e * 8 + g
                        if gu >= 2:
                            ep, gp_ = (gu - 2) // 8, (gu - 2) % 8
                            sy.wait_ge(sMM.h, mmb(ep) + 9 + gp_)
                        sW = WS[sl]
                        sW.dma(sy.dma_start(
                            out=A3(w1r[sl].offset, [[512, KC], [1, 512]]),
                            in_=bass.AP(w1_d.tensor, e * D * DFF + g * 512,
                                        [[DFF, 128], [128 * DFF, KC], [1, 512]])))
                    for mp in range(16):
                        sl = mp % 2
                        mu = e * 16 + mp
                        if mu >= 2:
                            ep, mp_ = (mu - 2) // 16, (mu - 2) % 16
                            te_i = 16 + (2 * mp_ + 1) + 1
                            sy.wait_ge(sMM.h, mmb(ep) + te_i)
                        sy2 = VS[sl]
                        sy2.dma(sy.dma_start(
                            out=A3(w2r[sl].offset, [[1024, 2], [1, 1024]]),
                            in_=bass.AP(w2_d.tensor, e * DFF * D + mp * 2 * 128 * D,
                                        [[D, 128], [128 * D, 2], [1, D]])))
        p10_done_v = sV.n
        p10_done_gd = sGd.n
        p10_done_gs = sGs.n

        p11_v = sV.n
        p11_a = sA.n
        FG3 = [sXa, sXb, sXc]
        FS3 = [sW0, sW1, sWc]
        FX3 = [sV0, sV1, sVc]
        g3base = [sXa.n, sXb.n, sXc.n]
        s3base = [sW0.n, sW1.n, sWc.n]
        x3base = [sV0.n, sV1.n, sVc.n]
        OUT3 = [sOut, sOutB, sOutC]
        outbase3 = [sOut.n, sOutB.n, sOutC.n]
        if True:
            @blk.vector
            def _(v):
                for tt in range(16):
                    sl = tt % 3
                    v.wait_ge(sA.h, p11_a + tt + 1)
                    v.wait_ge(FS3[sl].h, s3base[sl] + 16 * (tt // 3 + 1))
                    v.wait_ge(FX3[sl].h, x3base[sl] + 16 * (tt // 3 + 1))
                    sV.inc(v.tensor_tensor(fs[sl], fs[sl], fsc, Alu.mult))
                    v.wait_ge(sV.h, sV.n)
                    sV.inc(v.tensor_tensor(fx[sl], fx[sl], fs[sl], Alu.add))

            @blk.scalar
            def _(sc):
                sc.wait_ge(sS1.h, s_dma_done[1])
                for tt in range(16):
                    sl = tt % 3
                    sc.wait_ge(FG3[sl].h, g3base[sl] + 16 * (tt // 3 + 1))
                    if tt >= 1:
                        sc.wait_ge(sV.h, p11_v + 2 * (tt - 1) + 1)
                    sA.inc(sc.activation(fsc, fg[sl], Act.Silu))

            @blk.sync
            def _(sy):
                sy.wait_ge(sGd.h, p10_done_gd)
                sy.wait_ge(sGd1.h, sGd1.n)
                sy.wait_ge(sGs.h, p10_done_gs)
                sy.wait_ge(sS0.h, s_dma_done[0])
                sy.wait_ge(sS1.h, s_dma_done[1])
                for tt in range(18):
                    if tt >= 2:
                        tp = tt - 2
                        sy.wait_ge(sV.h, p11_v + 2 * tp + 2)
                        OUT3[tp % 3].dma(sy.dma_start(
                            out=y_out[tp * 128:(tp + 1) * 128, :], in_=fx[tp % 3]))
                    if tt < 16:
                        sl = tt % 3
                        if tt >= 3:
                            sy.wait_ge(sA.h, p11_a + (tt - 3) + 1)
                            sy.wait_ge(sV.h, p11_v + 2 * (tt - 3) + 2)
                            sy.wait_ge(OUT3[(tt - 3) % 3].h,
                                       outbase3[(tt - 3) % 3]
                                       + 16 * ((tt - 3) // 3 + 1))
                        FG3[sl].dma(sy.dma_start(
                            out=fg[sl], in_=g_stage[tt * 128:(tt + 1) * 128, :]))
                        FS3[sl].dma(sy.dma_start(
                            out=fs[sl], in_=s_stage[tt * 128:(tt + 1) * 128, :]))
                        FX3[sl].dma(sy.dma_start(
                            out=fx[sl], in_=x_att[8 + tt * 128: 8 + (tt + 1) * 128, :]))
                sy.wait_ge(sOut.h, sOut.n)
                sy.wait_ge(sOutB.h, sOutB.n)
                sy.wait_ge(sOutC.h, sOutC.n)

    lower_extended_insts(nc)
    return nc


def _consts():
    c = {}
    c["ident"] = np.eye(128, dtype=np.float32)
    band = np.zeros((128, 128), dtype=np.float32)
    for p_ in range(128):
        for cc in range(128):
            if not (0 <= cc - p_ <= 8):
                band[p_, cc] = -1e30
    c["maskA"] = band
    m0 = band.copy()
    m0[:, 0:8] = -1e30
    c["mask0"] = m0
    io = np.zeros((128, 16), dtype=np.float32)
    for p_ in range(128):
        for jj in range(16):
            io[p_, jj] = p_ + 128 * jj + 1
    c["iota16"] = io
    isl = np.zeros((16, CAPW), dtype=np.float32)
    for j in range(CAP):
        isl[j % 16, j // 16] = float(j)
    c["ioslot"] = isl
    bd = np.zeros((128, EXP), dtype=np.float32)
    for q in range(128):
        bd[q, q % EXP] = 1.0
    c["bd8"] = bd
    c["bdT"] = np.ascontiguousarray(bd.T)
    c["onesrow"] = np.ones((1, 128), dtype=np.float32)
    return c


def kernel(L0_attn_x, L0_attn_Wq, L0_attn_Wk, L0_attn_Wv, L0_attn_Wo,
           L0_ffn_router_w, L0_ffn_router_b, L0_ffn_w1, L0_ffn_b1,
           L0_ffn_w2, L0_ffn_b2, _debug_taps=()):
    key = tuple(sorted(_debug_taps))
    if key not in _CACHE:
        _CACHE[key] = build_program(key)
    nc = _CACHE[key]

    x = np.ascontiguousarray(np.asarray(L0_attn_x, dtype=np.float32)).reshape(16384, D)
    consts = _consts()
    shared = {
        "Wq": np.ascontiguousarray(L0_attn_Wq, dtype=np.float32),
        "Wk": np.ascontiguousarray(L0_attn_Wk, dtype=np.float32),
        "Wv": np.ascontiguousarray(L0_attn_Wv, dtype=np.float32),
        "Wo": np.ascontiguousarray(L0_attn_Wo, dtype=np.float32),
        "rw": np.ascontiguousarray(L0_ffn_router_w, dtype=np.float32),
        "rbias": np.ascontiguousarray(L0_ffn_router_b, dtype=np.float32).reshape(1, EXP),
        "w1": np.ascontiguousarray(L0_ffn_w1, dtype=np.float32),
        "b1": np.ascontiguousarray(L0_ffn_b1, dtype=np.float32),
        "w2": np.ascontiguousarray(L0_ffn_w2, dtype=np.float32),
        "b2": np.ascontiguousarray(L0_ffn_b2, dtype=np.float32),
        "ident": consts["ident"], "maskA": consts["maskA"],
        "iota16": consts["iota16"], "ioslot": consts["ioslot"],
        "bd8": consts["bd8"], "bdT": consts["bdT"], "onesrow": consts["onesrow"],
    }
    in_maps = []
    for c in range(8):
        start = c * TOK
        xatt = np.zeros((TA, D), dtype=np.float32)
        s0 = max(0, start - 8)
        nh = start - s0
        xatt[8 - nh:8] = x[s0:start]
        xatt[8:8 + TOK] = x[start:start + TOK]
        m = dict(shared)
        m["x_att"] = xatt
        xr = np.roll(x, -start, axis=0)
        m["x_full_T"] = np.ascontiguousarray(xr.T)
        m["mask0"] = consts["mask0"] if c in (0, 4) else consts["maskA"]
        in_maps.append(m)
    res = run_bass_kernel_spmd(nc, in_maps, core_ids=list(range(8)))
    outs = [res.results[c]["y"] for c in range(8)]
    y = np.concatenate(outs, axis=0).reshape(2, 8192, D).astype(np.float32)
    if _debug_taps:
        kernel._tap_results = res.results
    return y


# version 20
# speedup vs baseline: 1.0545x; 1.0017x over previous
import contextlib
import numpy as np

import concourse.bass as bass
import concourse.mybir as mybir
import concourse.library_config as libcfg
from concourse.bass_utils import run_bass_kernel_spmd
from concourse.library_overlay import lower_extended_insts

F32 = mybir.dt.float32
I32 = mybir.dt.int32
I16 = mybir.dt.int16
U32 = mybir.dt.uint32
Alu = mybir.AluOpType
Act = mybir.ActivationFunctionType
AX = mybir.AxisListType.X

TOK = 2048
NB = 17
TA = NB * 128
D = 1024
KC = 8
EXP = 8
DFF = 4096
CAP = 384
CAPF = 336
CAPW = CAP // 16
NIT = 32
WINM = 136
INV2T = 1.0 / 0.56
EPS_LN = 1e-5
LOOSE = 1.0 - 1e-4
SENT = 2048
NTILE = 128
XGW = 1088

_CACHE = {}


def build_program(debug_taps=()):
    nc = bass.Bass("TRN2", target_bir_lowering=False)

    x_att = nc.dram_tensor("x_att", [TA, D], F32, kind="ExternalInput").ap()
    x_full_T = nc.dram_tensor("x_full_T", [D, 16384], F32, kind="ExternalInput").ap()
    Wq_d = nc.dram_tensor("Wq", [D, D], F32, kind="ExternalInput").ap()
    Wk_d = nc.dram_tensor("Wk", [D, D], F32, kind="ExternalInput").ap()
    Wv_d = nc.dram_tensor("Wv", [D, D], F32, kind="ExternalInput").ap()
    Wo_d = nc.dram_tensor("Wo", [D, D], F32, kind="ExternalInput").ap()
    rw_d = nc.dram_tensor("rw", [D, EXP], F32, kind="ExternalInput").ap()
    rb_d = nc.dram_tensor("rbias", [1, EXP], F32, kind="ExternalInput").ap()
    w1_d = nc.dram_tensor("w1", [EXP, D, DFF], F32, kind="ExternalInput").ap()
    b1_d = nc.dram_tensor("b1", [EXP, DFF], F32, kind="ExternalInput").ap()
    w2_d = nc.dram_tensor("w2", [EXP, DFF, D], F32, kind="ExternalInput").ap()
    b2_d = nc.dram_tensor("b2", [EXP, D], F32, kind="ExternalInput").ap()
    ident_d = nc.dram_tensor("ident", [128, 128], F32, kind="ExternalInput").ap()
    maskA_d = nc.dram_tensor("maskA", [128, 128], F32, kind="ExternalInput").ap()
    mask0_d = nc.dram_tensor("mask0", [128, 128], F32, kind="ExternalInput").ap()
    iota_d = nc.dram_tensor("iota16", [128, 16], F32, kind="ExternalInput").ap()
    ioslot_d = nc.dram_tensor("ioslot", [16, CAPW], F32, kind="ExternalInput").ap()
    bd8_d = nc.dram_tensor("bd8", [128, EXP], F32, kind="ExternalInput").ap()
    bdT_d = nc.dram_tensor("bdT", [8, 128], F32, kind="ExternalInput").ap()
    ones_d = nc.dram_tensor("onesrow", [1, 128], F32, kind="ExternalInput").ap()

    y_out = nc.dram_tensor("y", [TOK, D], F32, kind="ExternalOutput").ap()
    taps = {}
    tapdefs = {"t_xn": [TA, D], "t_tau": [128, 2 * EXP], "t_scores": [TOK, EXP],
               "t_s": [TOK, D], "t_g": [TOK, D], "t_idx": [16, EXP * CAPW],
               "t_srch": [128, 1024]}
    for name in debug_taps:
        taps[name] = nc.dram_tensor(name, tapdefs[name], F32, kind="ExternalOutput").ap()

    xg_stage = nc.dram_tensor("xg_stage", [SENT + 1, XGW], F32, kind="Internal").ap()
    g_stage = nc.dram_tensor("g_stage", [SENT + 1, D], F32, kind="Internal").ap()
    s_stage = nc.dram_tensor("s_stage", [TOK, D], F32, kind="Internal").ap()

    ctx = contextlib.ExitStack()
    with ctx:
        AR = 51300
        arena = ctx.enter_context(nc.sbuf_tensor("arena", [128, AR], F32)).ap()

        def A(off, n):
            assert off + n <= AR, (off, n)
            return arena[:, off:off + n]

        def A3(off, dims, pcnt=128):
            return bass.AP(arena.tensor, off, [[AR, pcnt]] + dims)

        XA, XT, QKV, WB, SM, PS = 0, 17408, 34816, 41344, 47488, 49536

        xa = A(XA, NB * D)
        xnT = A(XT, KC * TA)

        xT = [A(QKV, 2048), A(QKV + 2048, 2048)]
        srch = A(XA + 6 * TA, 1024)
        gescrD = A(XA + 6 * TA + 1024, 1024)
        lgt = A(XA + 6 * TA + 2048, 96)
        sgP = A(XA + 7 * TA, 1024)
        lgTst = A(XA + 7 * TA + 1024, 1024)
        gescr = A(XA + 7 * TA + 2048, 64)
        fixs = A(XA + 7 * TA + 2112, 64)

        qT = A(QKV, 2048)
        kT = A(QKV + 2048, TA)
        vC = A(QKV + 2048 + TA, 18 * 128)
        wqb = [A(WB, 1024), A(WB + 3072, 1024)]
        wkb = [A(WB + 1024, 1024), A(WB + 4096, 1024)]
        wvb = [A(WB + 2048, 1024), A(WB + 5120, 1024)]

        woSM = A(SM, 2 * D)
        woWB = A(WB, 6 * D)
        stile = [A(QKV, D), A(QKV + D, D)]

        def woap(kc):
            return woSM[:, kc * D:(kc + 1) * D] if kc < 2 else \
                woWB[:, (kc - 2) * D:(kc - 1) * D]

        h1t = A(XA, 32 * CAP)
        toksT = A(XA + 32 * CAP, KC * CAP)
        b1pp = A(XA + 32 * CAP + KC * CAP, 32)
        w1r = [A(XT, 4096), A(XT + 4096, 4096)]
        w2r = [A(XT + 8192, 2048), A(XT + 8192 + 2048, 2048)]
        tg = [A(QKV, 3 * XGW), A(QKV + 3 * XGW, 3 * XGW)]
        tg0a = A(XT + 12288, 3 * XGW)

        def tgbuf(e):
            return tg0a if e == 0 else tg[e % 2]
        eob = A(WB, 3 * 1024)
        b2b = A(WB + 3 * 1024, D)
        zero = A(SM, XGW)

        fg = [A(QKV, D), A(QKV + D, D), A(XT + 12288, D)]
        fx = [A(QKV + 2 * D, D), A(QKV + 3 * D, D), A(XT + 12288 + D, D)]
        fs = [A(QKV + 4 * D, D), A(QKV + 5 * D, D), A(XT + 12288 + 2 * D, D)]
        fy = A(SM, D)
        fsc = A(SM + D, D)

        _p = [PS]

        def P_(n):
            r = A(_p[0], n)
            _p[0] += n
            return r

        identt = P_(128)
        maskAt = P_(128)
        mask0t = P_(128)
        iota16t = P_(16)
        bd8t = P_(EXP)
        bdTt = P_(128)
        onesrt = P_(128)
        rw9 = P_(KC * 9 + 8)
        rwsumb = P_(EXP)
        rbb = P_(EXP)
        taub = P_(EXP)
        taulb = P_(EXP)
        gm_all = P_(128)
        ssum = P_(NB + 1)
        sqsum = P_(NB + 1)
        mnegv = P_(NB + 1)
        varv = P_(NB + 1)
        rstdv = P_(NB + 1)
        tA_ = P_(NB + 1)
        tB_ = P_(NB + 1)
        tC_ = P_(NB + 1)
        g_sq = P_(EXP)
        g_mneg = P_(EXP)
        g_var = P_(EXP)
        g_rstd = P_(EXP)
        g_nm = P_(EXP)
        g_rs = P_(EXP)
        g_ri = P_(EXP)
        g_sA = P_(EXP)
        g_sB = P_(EXP)
        g_sC = P_(EXP)
        lo8 = [P_(1), P_(1)]
        hi8 = [P_(1), P_(1)]
        mid8 = P_(1)
        midpp = P_(1)
        cntp = P_(1)
        cnt8s = P_(1)
        taur = P_(8)
        selvt = P_(16)
        nmx2 = [P_(1), P_(1)]
        rsum2 = [P_(1), P_(1)]
        rinv2 = [P_(1), P_(1)]
        wtsb2 = [P_(128), P_(128)]
        wtsTa2 = [P_(128), P_(128)]
        attnT = [A(XA + kc * TA, TA) for kc in range(KC)]
        assert _p[0] <= AR, _p[0]

        selv_sb = ctx.enter_context(nc.sbuf_tensor("selv_sb", [16, 128], F32)).ap()
        idxraw = ctx.enter_context(nc.sbuf_tensor("idxraw", [16, EXP * CAPW], F32)).ap()
        idxsel = ctx.enter_context(nc.sbuf_tensor("idxsel", [16, EXP * CAPW], F32)).ap()
        c2048 = ctx.enter_context(nc.sbuf_tensor("c2048", [16, CAPW], F32)).ap()
        ioslt = ctx.enter_context(nc.sbuf_tensor("ioslt", [16, CAPW], F32)).ap()
        vmask = ctx.enter_context(nc.sbuf_tensor("vmask", [16, CAPW], U32)).ap()
        pred8 = ctx.enter_context(nc.sbuf_tensor("pred8", [8, 2], U32)).ap()
        idx16 = ctx.enter_context(nc.sbuf_tensor("idx16", [128, EXP * CAPW], I16)).ap()
        nfu = ctx.enter_context(nc.sbuf_tensor("nfu", [16, 2], U32)).ap()
        nfb = ctx.enter_context(nc.sbuf_tensor("nfb", [16, 2], F32)).ap()
        rbrow = ctx.enter_context(nc.sbuf_tensor("rbrow", [1, EXP], F32)).ap()
        b2row = ctx.enter_context(nc.sbuf_tensor("b2row", [1, D], F32)).ap()
        b1tmp = ctx.enter_context(nc.sbuf_tensor("b1tmp", [32, 128], F32)).ap()

        PBk = [ctx.enter_context(nc.psum_tensor(f"ps{i}", [128, 512], F32)).ap()
               for i in range(8)]

        class SemCtr:
            def __init__(self, name):
                self.h = ctx.enter_context(nc.semaphore(name))
                self.n = 0

            def inc(self, instr, k=1):
                instr.then_inc(self.h, k)
                self.n += k
                return self.n

            def dma(self, instr):
                return self.inc(instr, 16)

        sIn = SemCtr("sIn")
        sXa = SemCtr("sXa")
        sXb = SemCtr("sXb")
        sXc = SemCtr("sXc")
        sWc = SemCtr("sWc")
        sVc = SemCtr("sVc")
        sOutB = SemCtr("sOutB")
        sOutC = SemCtr("sOutC")
        sW0 = SemCtr("sW0")
        sW1 = SemCtr("sW1")
        sS0 = SemCtr("sS0")
        sS1 = SemCtr("sS1")
        sV0 = SemCtr("sV0")
        sV1 = SemCtr("sV1")
        sSt = SemCtr("sSt")
        sSt2 = SemCtr("sSt2")
        sOut = SemCtr("sOut")
        sBm = SemCtr("sBm")
        sGd = SemCtr("sGd")
        sGd1 = SemCtr("sGd1")
        sGs = SemCtr("sGs")
        sMM = SemCtr("sMM")
        sV = SemCtr("sV")
        sVg = SemCtr("sVg")
        sA = SemCtr("sA")
        sAe = SemCtr("sAe")
        sGp = SemCtr("sGp")
        XS = [sXa, sXb]
        WS = [sW0, sW1]
        SS = [sS0, sS1]
        VS = [sV0, sV1]

        blk = ctx.enter_context(nc.Block())

        def vchain(v, instr, ctr):
            ctr.inc(instr)
            v.wait_ge(ctr.h, ctr.n)

        RSQ_OPS = 3 + 4 * 4 + 1

        def emit_rsqrt(v, ctr, x_ap, out_ap, ta, tb, tc):
            xi = x_ap.bitcast(I32)
            yi = ta.bitcast(I32)
            vchain(v, v.tensor_scalar(yi, xi, 1, None, op0=Alu.logical_shift_right), ctr)
            vchain(v, v.tensor_scalar(yi, yi, 0, None, op0=Alu.bitwise_not), ctr)
            vchain(v, v.tensor_scalar(yi, yi, 0x5f3759e0, None, op0=Alu.add), ctr)
            for _ in range(4):
                vchain(v, v.tensor_tensor(tb, ta, ta, Alu.mult), ctr)
                vchain(v, v.tensor_tensor(tc, tb, x_ap, Alu.mult), ctr)
                vchain(v, v.tensor_scalar(tc, tc, -0.5, 1.5, op0=Alu.mult, op1=Alu.add), ctr)
                vchain(v, v.tensor_tensor(ta, ta, tc, Alu.mult), ctr)
            vchain(v, v.tensor_copy(out_ap, ta), ctr)

        if True:
            @blk.vector
            def _(v):
                v.memset(zero, 0.0)
                v.memset(c2048, float(SENT))
                v.memset(idx16, 0)
                v.memset(idxraw, -1.0)
                v.wait_ge(sIn.h, 16 * 10)
                sV.inc(v.memset(A3(rw9.offset + EXP, [[9, KC], [1, 1]]), 1.0))

            @blk.sync
            def _(sy):
                sIn.dma(sy.dma_start(out=identt, in_=ident_d))
                sIn.dma(sy.dma_start(out=maskAt, in_=maskA_d))
                sIn.dma(sy.dma_start(out=mask0t, in_=mask0_d))
                sIn.dma(sy.dma_start(out=iota16t, in_=iota_d))
                sIn.dma(sy.dma_start(out=bd8t, in_=bd8_d))
                sIn.dma(sy.dma_start(out=bdTt[0:8, :], in_=bdT_d))
                sIn.dma(sy.dma_start(out=onesrt[0:1, :], in_=ones_d))
                sIn.dma(sy.dma_start(out=ioslt, in_=ioslot_d))
                sIn.dma(sy.dma_start(out=rbrow, in_=rb_d))
                sIn.dma(sy.dma_start(
                    out=A3(rw9.offset, [[9, KC], [1, EXP]]),
                    in_=rw_d.rearrange("(kc p) e -> p kc e", p=128)))
                XQ = [sXc, sWc, sVc, sOutB]
                for q in range(4):
                    nb_q = [4, 4, 4, 5][q]
                    ob_q = [0, 4, 8, 12][q]
                    XQ[q].dma(sy.dma_start(
                        out=A3(XA + ob_q * D, [[D, nb_q], [1, D]]),
                        in_=bass.AP(x_att.tensor, ob_q * 128 * D,
                                    [[D, 128], [128 * D, nb_q], [1, D]])))
                assert sIn.n == 16 * 10
        p0_v = sV.n

        ssb8 = [sS0.n, sS1.n]
        if True:
            @blk.sync
            def _(sy):
                for hx in range(2):
                    WS[hx % 2].dma(sy.dma_start(
                        out=A3(xT[hx % 2].offset, [[256, KC], [1, 256]]),
                        in_=bass.AP(x_full_T.tensor, hx * 256,
                                    [[16384, 128], [128 * 16384, KC], [1, 256]])))
                for c in range(2):
                    sl = c % 2
                    SS[sl].dma(sy.dma_start(
                        out=A3(wqb[sl].offset, [[128, KC], [1, 128]]),
                        in_=Wq_d[:, c * 128:(c + 1) * 128].rearrange(
                            "(kc p) n -> p kc n", p=128)))
                    SS[sl].dma(sy.dma_start(
                        out=A3(wkb[sl].offset, [[128, KC], [1, 128]]),
                        in_=Wk_d[:, c * 128:(c + 1) * 128].rearrange(
                            "(kc p) n -> p kc n", p=128)))
                    SS[sl].dma(sy.dma_start(
                        out=A3(wvb[sl].offset, [[128, KC], [1, 128]]),
                        in_=Wv_d[:, c * 128:(c + 1) * 128].rearrange(
                            "(kc p) n -> p kc n", p=128)))

        if True:
            @blk.vector
            def _(v):
                v.wait_ge(sV.h, p0_v)
                for c in range(NB):
                    q = min(3, c // 4)
                    v.wait_ge([sXc, sWc, sVc, sOutB][q].h, 16)
                    v.tensor_reduce(ssum[:, c:c + 1], xa[:, c * D:(c + 1) * D], AX, Alu.add)
                sV.inc(v.engine_nop())
                v.wait_ge(sV.h, sV.n)
                v.wait_ge(sA.h, sA.n + NB)
                S, Q = ssum[:, 0:NB], sqsum[:, 0:NB]
                M, VV, R = mnegv[:, 0:NB], varv[:, 0:NB], rstdv[:, 0:NB]
                Ta, Tb, Tc = tA_[:, 0:NB], tB_[:, 0:NB], tC_[:, 0:NB]
                vchain(v, v.tensor_scalar(M, S, -1.0 / D, None, op0=Alu.mult), sV)
                vchain(v, v.tensor_tensor(Ta, M, M, Alu.mult), sV)
                vchain(v, v.tensor_scalar(Tb, Q, 1.0 / D, None, op0=Alu.mult), sV)
                vchain(v, v.tensor_tensor(VV, Tb, Ta, Alu.subtract), sV)
                vchain(v, v.tensor_scalar(VV, VV, 0.0, float(EPS_LN),
                                          op0=Alu.max, op1=Alu.add), sV)
                emit_rsqrt(v, sV, VV, R, Ta, Tb, Tc)
                for c in range(NB):
                    sV.inc(v.tensor_scalar(
                        xa[:, c * D:(c + 1) * D], xa[:, c * D:(c + 1) * D],
                        mnegv[:, c:c + 1], rstdv[:, c:c + 1],
                        op0=Alu.add, op1=Alu.mult))

            @blk.scalar
            def _(sc):
                sc.wait_ge(sV.h, p0_v)
                for c in range(NB):
                    sc.wait_ge([sXc, sWc, sVc, sOutB][min(3, c // 4)].h, 16)
                    sc.wait_ge(sA.h, sA.n)
                    sA.inc(sc.activation(vC[:, 0:1024], xa[:, c * D:(c + 1) * D],
                                         Act.Square,
                                         accum_out=sqsum[:, c:c + 1]))
        ln_done = sV.n

        _p2xn = [0]
        _p2sta = [0]
        if True:
            @blk.sync
            def _(sy):
                sy.wait_ge(sV.h, ln_done)
                if "t_xn" in taps:
                    sSt2.dma(sy.dma_start(
                        out=taps["t_xn"].rearrange("(b p) d -> p b d", p=128),
                        in_=A3(XA, [[D, NB], [1, D]])))
                for b in list(range(11, 16)) + list(range(0, 11)):
                    sb_ = sSt2 if b >= 11 else sSt
                    sb_.dma(sy.dma_start(
                        out=xg_stage[b * 128: b * 128 + 120, 0:D],
                        in_=xa[8:128, b * D:(b + 1) * D]))
                    sb_.dma(sy.dma_start(
                        out=xg_stage[b * 128 + 120: b * 128 + 128, 0:D],
                        in_=xa[0:8, (b + 1) * D:(b + 2) * D]))
                    if b == 15:
                        _p2xn[0] = sSt2.n
                sSt.dma(sy.dma_start(out=xg_stage[SENT:SENT + 1, :], in_=zero[0:1, :]))
                for b in range(16):
                    sSt.dma(sy.dma_start(
                        out=xg_stage[b * 128:(b + 1) * 128, 1024 + EXP:XGW],
                        in_=zero[:, 0:XGW - 1024 - EXP]))
                for b in range(16):
                    sSt.dma(sy.dma_start(out=g_stage[b * 128:(b + 1) * 128, :],
                                         in_=zero[:, 0:D]))
                sSt.dma(sy.dma_start(out=g_stage[SENT:SENT + 1, :], in_=zero[0:1, 0:D]))
                _p2sta[0] = sSt.n


        p3_mm = sMM.n
        p3_v = sV.n
        if True:
            @blk.vector
            def _(v):
                for i in range(NB * 2):
                    c, q = i // 2, i % 2
                    v.wait_ge(sMM.h, p3_mm + i + 1)
                    sV.inc(v.tensor_copy(
                        A3(XT + (4 * q) * TA + c * 128, [[TA, 4], [1, 128]]),
                        bass.AP(PBk[q].tensor, 0, [[512, 128], [128, 4], [1, 128]])))

            @blk.tensor
            def _(te):
                te.wait_ge(sV.h, ln_done)
                for i in range(NB * 2):
                    c, q = i // 2, i % 2
                    if i >= 2:
                        te.wait_ge(sV.h, p3_v + (i - 2) + 1)
                    for k4 in range(4):
                        kc = 4 * q + k4
                        mi = te.matmul(PBk[q][:, k4 * 128:(k4 + 1) * 128],
                                  xa[:, c * D + kc * 128: c * D + (kc + 1) * 128],
                                  identt, start=(k4 == 0), stop=(k4 == 3),
                                  is_transpose=True)
                    sMM.inc(mi)
        p3_done_v = sV.n

        p4i_mm = sMM.n
        p4i_v = sV.n
        if True:
            @blk.vector
            def _(v):
                v.wait_ge(sMM.h, p4i_mm + 1)
                sV.inc(v.tensor_copy(taur[0:8, 0:1], PBk[2][0:8, 0:1]))
                v.wait_ge(sMM.h, p4i_mm + 2)
                sV.inc(v.tensor_copy(taur[0:1, 0:8], PBk[3][0:1, 0:8]))
                v.wait_ge(sMM.h, p4i_mm + 3)
                v.tensor_copy(rwsumb, PBk[4][:, 0:8])
                sV.inc(v.tensor_copy(rbb, PBk[5][:, 0:8]))

            @blk.tensor
            def _(te):
                te.wait_ge(sV.h, p0_v)
                for kc in range(KC):
                    mi = te.matmul(PBk[2][0:8, 0:1],
                              A3(rw9.offset + kc * 9, [[1, EXP]]),
                              A3(rw9.offset + kc * 9 + EXP, [[1, 1]]),
                              start=(kc == 0), stop=(kc == KC - 1))
                sMM.inc(mi)
                te.wait_ge(sV.h, p4i_v + 1)
                mi = te.matmul(PBk[3][0:1, 0:8], taur[0:8, 0:1], identt[0:8, 0:8],
                          start=True, stop=True, is_transpose=True)
                sMM.inc(mi)
                te.wait_ge(sV.h, p4i_v + 2)
                mi = te.matmul(PBk[4][:, 0:8], onesrt[0:1, :], taur[0:1, 0:8],
                          start=True, stop=True)
                mi = te.matmul(PBk[5][:, 0:8], onesrt[0:1, :], rbrow, start=True, stop=True)
                sMM.inc(mi)
        p4i_done = sV.n

        p4_mm = sMM.n
        p4_v = sV.n
        p4_ae = sAe.n
        p4_vg = sVg.n
        GPOPS = 5 + RSQ_OPS + 4 + 2 + 3
        grp_scores_done = []
        grp_exp_wait = []
        p4_rel_evac = []
        NGRP = 16

        gtmp = srch[:, 0:128]

        if True:
            @blk.vector
            def _(v):
                v.wait_ge(sSt2.h, _p2xn[0])
                for g in range(NGRP):
                    gb_mm = p4_mm + g * 20
                    for q4 in range(4):
                        v.wait_ge(sMM.h, gb_mm + 3 * q4 + 1)
                        sV.inc(v.tensor_copy(lgTst[0:9, q4 * 256:(q4 + 1) * 256],
                                             PBk[2 + q4 % 2][0:9, 0:256]))
                        for tid in range(2):
                            v.wait_ge(sMM.h, gb_mm + 3 * q4 + 2 + tid)
                            vchain(v, v.tensor_tensor(
                                gtmp, PBk[4 + tid][:, 0:128], identt, Alu.mult), sV)
                            vchain(v, v.tensor_reduce(
                                g_sq[:, 2 * q4 + tid:2 * q4 + tid + 1],
                                gtmp, AX, Alu.add), sV)
                    for jj in range(8):
                        v.wait_ge(sMM.h, gb_mm + 12 + jj + 1)
                        sV.inc(v.tensor_copy(lgt[:, jj * 9:(jj + 1) * 9],
                                             PBk[jj % 2][:, 0:9]))
                    h, gp = g // 2, g % 2
                    L8 = A3(lgt.offset, [[9, 8], [1, 8]])
                    SX8 = A3(lgt.offset + 8, [[9, 8]])
                    FS_ = A3(fixs.offset, [[8, 8], [1, 8]])
                    EXG = A3(gescr.offset, [[8, 8], [1, 8]])
                    rw_bc = A3(rwsumb.offset, [[0, 8], [1, 8]])
                    rb_bc = A3(rbb.offset, [[0, 8], [1, 8]])
                    mn_bc = A3(g_mneg.offset, [[1, 8], [0, 8]])
                    rstd_bc = A3(g_rstd.offset, [[1, 8], [0, 8]])
                    nm_bc = A3(g_nm.offset, [[1, 8], [0, 8]])
                    ri_bc = A3(g_ri.offset, [[1, 8], [0, 8]])
                    v.wait_ge(sV.h, sV.n)
                    base_vg = sVg.n
                    vchain(v, v.tensor_scalar(g_mneg, SX8, -1.0 / D, None,
                                              op0=Alu.mult), sVg)
                    vchain(v, v.tensor_tensor(g_sA, g_mneg, g_mneg, Alu.mult), sVg)
                    vchain(v, v.tensor_scalar(g_sB, g_sq, 1.0 / D, None,
                                              op0=Alu.mult), sVg)
                    vchain(v, v.tensor_tensor(g_var, g_sB, g_sA, Alu.subtract), sVg)
                    vchain(v, v.tensor_scalar(g_var, g_var, 0.0, float(EPS_LN),
                                              op0=Alu.max, op1=Alu.add), sVg)
                    emit_rsqrt(v, sVg, g_var, g_rstd, g_sA, g_sB, g_sC)
                    vchain(v, v.tensor_tensor(FS_, rw_bc, mn_bc, Alu.mult), sVg)
                    vchain(v, v.tensor_tensor(FS_, FS_, L8, Alu.add), sVg)
                    vchain(v, v.tensor_tensor(FS_, FS_, rstd_bc, Alu.mult), sVg)
                    vchain(v, v.tensor_tensor(FS_, FS_, rb_bc, Alu.add), sVg)
                    vchain(v, v.tensor_reduce(g_nm, FS_, AX, Alu.max, negate=True), sVg)
                    vchain(v, v.tensor_tensor(FS_, FS_, nm_bc, Alu.add), sVg)
                    grp_exp_wait.append(sVg.n)
                    v.wait_ge(sAe.h, p4_ae + g + 1)
                    vchain(v, v.tensor_reduce(g_rs, EXG, AX, Alu.add), sVg)
                    vchain(v, v.reciprocal(g_ri, g_rs), sVg)
                    vchain(v, v.tensor_tensor(
                        A3(sgP.offset + h * 128 + gp * 64, [[8, 8], [1, 8]]),
                        EXG, ri_bc, Alu.mult), sVg)
                    grp_scores_done.append(sVg.n)
                    assert sVg.n - base_vg == GPOPS
                for h in range(8):
                    v.wait_ge(sMM.h, p4_mm + NGRP * 20 + h + 1)
                    p4_rel_evac.append(sV.inc(v.tensor_copy(
                        srch[:, h * 128:(h + 1) * 128], PBk[h % 2][:, 0:128])))

            @blk.scalar
            def _(sc):
                sc.wait_ge(sSt2.h, _p2xn[0])
                for g in range(NGRP):
                    sc.wait_ge(sVg.h, grp_exp_wait[g])
                    sAe.inc(sc.activation(A3(gescr.offset, [[8, 8], [1, 8]]),
                                          A3(fixs.offset, [[8, 8], [1, 8]]),
                                          Act.Exp))

            @blk.tensor
            def _(te):
                te.wait_ge(sV.h, p3_done_v)
                te.wait_ge(sV.h, p4i_done)
                for g in range(NGRP):
                    gb_v = p4_v + g * 28
                    for q4 in range(4):
                        hx = g * 4 + q4
                        sl = hx % 2
                        te.wait_ge(WS[sl].h, 16 * (hx // 2 + 1))
                        if hx >= 2:
                            pg, pq = (hx - 2) // 4, (hx - 2) % 4
                            te.wait_ge(sV.h, p4_v + pg * 28 + 5 * pq + 1)
                        for kc in range(KC):
                            mi = te.matmul(
                                PBk[2 + q4 % 2][0:9, 0:256],
                                A3(rw9.offset + kc * 9, [[1, 9]]),
                                xT[sl][:, kc * 256:(kc + 1) * 256],
                                start=(kc == 0), stop=(kc == KC - 1))
                        sMM.inc(mi)
                        for tid in range(2):
                            if hx >= 1:
                                pg, pq = (hx - 1) // 4, (hx - 1) % 4
                                te.wait_ge(sV.h,
                                           p4_v + pg * 28 + 5 * pq + 2 + 2 * tid)
                            for kc in range(KC):
                                xtile = xT[sl][:, kc * 256 + tid * 128:
                                               kc * 256 + (tid + 1) * 128]
                                mi = te.matmul(PBk[4 + tid][:, 0:128],
                                               xtile, xtile,
                                               start=(kc == 0), stop=(kc == KC - 1))
                            sMM.inc(mi)
                    te.wait_ge(sV.h, gb_v + 16)
                    for jj in range(8):
                        if jj >= 2:
                            te.wait_ge(sV.h, gb_v + 20 + (jj - 2) + 1)
                        elif g >= 1:
                            te.wait_ge(sV.h, p4_v + (g - 1) * 28 + 20 + (jj + 6) + 1)
                        mi = te.matmul(PBk[jj % 2][:, 0:9],
                                       lgTst[0:9, jj * 128:(jj + 1) * 128],
                                       identt[0:9, 0:9],
                                       start=True, stop=True, is_transpose=True)
                        sMM.inc(mi)
                te.wait_ge(sV.h, p4_v + NGRP * 28)
                for h in range(8):
                    te.wait_ge(sVg.h, grp_scores_done[2 * h + 1])
                    if h >= 2:
                        te.wait_ge(sV.h, p4_rel_evac[h - 2])
                    mi = te.matmul(PBk[h % 2][:, 0:128], sgP[:, h * 128:(h + 1) * 128],
                                   identt, start=True, stop=True, is_transpose=True)
                    sMM.inc(mi)

            @blk.sync
            def _(sy):
                for hx in range(2, 64):
                    sl = hx % 2
                    if True:
                        pg, pq = (hx - 2) // 4, (hx - 2) % 4
                        sy.wait_ge(sMM.h, p4_mm + pg * 20 + 3 * pq + 3)
                    WS[sl].dma(sy.dma_start(
                        out=A3(xT[sl].offset, [[256, KC], [1, 256]]),
                        in_=bass.AP(x_full_T.tensor, hx * 256,
                                    [[16384, 128], [128 * 16384, KC], [1, 256]])))

        p4_done_v = sV.n
        p4_done_vg = sVg.n
        p4_done_mm = sMM.n

        p5_vg = sVg.n
        p5_mid_ready = []
        p5_midpp_evac = []
        p5_cnt_ready = []
        p5_cnt_evac = []
        p5_fin = []
        p7_selv_evac = []
        p7_nfready = []
        p7_nfb_evac = []
        p7_selvt_done = []
        p7_gp = sGp.n
        fe_pairs = []

        def _p5v_init(v):
            v.wait_ge(sV.h, p4_done_v)
            v.wait_ge(sVg.h, p4_done_vg)
            vchain(v, v.memset(lo8[0][0:8, :], 0.0), sVg)
            vchain(v, v.memset(hi8[0][0:8, :], 2.0), sVg)

        def _p5t_init(te):
            te.wait_ge(sV.h, p4_done_v)

        fe_pairs.append((_p5v_init, _p5t_init))

        def _p5v_iter(it):
            def f(v):
                pa, pb = it % 2, 1 - it % 2
                vchain(v, v.tensor_tensor(mid8[0:8, :], lo8[pa][0:8, :],
                                          hi8[pa][0:8, :], Alu.add), sVg)
                vchain(v, v.tensor_scalar(mid8[0:8, :], mid8[0:8, :], 0.5, None,
                                          op0=Alu.mult), sVg)
                p5_mid_ready.append(sVg.n)
                v.wait_ge(sBm.h, it * 2 + 1)
                vchain(v, v.tensor_copy(midpp, PBk[3][:, 0:1]), sVg)
                p5_midpp_evac.append(sVg.n)
                vchain(v, v.tensor_scalar(gescrD, srch, midpp, None,
                                          op0=Alu.is_ge, op1=Alu.add,
                                          accum_out=cntp), sVg)
                p5_cnt_ready.append(sVg.n)
                v.wait_ge(sBm.h, it * 2 + 2)
                vchain(v, v.tensor_copy(cnt8s[0:8, :], PBk[3][0:8, 0:1]), sVg)
                p5_cnt_evac.append(sVg.n)
                vchain(v, v.tensor_scalar(pred8[:, 0:1], cnt8s[0:8, :], 2048.0,
                                          None, op0=Alu.is_ge), sVg)
                sVg.inc(v.select(lo8[pb][0:8, :], pred8[:, 0:1], mid8[0:8, :],
                                 lo8[pa][0:8, :], add_drain=True))
                v.wait_ge(sVg.h, sVg.n)
                sVg.inc(v.select(hi8[pb][0:8, :], pred8[:, 0:1], hi8[pa][0:8, :],
                                 mid8[0:8, :], add_drain=True))
                v.wait_ge(sVg.h, sVg.n)
            return f

        def _p5t_iter(it):
            def f(te):
                te.wait_ge(sVg.h, p5_mid_ready[it])
                if it >= 1:
                    te.wait_ge(sVg.h, p5_midpp_evac[it - 1])
                    te.wait_ge(sVg.h, p5_cnt_evac[it - 1])
                sBm.inc(te.matmul(PBk[3][:, 0:1], bdTt[0:8, :], mid8[0:8, 0:1],
                                  start=True, stop=True))
                te.wait_ge(sVg.h, p5_cnt_ready[it])
                sBm.inc(te.matmul(PBk[3][0:8, 0:1], bd8t, cntp,
                                  start=True, stop=True))
            return f

        for _it in range(NIT):
            fe_pairs.append((_p5v_iter(_it), _p5t_iter(_it)))

        def _p5v_fin(v):
            p5_fin.append(sVg.n)
            v.wait_ge(sBm.h, NIT * 2 + 1)
            vchain(v, v.tensor_copy(taur[0:1, 0:8], PBk[3][0:1, 0:8]), sVg)
            p5_fin.append(sVg.n)
            v.wait_ge(sBm.h, NIT * 2 + 2)
            vchain(v, v.tensor_copy(taub, PBk[3][:, 0:8]), sVg)
            vchain(v, v.tensor_scalar(taulb, taub, float(LOOSE), None,
                                      op0=Alu.mult), sVg)
            for e in range(EXP):
                ssl = A3(sgP.offset + e, [[8, 16]])
                sVg.inc(v.scalar_tensor_tensor(
                    A3(gm_all.offset + e, [[8, 16]]),
                    ssl, taub[:, e:e + 1], ssl, op0=Alu.is_ge, op1=Alu.mult))
            v.wait_ge(sVg.h, sVg.n)
            sVg.inc(v.engine_nop())

        def _p5t_fin(te):
            te.wait_ge(sVg.h, p5_fin[0])
            sBm.inc(te.matmul(PBk[3][0:1, 0:8], lo8[NIT % 2][0:8, 0:1],
                              identt[0:8, 0:8],
                              start=True, stop=True, is_transpose=True))
            te.wait_ge(sVg.h, p5_fin[1])
            sBm.inc(te.matmul(PBk[3][:, 0:8], onesrt[0:1, :], taur[0:1, 0:8],
                              start=True, stop=True))

        fe_pairs.append((_p5v_fin, _p5t_fin))
        p6_vg_box = [0]

        def _p6v_mark(v):
            p6_vg_box[0] = sVg.n

        fe_pairs.append((_p6v_mark, None))

        def _p7v(e):
            def f(v):
                ssl = A3(sgP.offset + e, [[8, 16]])
                if e >= 1:
                    v.wait_ge(sBm.h, 2 * NIT + 2 + 2 * e)
                vchain(v, v.scalar_tensor_tensor(
                    selvt, ssl, taulb[:, e:e + 1], iota16t,
                    op0=Alu.is_ge, op1=Alu.mult), sVg)
                vchain(v, v.tensor_scalar(selvt, selvt, -1.0, None,
                                          op0=Alu.add), sVg)
                p7_selvt_done.append(sVg.n)
                v.wait_ge(sBm.h, 2 * NIT + 2 + 2 * e + 1)
                vchain(v, v.tensor_copy(selv_sb, PBk[3][0:16, 0:128]), sVg)
                p7_selv_evac.append(sVg.n)
                v.wait_ge(sGp.h, p7_gp + e + 1)
                vchain(v, v.tensor_copy(nfb[0:1, 0:1], nfu[0:1, 0:1]), sVg)
                p7_nfready.append(sVg.n)
                v.wait_ge(sBm.h, 2 * NIT + 2 + 2 * e + 2)
                vchain(v, v.tensor_copy(nfb[0:16, 1:2], PBk[3][0:16, 0:1]), sVg)
                p7_nfb_evac.append(sVg.n)
                vchain(v, v.tensor_scalar(vmask, ioslt, nfb[0:16, 1:2], None,
                                          op0=Alu.is_lt), sVg)
                sVg.inc(v.select(idxsel[:, e * CAPW:(e + 1) * CAPW], vmask,
                                 idxraw[:, e * CAPW:(e + 1) * CAPW], c2048,
                                 add_drain=True))
                v.wait_ge(sVg.h, sVg.n)
                vchain(v, v.tensor_copy(idx16[0:16, e * CAPW:(e + 1) * CAPW],
                                        idxsel[:, e * CAPW:(e + 1) * CAPW]), sVg)
            return f

        def _p7t(e):
            def f(te):
                te.wait_ge(sVg.h, p7_selvt_done[e])
                if e >= 1:
                    te.wait_ge(sVg.h, p7_selv_evac[e - 1])
                sBm.inc(te.matmul(PBk[3][0:16, 0:128], selvt, identt,
                                  start=True, stop=True, is_transpose=True))
                te.wait_ge(sVg.h, p7_nfready[e])
                if e >= 1:
                    te.wait_ge(sVg.h, p7_nfb_evac[e - 1])
                sBm.inc(te.matmul(PBk[3][0:16, 0:1], onesrt[0:1, 0:16],
                                  nfb[0:1, 0:1], start=True, stop=True))
            return f

        for _e in range(EXP):
            fe_pairs.append((_p7v(_e), _p7t(_e)))
        p7dv_box = [0]

        def _p7v_mark(v):
            p7dv_box[0] = sVg.n

        fe_pairs.append((_p7v_mark, None))


        fe_vi = [0, 0]
        fe_ti = [0, 0]

        def weave_v(v):
            fe_vi[0] += 1
            if fe_vi[0] % 5 == 0 and fe_vi[1] < len(fe_pairs):
                fn = fe_pairs[fe_vi[1]][0]
                fe_vi[1] += 1
                if fn is not None:
                    fn(v)

        def weave_t(te):
            fe_ti[0] += 1
            if fe_ti[0] % 5 == 0 and fe_ti[1] < len(fe_pairs):
                fn = fe_pairs[fe_ti[1]][1]
                fe_ti[1] += 1
                if fn is not None:
                    fn(te)

        p8_mm = sMM.n
        p8_v = sV.n
        p8_a = sA.n
        NJT = 18
        NHT = 2 * NJT
        PE_PER_C = 27 + 3 * NHT
        V_PER_C = 27 + 5 * NHT
        A_PER_C = 2 * NHT
        QP, KP = 4, 5

        def mmv(k):
            return 27 + 3 * k

        p8_ev = {}
        p8_sc = {}
        p8_nm = {}
        p8_qkv = {}
        pe_scores = {}
        pe_T = {}
        pe_attn = {}
        _n = 27
        for _k in range(NHT + 2):
            if _k < NHT:
                _n += 1; pe_scores[_k] = _n
            if 1 <= _k <= NHT:
                _n += 1; pe_T[_k - 1] = _n
            if _k >= 2:
                _n += 1; pe_attn[_k - 2] = _n
        assert _n == PE_PER_C, _n

        if True:
            @blk.vector
            def _(v):
                v.wait_ge(sSt.h, _p2sta[0])
                for c in range(KC):
                    vb = p8_v + c * V_PER_C
                    mb = p8_mm + c * PE_PER_C
                    ab = p8_a + c * A_PER_C
                    for i in range(QP):
                        v.wait_ge(sMM.h, mb + i + 1)
                        sV.inc(v.tensor_scalar(qT[:, i * 512:(i + 1) * 512],
                                               PBk[i % 2][:, 0:512],
                                               float(INV2T), None, op0=Alu.mult))
                        weave_v(v)
                    for i in range(KP):
                        npz = 512 if i < 4 else 128
                        v.wait_ge(sMM.h, mb + QP + i + 1)
                        sV.inc(v.tensor_copy(kT[:, i * 512:i * 512 + npz],
                                             PBk[i % 2][:, 0:npz]))
                        weave_v(v)
                    for b in range(NJT):
                        v.wait_ge(sMM.h, mb + QP + KP + b + 1)
                        sV.inc(v.tensor_copy(vC[:, b * 128:(b + 1) * 128],
                                             PBk[b % 2][:, 0:128]))
                        weave_v(v)
                    p8_qkv[c] = sV.n
                    def qwof(jt):
                        return 120 if jt < NJT - 1 else TOK - 120 * (NJT - 1)
                    for k in range(NHT + 2):
                        if k < NHT:
                            h2, jt = k // NJT, k % NJT
                            qw = qwof(jt)
                            sb = [4, 5, 6][k % 3]
                            mt = mask0t if jt == 0 else maskAt
                            v.wait_ge(sMM.h, mb + pe_scores[k])
                            sV.inc(v.tensor_tensor(PBk[sb][0:qw, 0:128],
                                                   PBk[sb][0:qw, 0:128],
                                                   mt[0:qw, 0:128], Alu.add))
                            v.wait_ge(sV.h, sV.n)
                            sV.inc(v.tensor_reduce(nmx2[k % 2][0:qw, :],
                                                   PBk[sb][0:qw, 0:128], AX,
                                                   Alu.max, negate=True))
                            p8_nm[(c, k)] = sV.n
                        if 1 <= k <= NHT:
                            kp = k - 1
                            qwp = qwof(kp % NJT)
                            v.wait_ge(sA.h, ab + 2 * kp + 1)
                            sV.inc(v.reciprocal(rinv2[kp % 2][0:qwp, :],
                                                rsum2[kp % 2][0:qwp, :]))
                            v.wait_ge(sV.h, sV.n)
                            sV.inc(v.tensor_scalar(wtsb2[kp % 2][0:qwp, :],
                                                   wtsb2[kp % 2][0:qwp, :],
                                                   rinv2[kp % 2][0:qwp, :], None,
                                                   op0=Alu.mult))
                            p8_sc[(c, kp)] = sV.n
                        if k >= 2:
                            kp = k - 2
                            h2p, jtp = kp // NJT, kp % NJT
                            qwp = qwof(jtp)
                            v.wait_ge(sMM.h, mb + pe_attn[kp])
                            sV.inc(v.tensor_copy(
                                attnT[c][64 * h2p:64 * h2p + 64,
                                         jtp * 120:jtp * 120 + qwp],
                                PBk[[7, 2][kp % 2]][64 * h2p:64 * h2p + 64, 0:qwp]))
                            p8_ev[(c, kp)] = sV.n
                        weave_v(v)

            @blk.scalar
            def _(sc):
                for c in range(KC):
                    mb = p8_mm + c * PE_PER_C
                    for k in range(NHT + 1):
                        if k < NHT:
                            jt = k % NJT
                            qw = 120 if jt < NJT - 1 else TOK - 120 * (NJT - 1)
                            sb = [4, 5, 6][k % 3]
                            sc.wait_ge(sV.h, p8_nm[(c, k)])
                            if k >= 2:
                                sc.wait_ge(sMM.h, mb + pe_T[k - 2])
                            elif c >= 1:
                                sc.wait_ge(sMM.h, p8_mm + (c - 1) * PE_PER_C
                                           + pe_T[NHT - 2 + k])
                            sA.inc(sc.activation(wtsb2[k % 2][0:qw, :],
                                                 PBk[sb][0:qw, 0:128],
                                                 Act.Exp, bias=nmx2[k % 2][0:qw, :],
                                                 scale=1.0,
                                                 accum_out=rsum2[k % 2][0:qw, :]))
                        if k >= 1:
                            kp = k - 1
                            qwp = 120 if kp % NJT < NJT - 1 else TOK - 120 * (NJT - 1)
                            sc.wait_ge(sMM.h, mb + pe_T[kp])
                            if kp >= 2:
                                sc.wait_ge(sMM.h, mb + pe_attn[kp - 2])
                            elif c >= 1:
                                sc.wait_ge(sMM.h, p8_mm + (c - 1) * PE_PER_C
                                           + pe_attn[NHT - 2 + kp])
                            sA.inc(sc.copy(wtsTa2[kp % 2][:, 0:qwp],
                                           PBk[kp % 2][:, 0:qwp]))

            @blk.tensor
            def _(te):
                te.wait_ge(sV.h, p4_done_v)
                for c in range(KC):
                    vb = p8_v + c * V_PER_C
                    ab = p8_a + c * A_PER_C
                    mb = p8_mm + c * PE_PER_C
                    sl = c % 2
                    te.wait_ge(SS[sl].h, ssb8[sl] + 48 * (c // 2 + 1))
                    if c == 0:
                        te.wait_ge(sV.h, p3_done_v)
                    for i in range(QP):
                        bk = i % 2
                        if i >= 2:
                            te.wait_ge(sV.h, vb + (i - 2) + 1)
                        elif c >= 1:
                            te.wait_ge(sA.h, p8_a + c * A_PER_C)
                            te.wait_ge(sV.h, p8_v + c * V_PER_C)
                        for kc in range(KC):
                            mi = te.matmul(PBk[bk][:, 0:512],
                                           wqb[sl][:, kc * 128:(kc + 1) * 128],
                                           xnT[:, kc * TA + 8 + i * 512:
                                               kc * TA + 8 + (i + 1) * 512],
                                           start=(kc == 0), stop=(kc == KC - 1))
                        sMM.inc(mi)
                        weave_t(te)
                    for i in range(KP):
                        npz = 512 if i < 4 else 128
                        bk = i % 2
                        te.wait_ge(sV.h, vb + QP + (i - 2) + 1 if i >= 2 else vb + QP)
                        for kc in range(KC):
                            mi = te.matmul(PBk[bk][:, 0:npz],
                                           wkb[sl][:, kc * 128:(kc + 1) * 128],
                                           xnT[:, kc * TA + i * 512: kc * TA + i * 512 + npz],
                                           start=(kc == 0), stop=(kc == KC - 1))
                        sMM.inc(mi)
                        weave_t(te)
                    for b in range(NJT):
                        bk = b % 2
                        te.wait_ge(sV.h, vb + QP + KP + (b - 2) + 1 if b >= 2
                                   else vb + QP + KP)
                        for kc in range(KC):
                            mi = te.matmul(PBk[bk][:, 0:128],
                                           xnT[:, kc * TA + b * 120: kc * TA + b * 120 + 128],
                                           wvb[sl][:, kc * 128:(kc + 1) * 128],
                                           start=(kc == 0), stop=(kc == KC - 1))
                        sMM.inc(mi)
                        weave_t(te)
                    for k in range(NHT + 2):
                        if k < NHT:
                            h2, jt = k // NJT, k % NJT
                            qw = 120 if jt < NJT - 1 else TOK - 120 * (NJT - 1)
                            sb = [4, 5, 6][k % 3]
                            te.wait_ge(sV.h, p8_qkv[c])
                            if k >= 3:
                                te.wait_ge(sA.h, ab + 2 * (k - 3) + 1)
                            elif c >= 1:
                                te.wait_ge(sA.h, p8_a + (c - 1) * A_PER_C
                                           + 2 * (NHT - 3 + k) + 1)
                            mi = te.matmul(PBk[sb][0:qw, 0:128],
                                           qT[64 * h2:64 * h2 + 64,
                                              jt * 120: jt * 120 + qw],
                                           kT[64 * h2:64 * h2 + 64,
                                              jt * 120: jt * 120 + 128],
                                           start=True, stop=True)
                            sMM.inc(mi)
                        if 1 <= k <= NHT:
                            kp = k - 1
                            qwp = 120 if kp % NJT < NJT - 1 else TOK - 120 * (NJT - 1)
                            te.wait_ge(sV.h, p8_sc[(c, kp)])
                            mi = te.matmul(PBk[kp % 2][:, 0:qwp],
                                           wtsb2[kp % 2][0:qwp, 0:128],
                                           identt[0:qwp, 0:qwp],
                                           start=True, stop=True, is_transpose=True)
                            sMM.inc(mi)
                        if k >= 2:
                            kp = k - 2
                            h2p, jtp = kp // NJT, kp % NJT
                            qwp = 120 if jtp < NJT - 1 else TOK - 120 * (NJT - 1)
                            te.wait_ge(sA.h, ab + 2 * kp + 2)
                            if kp >= 2:
                                te.wait_ge(sV.h, p8_ev[(c, kp - 2)])
                            elif c >= 1:
                                te.wait_ge(sV.h, p8_ev[(c - 1, NHT - 2 + kp)])
                            tp = {"tile_position": (0, 64)} if h2p else {}
                            mi = te.matmul(PBk[[7, 2][kp % 2]][64 * h2p:64 * h2p + 64,
                                                               0:qwp],
                                           vC[0:128, jtp * 128 + 64 * h2p:
                                              jtp * 128 + 64 * h2p + 64],
                                           wtsTa2[kp % 2][:, 0:qwp],
                                           start=True, stop=True, **tp)
                            sMM.inc(mi)
                        weave_t(te)

            @blk.sync
            def _(sy):
                for c in range(2, KC):
                    sl = c % 2
                    sy.wait_ge(sMM.h, p8_mm + (c - 1) * PE_PER_C + 27)
                    SS[sl].dma(sy.dma_start(
                        out=A3(wqb[sl].offset, [[128, KC], [1, 128]]),
                        in_=Wq_d[:, c * 128:(c + 1) * 128].rearrange(
                            "(kc p) n -> p kc n", p=128)))
                    SS[sl].dma(sy.dma_start(
                        out=A3(wkb[sl].offset, [[128, KC], [1, 128]]),
                        in_=Wk_d[:, c * 128:(c + 1) * 128].rearrange(
                            "(kc p) n -> p kc n", p=128)))
                    SS[sl].dma(sy.dma_start(
                        out=A3(wvb[sl].offset, [[128, KC], [1, 128]]),
                        in_=Wv_d[:, c * 128:(c + 1) * 128].rearrange(
                            "(kc p) n -> p kc n", p=128)))
        p8_done_v = sV.n
        p8_done_mm = sMM.n
        p8_done_a = sA.n
        p7_done_vg = p7dv_box[0]
        p6_vg = p6_vg_box[0]
        assert fe_vi[1] == len(fe_pairs) and fe_ti[1] == len(fe_pairs)

        if True:
            @blk.gpsimd
            def _(gp):
                gp.load_library(libcfg.sparse_gather)
                for e in range(EXP):
                    gp.wait_ge(sVg.h, p7_selv_evac[e])
                    sGp.inc(gp.sparse_gather(idxraw[:, e * CAPW:(e + 1) * CAPW],
                                             selv_sb, num_found=nfu[0:1, 0:1]))
                gp.wait_ge(sVg.h, p7_done_vg)
                sGd.dma(gp.dma_start(out=idx16[16:32, :], in_=idx16[0:16, :]))

            @blk.sync
            def _(sy):
                sy.wait_ge(sVg.h, p6_vg)
                sSt.dma(sy.dma_start(
                    out=bass.AP(xg_stage.tensor, 1024,
                                [[XGW, 128], [128 * XGW, 16], [1, 8]]),
                    in_=A3(gm_all.offset, [[8, 16], [1, 8]])))
                if "t_tau" in taps:
                    sSt.dma(sy.dma_start(out=taps["t_tau"][:, 0:EXP], in_=taub))
                    sSt.dma(sy.dma_start(out=taps["t_tau"][:, EXP:2 * EXP],
                                         in_=taulb))
                if "t_idx" in taps:
                    sy.wait_ge(sVg.h, p7_done_vg)
                    sSt.dma(sy.dma_start(out=taps["t_idx"], in_=idxsel))


        p9_mm = sMM.n
        p9_v = sV.n
        w0base = sW0.n
        p9_evac = []
        if True:
            @blk.vector
            def _(v):
                S_PER = 32 if "t_s" in taps else 16
                ssb = [sS0.n, sS1.n]
                for tt in range(16):
                    sl = tt % 2
                    for half in range(2):
                        v.wait_ge(sMM.h, p9_mm + tt * 2 + half + 1)
                        if tt >= 2:
                            v.wait_ge(SS[sl].h, ssb[sl] + S_PER * (tt // 2))
                        p9_evac.append(sV.inc(v.tensor_copy(
                            stile[sl][:, half * 512:(half + 1) * 512],
                            PBk[4 + half][:, 0:512])))

            @blk.tensor
            def _(te):
                te.wait_ge(sV.h, p8_done_v)
                te.wait_ge(sA.h, p8_done_a)
                te.wait_ge(sW0.h, w0base + 16 * 8)
                for tt in range(16):
                    for half in range(2):
                        if tt >= 1:
                            te.wait_ge(sV.h, p9_evac[(tt - 1) * 2 + half])
                        for kc in range(KC):
                            mi = te.matmul(PBk[4 + half][:, 0:512],
                                      attnT[kc][:, tt * 128:(tt + 1) * 128],
                                      woap(kc)[:, half * 512:half * 512 + 512],
                                      start=(kc == 0), stop=(kc == KC - 1))
                        sMM.inc(mi)

            @blk.sync
            def _(sy):
                sy.wait_ge(sSt.h, sSt.n)
                for kc in range(2):
                    sW0.dma(sy.dma_start(
                        out=A3(woSM.offset + kc * D, [[1, D]]),
                        in_=Wo_d[kc * 128:(kc + 1) * 128, :]))
                sy.wait_ge(sMM.h, p8_mm + (KC - 1) * PE_PER_C + 27)
                for kc in range(2, KC):
                    sW0.dma(sy.dma_start(
                        out=A3(woWB.offset + (kc - 2) * D, [[1, D]]),
                        in_=Wo_d[kc * 128:(kc + 1) * 128, :]))
                for tt in range(16):
                    sl = tt % 2
                    sy.wait_ge(sV.h, p9_v + tt * 2 + 2)
                    SS[sl].dma(sy.dma_start(
                        out=s_stage[tt * 128:(tt + 1) * 128, :], in_=stile[sl]))
                    if "t_s" in taps:
                        SS[sl].dma(sy.dma_start(
                            out=taps["t_s"][tt * 128:(tt + 1) * 128, :], in_=stile[sl]))
        p9_done_v = sV.n
        p9_done_mm = sMM.n
        s_dma_done = (sS0.n, sS1.n)

        p10_mm = sMM.n
        p10_v = sV.n
        p10_a = sA.n
        p10_gd = sGd.n
        p10_gd1 = sGd1.n
        GD = [sGd, sGd1]
        gdb = [p10_gd, p10_gd1]

        def gdone(e):
            return GD[e % 2].h, gdb[e % 2] + 16 * (e // 2 + 1)
        p10_gs = sGs.n
        inbase = sIn.n
        w1base = [sW0.n, sW1.n]
        w2base = [sV0.n, sV1.n]
        PE_E = 48
        V_E = 38
        A_E = 16

        def mmb(e):
            return p10_mm + e * PE_E

        def vbb(e):
            return p10_v + e * V_E

        def abb(e):
            return p10_a + e * A_E

        if True:
            @blk.vector
            def _(v):
                for e in range(EXP):
                    v.wait_ge(sMM.h, mmb(e) + 1)
                    sV.inc(v.tensor_copy(b1pp, PBk[0][:, 0:32]))
                    v.wait_ge(sMM.h, mmb(e) + 2)
                    sV.inc(v.tensor_copy(b2b[:, 0:512], PBk[2][:, 0:512]))
                    sV.inc(v.tensor_copy(b2b[:, 512:1024], PBk[3][:, 0:512]))
                    for qd in range(6):
                        v.wait_ge(sMM.h, mmb(e) + 3 + qd)
                        blk_i, q = qd // 2, qd % 2
                        sV.inc(v.tensor_copy(
                            A3(toksT.offset + (4 * q) * CAP + blk_i * 128,
                               [[CAP, 4], [1, 128]]),
                            bass.AP(PBk[qd % 2].tensor, 0, [[512, 128], [128, 4], [1, 128]])))
                    for g in range(8):
                        bk = (4 if g % 2 == 0 else 0)
                        v.wait_ge(sMM.h, mmb(e) + 9 + g)
                        for m4 in (0, 2):
                            m = g * 4 + m4
                            sV.inc(v.tensor_scalar(
                                h1t[:, m * CAP:m * CAP + CAPF], PBk[bk + m4][:, 0:CAPF],
                                b1pp[:, m:m + 1], 0.0, op0=Alu.add, op1=Alu.max))
                    v.wait_ge(sMM.h, mmb(e) + PE_E)
                    v.wait_ge(*gdone(e))
                    if e >= 1:
                        v.wait_ge(sGs.h, p10_gs + 16 * e)
                    sV.inc(v.memset(eob[64:128, 2 * 1024:3 * 1024], 0.0))
                    v.wait_ge(sV.h, sV.n)
                    for blk_i in range(3):
                        tw = 128 if blk_i < 2 else CAPF - 256
                        for half in range(2):
                            bk = 2 + blk_i * 2 + half
                            dst = eob[0:tw, blk_i * 1024 + half * 512:
                                      blk_i * 1024 + half * 512 + 512]
                            sV.inc(v.tensor_tensor(dst, PBk[bk][0:tw, 0:512],
                                                   b2b[0:tw, half * 512:(half + 1) * 512],
                                                   Alu.add))
                    v.wait_ge(sV.h, sV.n)
                    for blk_i in range(3):
                        tw = 128 if blk_i < 2 else CAPF - 256
                        for half in range(2):
                            dst = eob[0:tw, blk_i * 1024 + half * 512:
                                      blk_i * 1024 + half * 512 + 512]
                            gmcol = tgbuf(e)[0:tw, blk_i * XGW + 1024 + e:
                                             blk_i * XGW + 1024 + e + 1]
                            sV.inc(v.tensor_scalar(dst, dst, gmcol, None, op0=Alu.mult))
                    v.wait_ge(sV.h, sV.n)

            @blk.scalar
            def _(sc):
                for e in range(EXP):
                    for g in range(8):
                        bk = (4 if g % 2 == 0 else 0)
                        sc.wait_ge(sMM.h, mmb(e) + 9 + g)
                        for m4 in (1, 3):
                            m = g * 4 + m4
                            sA.inc(sc.activation(
                                h1t[:, m * CAP:m * CAP + CAPF], PBk[bk + m4][:, 0:CAPF],
                                Act.Relu, bias=b1pp[:, m:m + 1]))

            @blk.gpsimd
            def _(gp):
                gp.load_library(libcfg.mlp)
                gp.wait_ge(sSt.h, sSt.n)
                gp.wait_ge(sSt2.h, sSt2.n)
                gp.wait_ge(sGd.h, p10_gd)
                for k in range(EXP + 2):
                    if k >= 2:
                        ep = k - 2
                        gp.wait_ge(sV.h, vbb(ep) + V_E)
                        sGs.dma(gp.dma_scatter_add(
                            g_stage, eob.rearrange("p (b w) -> p b w", w=D),
                            idx16[:, ep * CAPW:(ep + 1) * CAPW],
                            num_idxs=CAP, num_idxs_reg=CAP, elem_size=D))
                    if k < EXP:
                        if k == 0:
                            gp.wait_ge(sMM.h, p8_mm + (KC - 1) * PE_PER_C + 27)
                        elif k == 1:
                            gp.wait_ge(sMM.h, p8_done_mm)
                        else:
                            if k == 2:
                                gp.wait_ge(sS0.h, s_dma_done[0])
                                gp.wait_ge(sS1.h, s_dma_done[1])
                            gp.wait_ge(sMM.h, mmb(k - 2) + 8)
                        GD[k % 2].dma(gp.dma_gather(
                            tgbuf(k).rearrange("p (b w) -> p b w", w=XGW),
                            xg_stage, idx16[:, k * CAPW:(k + 1) * CAPW],
                            num_idxs=CAP, num_idxs_reg=CAP, elem_size=XGW))

            @blk.tensor
            def _(te):
                te.wait_ge(sMM.h, p9_done_mm)
                te.wait_ge(sA.h, p8_done_a)
                for e in range(EXP):
                    te.wait_ge(sIn.h, inbase + 32 * e + 32)
                    if e >= 1:
                        te.wait_ge(sV.h, vbb(e - 1) + 25)
                        te.wait_ge(sA.h, abb(e - 1) + A_E)
                    mi = te.matmul(PBk[0][:, 0:32], b1tmp, identt[0:32, 0:32],
                              start=True, stop=True, is_transpose=True)
                    sMM.inc(mi)
                    if e >= 1:
                        te.wait_ge(sV.h, vbb(e - 1) + 28)
                    mi = te.matmul(PBk[2][:, 0:512], onesrt[0:1, :], b2row[0:1, 0:512],
                              start=True, stop=True)
                    mi = te.matmul(PBk[3][:, 0:512], onesrt[0:1, :], b2row[0:1, 512:1024],
                              start=True, stop=True)
                    sMM.inc(mi)
                    te.wait_ge(*gdone(e))
                    te.wait_ge(sV.h, vbb(e) + 1)
                    for qd in range(6):
                        blk_i, q = qd // 2, qd % 2
                        if qd >= 2:
                            te.wait_ge(sV.h, vbb(e) + 3 + (qd - 2) + 1)
                        for k4 in range(4):
                            kc = 4 * q + k4
                            mi = te.matmul(PBk[qd % 2][:, k4 * 128:(k4 + 1) * 128],
                                      tgbuf(e)[:, blk_i * XGW + kc * 128:
                                               blk_i * XGW + (kc + 1) * 128],
                                      identt, start=(k4 == 0), stop=(k4 == 3),
                                      is_transpose=True)
                        sMM.inc(mi)
                    for g in range(8):
                        sl = g % 2
                        bk = (4 if g % 2 == 0 else 0)
                        te.wait_ge(WS[sl].h, w1base[sl] + 16 * (g // 2 + 1 + (4 * e)))
                        if g >= 2:
                            te.wait_ge(sV.h, vbb(e) + 9 + 2 * (g - 2) + 2)
                            te.wait_ge(sA.h, abb(e) + 2 * (g - 2) + 2)
                        elif g == 0 and e >= 1:
                            te.wait_ge(sV.h, vbb(e - 1) + 32)
                        elif g == 0:
                            te.wait_ge(sV.h, p9_done_v)
                        if g < 2:
                            te.wait_ge(sV.h, vbb(e) + 9)
                        for m4 in range(4):
                            for kc in range(KC):
                                mi = te.matmul(PBk[bk + m4][:, 0:CAPF],
                                          w1r[sl][:, kc * 512 + m4 * 128:
                                                  kc * 512 + (m4 + 1) * 128],
                                          toksT[:, kc * CAP:kc * CAP + CAPF],
                                          start=(kc == 0), stop=(kc == KC - 1))
                        sMM.inc(mi)
                    te.wait_ge(sV.h, vbb(e) + 25)
                    te.wait_ge(sA.h, abb(e) + A_E)
                    for m in range(32):
                        sl = (m // 2) % 2
                        te.wait_ge(VS[sl].h, w2base[sl] + 16 * (m // 4 + 1 + 8 * e))
                        for blk_i in range(3):
                            tw = 128 if blk_i < 2 else CAPF - 256
                            for half in range(2):
                                mi = te.matmul(PBk[2 + blk_i * 2 + half][0:tw, 0:512],
                                          h1t[:, m * CAP + blk_i * 128:
                                              m * CAP + blk_i * 128 + tw],
                                          w2r[sl][:, (m % 2) * 1024 + half * 512:
                                                  (m % 2) * 1024 + (half + 1) * 512],
                                          start=(m == 0), stop=(m == 31))
                        sMM.inc(mi)

            @blk.sync
            def _(sy):
                sy.wait_ge(sMM.h, p8_done_mm)
                for e in range(EXP):
                    if e >= 1:
                        sy.wait_ge(sMM.h, mmb(e - 1) + 2)
                    sIn.dma(sy.dma_start(out=b1tmp,
                                         in_=b1_d[e:e + 1, :].rearrange(
                                             "o (m n) -> (o m) n", n=128)))
                    sIn.dma(sy.dma_start(out=b2row, in_=b2_d[e:e + 1, :]))
                    for g in range(8):
                        sl = g % 2
                        gu = e * 8 + g
                        if gu >= 2:
                            ep, gp_ = (gu - 2) // 8, (gu - 2) % 8
                            sy.wait_ge(sMM.h, mmb(ep) + 9 + gp_)
                        sW = WS[sl]
                        sW.dma(sy.dma_start(
                            out=A3(w1r[sl].offset, [[512, KC], [1, 512]]),
                            in_=bass.AP(w1_d.tensor, e * D * DFF + g * 512,
                                        [[DFF, 128], [128 * DFF, KC], [1, 512]])))
                    for mp in range(16):
                        sl = mp % 2
                        mu = e * 16 + mp
                        if mu >= 2:
                            ep, mp_ = (mu - 2) // 16, (mu - 2) % 16
                            te_i = 16 + (2 * mp_ + 1) + 1
                            sy.wait_ge(sMM.h, mmb(ep) + te_i)
                        sy2 = VS[sl]
                        sy2.dma(sy.dma_start(
                            out=A3(w2r[sl].offset, [[1024, 2], [1, 1024]]),
                            in_=bass.AP(w2_d.tensor, e * DFF * D + mp * 2 * 128 * D,
                                        [[D, 128], [128 * D, 2], [1, D]])))
        p10_done_v = sV.n
        p10_done_gd = sGd.n
        p10_done_gs = sGs.n

        p11_v = sV.n
        p11_a = sA.n
        FG3 = [sXa, sXb, sXc]
        FS3 = [sW0, sW1, sWc]
        FX3 = [sV0, sV1, sVc]
        g3base = [sXa.n, sXb.n, sXc.n]
        s3base = [sW0.n, sW1.n, sWc.n]
        x3base = [sV0.n, sV1.n, sVc.n]
        OUT3 = [sOut, sOutB, sOutC]
        outbase3 = [sOut.n, sOutB.n, sOutC.n]
        if True:
            @blk.vector
            def _(v):
                for tt in range(16):
                    sl = tt % 3
                    v.wait_ge(sA.h, p11_a + tt + 1)
                    v.wait_ge(FS3[sl].h, s3base[sl] + 16 * (tt // 3 + 1))
                    v.wait_ge(FX3[sl].h, x3base[sl] + 16 * (tt // 3 + 1))
                    sV.inc(v.tensor_tensor(fs[sl], fs[sl], fsc, Alu.mult))
                    v.wait_ge(sV.h, sV.n)
                    sV.inc(v.tensor_tensor(fx[sl], fx[sl], fs[sl], Alu.add))

            @blk.scalar
            def _(sc):
                sc.wait_ge(sS1.h, s_dma_done[1])
                for tt in range(16):
                    sl = tt % 3
                    sc.wait_ge(FG3[sl].h, g3base[sl] + 16 * (tt // 3 + 1))
                    if tt >= 1:
                        sc.wait_ge(sV.h, p11_v + 2 * (tt - 1) + 1)
                    sA.inc(sc.activation(fsc, fg[sl], Act.Silu))

            @blk.sync
            def _(sy):
                sy.wait_ge(sGd.h, p10_done_gd)
                sy.wait_ge(sGd1.h, sGd1.n)
                sy.wait_ge(sGs.h, p10_done_gs)
                sy.wait_ge(sS0.h, s_dma_done[0])
                sy.wait_ge(sS1.h, s_dma_done[1])
                for tt in range(18):
                    if tt >= 2:
                        tp = tt - 2
                        sy.wait_ge(sV.h, p11_v + 2 * tp + 2)
                        OUT3[tp % 3].dma(sy.dma_start(
                            out=y_out[tp * 128:(tp + 1) * 128, :], in_=fx[tp % 3]))
                    if tt < 16:
                        sl = tt % 3
                        if tt >= 3:
                            sy.wait_ge(sA.h, p11_a + (tt - 3) + 1)
                            sy.wait_ge(sV.h, p11_v + 2 * (tt - 3) + 2)
                            sy.wait_ge(OUT3[(tt - 3) % 3].h,
                                       outbase3[(tt - 3) % 3]
                                       + 16 * ((tt - 3) // 3 + 1))
                        FG3[sl].dma(sy.dma_start(
                            out=fg[sl], in_=g_stage[tt * 128:(tt + 1) * 128, :]))
                        FS3[sl].dma(sy.dma_start(
                            out=fs[sl], in_=s_stage[tt * 128:(tt + 1) * 128, :]))
                        FX3[sl].dma(sy.dma_start(
                            out=fx[sl], in_=x_att[8 + tt * 128: 8 + (tt + 1) * 128, :]))
                sy.wait_ge(sOut.h, sOut.n)
                sy.wait_ge(sOutB.h, sOutB.n)
                sy.wait_ge(sOutC.h, sOutC.n)

    lower_extended_insts(nc)
    return nc


def _consts():
    c = {}
    c["ident"] = np.eye(128, dtype=np.float32)
    band = np.zeros((128, 128), dtype=np.float32)
    for p_ in range(128):
        for cc in range(128):
            if not (0 <= cc - p_ <= 8):
                band[p_, cc] = -1e30
    c["maskA"] = band
    m0 = band.copy()
    m0[:, 0:8] = -1e30
    c["mask0"] = m0
    io = np.zeros((128, 16), dtype=np.float32)
    for p_ in range(128):
        for jj in range(16):
            io[p_, jj] = p_ + 128 * jj + 1
    c["iota16"] = io
    isl = np.zeros((16, CAPW), dtype=np.float32)
    for j in range(CAP):
        isl[j % 16, j // 16] = float(j)
    c["ioslot"] = isl
    bd = np.zeros((128, EXP), dtype=np.float32)
    for q in range(128):
        bd[q, q % EXP] = 1.0
    c["bd8"] = bd
    c["bdT"] = np.ascontiguousarray(bd.T)
    c["onesrow"] = np.ones((1, 128), dtype=np.float32)
    return c


def kernel(L0_attn_x, L0_attn_Wq, L0_attn_Wk, L0_attn_Wv, L0_attn_Wo,
           L0_ffn_router_w, L0_ffn_router_b, L0_ffn_w1, L0_ffn_b1,
           L0_ffn_w2, L0_ffn_b2, _debug_taps=()):
    key = tuple(sorted(_debug_taps))
    if key not in _CACHE:
        _CACHE[key] = build_program(key)
    nc = _CACHE[key]

    x = np.ascontiguousarray(np.asarray(L0_attn_x, dtype=np.float32)).reshape(16384, D)
    consts = _consts()
    shared = {
        "Wq": np.ascontiguousarray(L0_attn_Wq, dtype=np.float32),
        "Wk": np.ascontiguousarray(L0_attn_Wk, dtype=np.float32),
        "Wv": np.ascontiguousarray(L0_attn_Wv, dtype=np.float32),
        "Wo": np.ascontiguousarray(L0_attn_Wo, dtype=np.float32),
        "rw": np.ascontiguousarray(L0_ffn_router_w, dtype=np.float32),
        "rbias": np.ascontiguousarray(L0_ffn_router_b, dtype=np.float32).reshape(1, EXP),
        "w1": np.ascontiguousarray(L0_ffn_w1, dtype=np.float32),
        "b1": np.ascontiguousarray(L0_ffn_b1, dtype=np.float32),
        "w2": np.ascontiguousarray(L0_ffn_w2, dtype=np.float32),
        "b2": np.ascontiguousarray(L0_ffn_b2, dtype=np.float32),
        "ident": consts["ident"], "maskA": consts["maskA"],
        "iota16": consts["iota16"], "ioslot": consts["ioslot"],
        "bd8": consts["bd8"], "bdT": consts["bdT"], "onesrow": consts["onesrow"],
    }
    in_maps = []
    for c in range(8):
        start = c * TOK
        xatt = np.zeros((TA, D), dtype=np.float32)
        s0 = max(0, start - 8)
        nh = start - s0
        xatt[8 - nh:8] = x[s0:start]
        xatt[8:8 + TOK] = x[start:start + TOK]
        m = dict(shared)
        m["x_att"] = xatt
        xr = np.roll(x, -start, axis=0)
        m["x_full_T"] = np.ascontiguousarray(xr.T)
        m["mask0"] = consts["mask0"] if c in (0, 4) else consts["maskA"]
        in_maps.append(m)
    res = run_bass_kernel_spmd(nc, in_maps, core_ids=list(range(8)))
    outs = [res.results[c]["y"] for c in range(8)]
    y = np.concatenate(outs, axis=0).reshape(2, 8192, D).astype(np.float32)
    if _debug_taps:
        kernel._tap_results = res.results
    return y


# version 21 (final)
# speedup vs baseline: 1.0560x; 1.0014x over previous
import contextlib
import numpy as np

import concourse.bass as bass
import concourse.mybir as mybir
import concourse.library_config as libcfg
from concourse.bass_utils import run_bass_kernel_spmd
from concourse.library_overlay import lower_extended_insts

F32 = mybir.dt.float32
I32 = mybir.dt.int32
I16 = mybir.dt.int16
U32 = mybir.dt.uint32
Alu = mybir.AluOpType
Act = mybir.ActivationFunctionType
AX = mybir.AxisListType.X

TOK = 2048
NB = 17
TA = NB * 128
D = 1024
KC = 8
EXP = 8
DFF = 4096
CAP = 384
CAPF = 336
CAPW = CAP // 16
NIT = 32
WINM = 136
INV2T = 1.0 / 0.56
EPS_LN = 1e-5
LOOSE = 1.0 - 1e-4
SENT = 2048
NTILE = 128
XGW = 1088

_CACHE = {}


def build_program(debug_taps=()):
    nc = bass.Bass("TRN2", target_bir_lowering=False)

    x_att = nc.dram_tensor("x_att", [TA, D], F32, kind="ExternalInput").ap()
    x_full_T = nc.dram_tensor("x_full_T", [D, 16384], F32, kind="ExternalInput").ap()
    Wq_d = nc.dram_tensor("Wq", [D, D], F32, kind="ExternalInput").ap()
    Wk_d = nc.dram_tensor("Wk", [D, D], F32, kind="ExternalInput").ap()
    Wv_d = nc.dram_tensor("Wv", [D, D], F32, kind="ExternalInput").ap()
    Wo_d = nc.dram_tensor("Wo", [D, D], F32, kind="ExternalInput").ap()
    rw_d = nc.dram_tensor("rw", [D, EXP], F32, kind="ExternalInput").ap()
    rb_d = nc.dram_tensor("rbias", [1, EXP], F32, kind="ExternalInput").ap()
    w1_d = nc.dram_tensor("w1", [EXP, D, DFF], F32, kind="ExternalInput").ap()
    b1_d = nc.dram_tensor("b1", [EXP, DFF], F32, kind="ExternalInput").ap()
    w2_d = nc.dram_tensor("w2", [EXP, DFF, D], F32, kind="ExternalInput").ap()
    b2_d = nc.dram_tensor("b2", [EXP, D], F32, kind="ExternalInput").ap()
    ident_d = nc.dram_tensor("ident", [128, 128], F32, kind="ExternalInput").ap()
    maskA_d = nc.dram_tensor("maskA", [128, 128], F32, kind="ExternalInput").ap()
    mask0_d = nc.dram_tensor("mask0", [128, 128], F32, kind="ExternalInput").ap()
    iota_d = nc.dram_tensor("iota16", [128, 16], F32, kind="ExternalInput").ap()
    ioslot_d = nc.dram_tensor("ioslot", [16, CAPW], F32, kind="ExternalInput").ap()
    bd8_d = nc.dram_tensor("bd8", [128, EXP], F32, kind="ExternalInput").ap()
    bdT_d = nc.dram_tensor("bdT", [8, 128], F32, kind="ExternalInput").ap()
    ones_d = nc.dram_tensor("onesrow", [1, 128], F32, kind="ExternalInput").ap()

    y_out = nc.dram_tensor("y", [TOK, D], F32, kind="ExternalOutput").ap()
    taps = {}
    tapdefs = {"t_xn": [TA, D], "t_tau": [128, 2 * EXP], "t_scores": [TOK, EXP],
               "t_s": [TOK, D], "t_g": [TOK, D], "t_idx": [16, EXP * CAPW],
               "t_srch": [128, 1024]}
    for name in debug_taps:
        taps[name] = nc.dram_tensor(name, tapdefs[name], F32, kind="ExternalOutput").ap()

    xg_stage = nc.dram_tensor("xg_stage", [SENT + 1, XGW], F32, kind="Internal").ap()
    g_stage = nc.dram_tensor("g_stage", [SENT + 1, D], F32, kind="Internal").ap()
    s_stage = nc.dram_tensor("s_stage", [TOK, D], F32, kind="Internal").ap()

    ctx = contextlib.ExitStack()
    with ctx:
        AR = 51300
        arena = ctx.enter_context(nc.sbuf_tensor("arena", [128, AR], F32)).ap()

        def A(off, n):
            assert off + n <= AR, (off, n)
            return arena[:, off:off + n]

        def A3(off, dims, pcnt=128):
            return bass.AP(arena.tensor, off, [[AR, pcnt]] + dims)

        XA, XT, QKV, WB, SM, PS = 0, 17408, 34816, 41344, 47488, 49536

        xa = A(XA, NB * D)
        xnT = A(XT, KC * TA)

        xT = [A(QKV, 2048), A(QKV + 2048, 2048), A(QKV + 4096, 2048)]
        srch = A(XA + 6 * TA, 1024)
        gescrD = A(XA + 6 * TA + 1024, 1024)
        lgt = A(XA + 6 * TA + 2048, 96)
        sgP = A(XA + 7 * TA, 1024)
        lgTst = A(XA + 7 * TA + 1024, 1024)
        gescr = A(XA + 7 * TA + 2048, 64)
        fixs = A(XA + 7 * TA + 2112, 64)

        qT = A(QKV, 2048)
        kT = A(QKV + 2048, TA)
        vC = A(QKV + 2048 + TA, 18 * 128)
        wqb = [A(WB, 1024), A(WB + 3072, 1024)]
        wkb = [A(WB + 1024, 1024), A(WB + 4096, 1024)]
        wvb = [A(WB + 2048, 1024), A(WB + 5120, 1024)]

        woSM = A(SM, 2 * D)
        woWB = A(WB, 6 * D)
        stile = [A(QKV, D), A(QKV + D, D)]

        def woap(kc):
            return woSM[:, kc * D:(kc + 1) * D] if kc < 2 else \
                woWB[:, (kc - 2) * D:(kc - 1) * D]

        h1t = A(XA, 32 * CAP)
        toksT = A(XA + 32 * CAP, KC * CAP)
        b1pp = A(XA + 32 * CAP + KC * CAP, 32)
        w1r = [A(XT, 4096), A(XT + 4096, 4096)]
        w2r = [A(XT + 8192, 2048), A(XT + 8192 + 2048, 2048)]
        tg = [A(QKV, 3 * XGW), A(QKV + 3 * XGW, 3 * XGW)]
        tg0a = A(XT + 12288, 3 * XGW)

        def tgbuf(e):
            return tg0a if e == 0 else tg[e % 2]
        eob = A(WB, 3 * 1024)
        b2b = A(WB + 3 * 1024, D)
        zero = A(SM, XGW)

        fg = [A(QKV, D), A(QKV + D, D), A(XT + 12288, D)]
        fx = [A(QKV + 2 * D, D), A(QKV + 3 * D, D), A(XT + 12288 + D, D)]
        fs = [A(QKV + 4 * D, D), A(QKV + 5 * D, D), A(XT + 12288 + 2 * D, D)]
        fy = A(SM, D)
        fsc = A(SM + D, D)

        _p = [PS]

        def P_(n):
            r = A(_p[0], n)
            _p[0] += n
            return r

        identt = P_(128)
        maskAt = P_(128)
        mask0t = P_(128)
        iota16t = P_(16)
        bd8t = P_(EXP)
        bdTt = P_(128)
        onesrt = P_(128)
        rw9 = P_(KC * 9 + 8)
        rwsumb = P_(EXP)
        rbb = P_(EXP)
        taub = P_(EXP)
        taulb = P_(EXP)
        gm_all = P_(128)
        ssum = P_(NB + 1)
        sqsum = P_(NB + 1)
        mnegv = P_(NB + 1)
        varv = P_(NB + 1)
        rstdv = P_(NB + 1)
        tA_ = P_(NB + 1)
        tB_ = P_(NB + 1)
        tC_ = P_(NB + 1)
        g_sq = P_(EXP)
        g_mneg = P_(EXP)
        g_var = P_(EXP)
        g_rstd = P_(EXP)
        g_nm = P_(EXP)
        g_rs = P_(EXP)
        g_ri = P_(EXP)
        g_sA = P_(EXP)
        g_sB = P_(EXP)
        g_sC = P_(EXP)
        lo8 = [P_(1), P_(1)]
        hi8 = [P_(1), P_(1)]
        mid8 = P_(1)
        midpp = P_(1)
        cntp = P_(1)
        cnt8s = P_(1)
        taur = P_(8)
        selvt = P_(16)
        nmx2 = [P_(1), P_(1)]
        rsum2 = [P_(1), P_(1)]
        rinv2 = [P_(1), P_(1)]
        wtsb2 = [P_(128), P_(128)]
        wtsTa2 = [P_(128), P_(128)]
        attnT = [A(XA + kc * TA, TA) for kc in range(KC)]
        assert _p[0] <= AR, _p[0]

        selv_sb = ctx.enter_context(nc.sbuf_tensor("selv_sb", [16, 128], F32)).ap()
        idxraw = ctx.enter_context(nc.sbuf_tensor("idxraw", [16, EXP * CAPW], F32)).ap()
        idxsel = ctx.enter_context(nc.sbuf_tensor("idxsel", [16, EXP * CAPW], F32)).ap()
        c2048 = ctx.enter_context(nc.sbuf_tensor("c2048", [16, CAPW], F32)).ap()
        ioslt = ctx.enter_context(nc.sbuf_tensor("ioslt", [16, CAPW], F32)).ap()
        vmask = ctx.enter_context(nc.sbuf_tensor("vmask", [16, CAPW], U32)).ap()
        pred8 = ctx.enter_context(nc.sbuf_tensor("pred8", [8, 2], U32)).ap()
        idx16 = ctx.enter_context(nc.sbuf_tensor("idx16", [128, EXP * CAPW], I16)).ap()
        nfu = ctx.enter_context(nc.sbuf_tensor("nfu", [16, 2], U32)).ap()
        nfb = ctx.enter_context(nc.sbuf_tensor("nfb", [16, 2], F32)).ap()
        rbrow = ctx.enter_context(nc.sbuf_tensor("rbrow", [1, EXP], F32)).ap()
        b2row = ctx.enter_context(nc.sbuf_tensor("b2row", [1, D], F32)).ap()
        b1tmp = ctx.enter_context(nc.sbuf_tensor("b1tmp", [32, 128], F32)).ap()

        PBk = [ctx.enter_context(nc.psum_tensor(f"ps{i}", [128, 512], F32)).ap()
               for i in range(8)]

        class SemCtr:
            def __init__(self, name):
                self.h = ctx.enter_context(nc.semaphore(name))
                self.n = 0

            def inc(self, instr, k=1):
                instr.then_inc(self.h, k)
                self.n += k
                return self.n

            def dma(self, instr):
                return self.inc(instr, 16)

        sIn = SemCtr("sIn")
        sXa = SemCtr("sXa")
        sXb = SemCtr("sXb")
        sXc = SemCtr("sXc")
        sWc = SemCtr("sWc")
        sVc = SemCtr("sVc")
        sOutB = SemCtr("sOutB")
        sOutC = SemCtr("sOutC")
        sW0 = SemCtr("sW0")
        sW1 = SemCtr("sW1")
        sS0 = SemCtr("sS0")
        sS1 = SemCtr("sS1")
        sV0 = SemCtr("sV0")
        sV1 = SemCtr("sV1")
        sSt = SemCtr("sSt")
        sSt2 = SemCtr("sSt2")
        sOut = SemCtr("sOut")
        sBm = SemCtr("sBm")
        sGd = SemCtr("sGd")
        sGd1 = SemCtr("sGd1")
        sGs = SemCtr("sGs")
        sMM = SemCtr("sMM")
        sV = SemCtr("sV")
        sVg = SemCtr("sVg")
        sA = SemCtr("sA")
        sAe = SemCtr("sAe")
        sGp = SemCtr("sGp")
        XS = [sXa, sXb]
        WS = [sW0, sW1]
        SS = [sS0, sS1]
        VS = [sV0, sV1]

        blk = ctx.enter_context(nc.Block())

        def vchain(v, instr, ctr):
            ctr.inc(instr)
            v.wait_ge(ctr.h, ctr.n)

        RSQ_OPS = 3 + 4 * 4 + 1

        def emit_rsqrt(v, ctr, x_ap, out_ap, ta, tb, tc):
            xi = x_ap.bitcast(I32)
            yi = ta.bitcast(I32)
            vchain(v, v.tensor_scalar(yi, xi, 1, None, op0=Alu.logical_shift_right), ctr)
            vchain(v, v.tensor_scalar(yi, yi, 0, None, op0=Alu.bitwise_not), ctr)
            vchain(v, v.tensor_scalar(yi, yi, 0x5f3759e0, None, op0=Alu.add), ctr)
            for _ in range(4):
                vchain(v, v.tensor_tensor(tb, ta, ta, Alu.mult), ctr)
                vchain(v, v.tensor_tensor(tc, tb, x_ap, Alu.mult), ctr)
                vchain(v, v.tensor_scalar(tc, tc, -0.5, 1.5, op0=Alu.mult, op1=Alu.add), ctr)
                vchain(v, v.tensor_tensor(ta, ta, tc, Alu.mult), ctr)
            vchain(v, v.tensor_copy(out_ap, ta), ctr)

        if True:
            @blk.vector
            def _(v):
                v.memset(zero, 0.0)
                v.memset(c2048, float(SENT))
                v.memset(idx16, 0)
                v.memset(idxraw, -1.0)
                v.wait_ge(sIn.h, 16 * 10)
                sV.inc(v.memset(A3(rw9.offset + EXP, [[9, KC], [1, 1]]), 1.0))

            @blk.sync
            def _(sy):
                sIn.dma(sy.dma_start(out=identt, in_=ident_d))
                sIn.dma(sy.dma_start(out=maskAt, in_=maskA_d))
                sIn.dma(sy.dma_start(out=mask0t, in_=mask0_d))
                sIn.dma(sy.dma_start(out=iota16t, in_=iota_d))
                sIn.dma(sy.dma_start(out=bd8t, in_=bd8_d))
                sIn.dma(sy.dma_start(out=bdTt[0:8, :], in_=bdT_d))
                sIn.dma(sy.dma_start(out=onesrt[0:1, :], in_=ones_d))
                sIn.dma(sy.dma_start(out=ioslt, in_=ioslot_d))
                sIn.dma(sy.dma_start(out=rbrow, in_=rb_d))
                sIn.dma(sy.dma_start(
                    out=A3(rw9.offset, [[9, KC], [1, EXP]]),
                    in_=rw_d.rearrange("(kc p) e -> p kc e", p=128)))
                XQ = [sXc, sWc, sVc, sOutB]
                for q in range(4):
                    nb_q = [4, 4, 4, 5][q]
                    ob_q = [0, 4, 8, 12][q]
                    XQ[q].dma(sy.dma_start(
                        out=A3(XA + ob_q * D, [[D, nb_q], [1, D]]),
                        in_=bass.AP(x_att.tensor, ob_q * 128 * D,
                                    [[D, 128], [128 * D, nb_q], [1, D]])))
                assert sIn.n == 16 * 10
        p0_v = sV.n

        ssb8 = [sS0.n, sS1.n]
        if True:
            @blk.sync
            def _(sy):
                for hx in range(2):
                    WS[hx % 2].dma(sy.dma_start(
                        out=A3(xT[hx % 2].offset, [[256, KC], [1, 256]]),
                        in_=bass.AP(x_full_T.tensor, hx * 256,
                                    [[16384, 128], [128 * 16384, KC], [1, 256]])))
                for c in range(2):
                    sl = c % 2
                    SS[sl].dma(sy.dma_start(
                        out=A3(wqb[sl].offset, [[128, KC], [1, 128]]),
                        in_=Wq_d[:, c * 128:(c + 1) * 128].rearrange(
                            "(kc p) n -> p kc n", p=128)))
                    SS[sl].dma(sy.dma_start(
                        out=A3(wkb[sl].offset, [[128, KC], [1, 128]]),
                        in_=Wk_d[:, c * 128:(c + 1) * 128].rearrange(
                            "(kc p) n -> p kc n", p=128)))
                    SS[sl].dma(sy.dma_start(
                        out=A3(wvb[sl].offset, [[128, KC], [1, 128]]),
                        in_=Wv_d[:, c * 128:(c + 1) * 128].rearrange(
                            "(kc p) n -> p kc n", p=128)))

        if True:
            @blk.vector
            def _(v):
                v.wait_ge(sV.h, p0_v)
                for c in range(NB):
                    q = min(3, c // 4)
                    v.wait_ge([sXc, sWc, sVc, sOutB][q].h, 16)
                    v.tensor_reduce(ssum[:, c:c + 1], xa[:, c * D:(c + 1) * D], AX, Alu.add)
                sV.inc(v.engine_nop())
                v.wait_ge(sV.h, sV.n)
                v.wait_ge(sA.h, sA.n + NB)
                S, Q = ssum[:, 0:NB], sqsum[:, 0:NB]
                M, VV, R = mnegv[:, 0:NB], varv[:, 0:NB], rstdv[:, 0:NB]
                Ta, Tb, Tc = tA_[:, 0:NB], tB_[:, 0:NB], tC_[:, 0:NB]
                vchain(v, v.tensor_scalar(M, S, -1.0 / D, None, op0=Alu.mult), sV)
                vchain(v, v.tensor_tensor(Ta, M, M, Alu.mult), sV)
                vchain(v, v.tensor_scalar(Tb, Q, 1.0 / D, None, op0=Alu.mult), sV)
                vchain(v, v.tensor_tensor(VV, Tb, Ta, Alu.subtract), sV)
                vchain(v, v.tensor_scalar(VV, VV, 0.0, float(EPS_LN),
                                          op0=Alu.max, op1=Alu.add), sV)
                emit_rsqrt(v, sV, VV, R, Ta, Tb, Tc)
                for c in range(NB):
                    sV.inc(v.tensor_scalar(
                        xa[:, c * D:(c + 1) * D], xa[:, c * D:(c + 1) * D],
                        mnegv[:, c:c + 1], rstdv[:, c:c + 1],
                        op0=Alu.add, op1=Alu.mult))

            @blk.scalar
            def _(sc):
                sc.wait_ge(sV.h, p0_v)
                for c in range(NB):
                    sc.wait_ge([sXc, sWc, sVc, sOutB][min(3, c // 4)].h, 16)
                    sc.wait_ge(sA.h, sA.n)
                    sA.inc(sc.activation(vC[:, 0:1024], xa[:, c * D:(c + 1) * D],
                                         Act.Square,
                                         accum_out=sqsum[:, c:c + 1]))
        ln_done = sV.n

        _p2xn = [0]
        _p2sta = [0]
        if True:
            @blk.sync
            def _(sy):
                sy.wait_ge(sV.h, ln_done)
                if "t_xn" in taps:
                    sSt2.dma(sy.dma_start(
                        out=taps["t_xn"].rearrange("(b p) d -> p b d", p=128),
                        in_=A3(XA, [[D, NB], [1, D]])))
                for b in list(range(11, 16)) + list(range(0, 11)):
                    sb_ = sSt2 if b >= 11 else sSt
                    sb_.dma(sy.dma_start(
                        out=xg_stage[b * 128: b * 128 + 120, 0:D],
                        in_=xa[8:128, b * D:(b + 1) * D]))
                    sb_.dma(sy.dma_start(
                        out=xg_stage[b * 128 + 120: b * 128 + 128, 0:D],
                        in_=xa[0:8, (b + 1) * D:(b + 2) * D]))
                    if b == 15:
                        _p2xn[0] = sSt2.n
                sSt.dma(sy.dma_start(out=xg_stage[SENT:SENT + 1, :], in_=zero[0:1, :]))
                for b in range(16):
                    sSt.dma(sy.dma_start(
                        out=xg_stage[b * 128:(b + 1) * 128, 1024 + EXP:XGW],
                        in_=zero[:, 0:XGW - 1024 - EXP]))
                for b in range(16):
                    sSt.dma(sy.dma_start(out=g_stage[b * 128:(b + 1) * 128, :],
                                         in_=zero[:, 0:D]))
                sSt.dma(sy.dma_start(out=g_stage[SENT:SENT + 1, :], in_=zero[0:1, 0:D]))
                _p2sta[0] = sSt.n


        p3_mm = sMM.n
        p3_v = sV.n
        if True:
            @blk.vector
            def _(v):
                for i in range(NB * 2):
                    c, q = i // 2, i % 2
                    v.wait_ge(sMM.h, p3_mm + i + 1)
                    sV.inc(v.tensor_copy(
                        A3(XT + (4 * q) * TA + c * 128, [[TA, 4], [1, 128]]),
                        bass.AP(PBk[q].tensor, 0, [[512, 128], [128, 4], [1, 128]])))

            @blk.tensor
            def _(te):
                te.wait_ge(sV.h, ln_done)
                for i in range(NB * 2):
                    c, q = i // 2, i % 2
                    if i >= 2:
                        te.wait_ge(sV.h, p3_v + (i - 2) + 1)
                    for k4 in range(4):
                        kc = 4 * q + k4
                        mi = te.matmul(PBk[q][:, k4 * 128:(k4 + 1) * 128],
                                  xa[:, c * D + kc * 128: c * D + (kc + 1) * 128],
                                  identt, start=(k4 == 0), stop=(k4 == 3),
                                  is_transpose=True)
                    sMM.inc(mi)
        p3_done_v = sV.n

        p4i_mm = sMM.n
        p4i_v = sV.n
        if True:
            @blk.vector
            def _(v):
                v.wait_ge(sMM.h, p4i_mm + 1)
                sV.inc(v.tensor_copy(taur[0:8, 0:1], PBk[2][0:8, 0:1]))
                v.wait_ge(sMM.h, p4i_mm + 2)
                sV.inc(v.tensor_copy(taur[0:1, 0:8], PBk[3][0:1, 0:8]))
                v.wait_ge(sMM.h, p4i_mm + 3)
                v.tensor_copy(rwsumb, PBk[4][:, 0:8])
                sV.inc(v.tensor_copy(rbb, PBk[5][:, 0:8]))

            @blk.tensor
            def _(te):
                te.wait_ge(sV.h, p0_v)
                for kc in range(KC):
                    mi = te.matmul(PBk[2][0:8, 0:1],
                              A3(rw9.offset + kc * 9, [[1, EXP]]),
                              A3(rw9.offset + kc * 9 + EXP, [[1, 1]]),
                              start=(kc == 0), stop=(kc == KC - 1))
                sMM.inc(mi)
                te.wait_ge(sV.h, p4i_v + 1)
                mi = te.matmul(PBk[3][0:1, 0:8], taur[0:8, 0:1], identt[0:8, 0:8],
                          start=True, stop=True, is_transpose=True)
                sMM.inc(mi)
                te.wait_ge(sV.h, p4i_v + 2)
                mi = te.matmul(PBk[4][:, 0:8], onesrt[0:1, :], taur[0:1, 0:8],
                          start=True, stop=True)
                mi = te.matmul(PBk[5][:, 0:8], onesrt[0:1, :], rbrow, start=True, stop=True)
                sMM.inc(mi)
        p4i_done = sV.n

        p4_mm = sMM.n
        p4_v = sV.n
        p4_ae = sAe.n
        p4_vg = sVg.n
        GPOPS = 5 + RSQ_OPS + 4 + 2 + 3
        grp_scores_done = []
        grp_exp_wait = []
        p4_rel_evac = []
        NGRP = 16

        gtmp = srch[:, 0:128]

        if True:
            @blk.vector
            def _(v):
                v.wait_ge(sSt2.h, _p2xn[0])
                for g in range(NGRP):
                    gb_mm = p4_mm + g * 20
                    for q4 in range(4):
                        v.wait_ge(sMM.h, gb_mm + 3 * q4 + 1)
                        sV.inc(v.tensor_copy(lgTst[0:9, q4 * 256:(q4 + 1) * 256],
                                             PBk[2 + q4 % 2][0:9, 0:256]))
                        for tid in range(2):
                            v.wait_ge(sMM.h, gb_mm + 3 * q4 + 2 + tid)
                            vchain(v, v.tensor_tensor(
                                gtmp, PBk[4 + tid][:, 0:128], identt, Alu.mult), sV)
                            vchain(v, v.tensor_reduce(
                                g_sq[:, 2 * q4 + tid:2 * q4 + tid + 1],
                                gtmp, AX, Alu.add), sV)
                    for jj in range(8):
                        v.wait_ge(sMM.h, gb_mm + 12 + jj + 1)
                        sV.inc(v.tensor_copy(lgt[:, jj * 9:(jj + 1) * 9],
                                             PBk[jj % 2][:, 0:9]))
                    h, gp = g // 2, g % 2
                    L8 = A3(lgt.offset, [[9, 8], [1, 8]])
                    SX8 = A3(lgt.offset + 8, [[9, 8]])
                    FS_ = A3(fixs.offset, [[8, 8], [1, 8]])
                    EXG = A3(gescr.offset, [[8, 8], [1, 8]])
                    rw_bc = A3(rwsumb.offset, [[0, 8], [1, 8]])
                    rb_bc = A3(rbb.offset, [[0, 8], [1, 8]])
                    mn_bc = A3(g_mneg.offset, [[1, 8], [0, 8]])
                    rstd_bc = A3(g_rstd.offset, [[1, 8], [0, 8]])
                    nm_bc = A3(g_nm.offset, [[1, 8], [0, 8]])
                    ri_bc = A3(g_ri.offset, [[1, 8], [0, 8]])
                    v.wait_ge(sV.h, sV.n)
                    base_vg = sVg.n
                    vchain(v, v.tensor_scalar(g_mneg, SX8, -1.0 / D, None,
                                              op0=Alu.mult), sVg)
                    vchain(v, v.tensor_tensor(g_sA, g_mneg, g_mneg, Alu.mult), sVg)
                    vchain(v, v.tensor_scalar(g_sB, g_sq, 1.0 / D, None,
                                              op0=Alu.mult), sVg)
                    vchain(v, v.tensor_tensor(g_var, g_sB, g_sA, Alu.subtract), sVg)
                    vchain(v, v.tensor_scalar(g_var, g_var, 0.0, float(EPS_LN),
                                              op0=Alu.max, op1=Alu.add), sVg)
                    emit_rsqrt(v, sVg, g_var, g_rstd, g_sA, g_sB, g_sC)
                    vchain(v, v.tensor_tensor(FS_, rw_bc, mn_bc, Alu.mult), sVg)
                    vchain(v, v.tensor_tensor(FS_, FS_, L8, Alu.add), sVg)
                    vchain(v, v.tensor_tensor(FS_, FS_, rstd_bc, Alu.mult), sVg)
                    vchain(v, v.tensor_tensor(FS_, FS_, rb_bc, Alu.add), sVg)
                    vchain(v, v.tensor_reduce(g_nm, FS_, AX, Alu.max, negate=True), sVg)
                    vchain(v, v.tensor_tensor(FS_, FS_, nm_bc, Alu.add), sVg)
                    grp_exp_wait.append(sVg.n)
                    v.wait_ge(sAe.h, p4_ae + g + 1)
                    vchain(v, v.tensor_reduce(g_rs, EXG, AX, Alu.add), sVg)
                    vchain(v, v.reciprocal(g_ri, g_rs), sVg)
                    vchain(v, v.tensor_tensor(
                        A3(sgP.offset + h * 128 + gp * 64, [[8, 8], [1, 8]]),
                        EXG, ri_bc, Alu.mult), sVg)
                    grp_scores_done.append(sVg.n)
                    assert sVg.n - base_vg == GPOPS
                for h in range(8):
                    v.wait_ge(sMM.h, p4_mm + NGRP * 20 + h + 1)
                    p4_rel_evac.append(sV.inc(v.tensor_copy(
                        srch[:, h * 128:(h + 1) * 128], PBk[h % 2][:, 0:128])))

            @blk.scalar
            def _(sc):
                sc.wait_ge(sSt2.h, _p2xn[0])
                for g in range(NGRP):
                    sc.wait_ge(sVg.h, grp_exp_wait[g])
                    sAe.inc(sc.activation(A3(gescr.offset, [[8, 8], [1, 8]]),
                                          A3(fixs.offset, [[8, 8], [1, 8]]),
                                          Act.Exp))

            @blk.tensor
            def _(te):
                te.wait_ge(sV.h, p3_done_v)
                te.wait_ge(sV.h, p4i_done)
                XT3 = [sW0, sW1, sOutC]
                for g in range(NGRP):
                    gb_v = p4_v + g * 28
                    for q4 in range(4):
                        hx = g * 4 + q4
                        sl = hx % 3
                        te.wait_ge(XT3[sl].h, 16 * (hx // 3 + 1))
                        if hx >= 2:
                            pg, pq = (hx - 2) // 4, (hx - 2) % 4
                            te.wait_ge(sV.h, p4_v + pg * 28 + 5 * pq + 1)
                        for kc in range(KC):
                            mi = te.matmul(
                                PBk[2 + q4 % 2][0:9, 0:256],
                                A3(rw9.offset + kc * 9, [[1, 9]]),
                                xT[sl][:, kc * 256:(kc + 1) * 256],
                                start=(kc == 0), stop=(kc == KC - 1))
                        sMM.inc(mi)
                        for tid in range(2):
                            if hx >= 1:
                                pg, pq = (hx - 1) // 4, (hx - 1) % 4
                                te.wait_ge(sV.h,
                                           p4_v + pg * 28 + 5 * pq + 2 + 2 * tid)
                            for kc in range(KC):
                                xtile = xT[sl][:, kc * 256 + tid * 128:
                                               kc * 256 + (tid + 1) * 128]
                                mi = te.matmul(PBk[4 + tid][:, 0:128],
                                               xtile, xtile,
                                               start=(kc == 0), stop=(kc == KC - 1))
                            sMM.inc(mi)
                    te.wait_ge(sV.h, gb_v + 16)
                    for jj in range(8):
                        if jj >= 2:
                            te.wait_ge(sV.h, gb_v + 20 + (jj - 2) + 1)
                        elif g >= 1:
                            te.wait_ge(sV.h, p4_v + (g - 1) * 28 + 20 + (jj + 6) + 1)
                        mi = te.matmul(PBk[jj % 2][:, 0:9],
                                       lgTst[0:9, jj * 128:(jj + 1) * 128],
                                       identt[0:9, 0:9],
                                       start=True, stop=True, is_transpose=True)
                        sMM.inc(mi)
                te.wait_ge(sV.h, p4_v + NGRP * 28)
                for h in range(8):
                    te.wait_ge(sVg.h, grp_scores_done[2 * h + 1])
                    if h >= 2:
                        te.wait_ge(sV.h, p4_rel_evac[h - 2])
                    mi = te.matmul(PBk[h % 2][:, 0:128], sgP[:, h * 128:(h + 1) * 128],
                                   identt, start=True, stop=True, is_transpose=True)
                    sMM.inc(mi)

            @blk.sync
            def _(sy):
                XT3 = [sW0, sW1, sOutC]
                for hx in range(2, 64):
                    sl = hx % 3
                    if hx == 2:
                        sy.wait_ge(sV.h, ln_done)
                    if hx >= 3:
                        pg, pq = (hx - 3) // 4, (hx - 3) % 4
                        sy.wait_ge(sMM.h, p4_mm + pg * 20 + 3 * pq + 3)
                    XT3[sl].dma(sy.dma_start(
                        out=A3(xT[sl].offset, [[256, KC], [1, 256]]),
                        in_=bass.AP(x_full_T.tensor, hx * 256,
                                    [[16384, 128], [128 * 16384, KC], [1, 256]])))

        p4_done_v = sV.n
        p4_done_vg = sVg.n
        p4_done_mm = sMM.n

        p5_vg = sVg.n
        p5_mid_ready = []
        p5_midpp_evac = []
        p5_cnt_ready = []
        p5_cnt_evac = []
        p5_fin = []
        p7_selv_evac = []
        p7_nfready = []
        p7_nfb_evac = []
        p7_selvt_done = []
        p7_gp = sGp.n
        fe_pairs = []

        def _p5v_init(v):
            v.wait_ge(sV.h, p4_done_v)
            v.wait_ge(sVg.h, p4_done_vg)
            vchain(v, v.memset(lo8[0][0:8, :], 0.0), sVg)
            vchain(v, v.memset(hi8[0][0:8, :], 2.0), sVg)

        def _p5t_init(te):
            te.wait_ge(sV.h, p4_done_v)

        fe_pairs.append((_p5v_init, _p5t_init))

        def _p5v_iter(it):
            def f(v):
                pa, pb = it % 2, 1 - it % 2
                vchain(v, v.tensor_tensor(mid8[0:8, :], lo8[pa][0:8, :],
                                          hi8[pa][0:8, :], Alu.add), sVg)
                vchain(v, v.tensor_scalar(mid8[0:8, :], mid8[0:8, :], 0.5, None,
                                          op0=Alu.mult), sVg)
                p5_mid_ready.append(sVg.n)
                v.wait_ge(sBm.h, it * 2 + 1)
                vchain(v, v.tensor_copy(midpp, PBk[3][:, 0:1]), sVg)
                p5_midpp_evac.append(sVg.n)
                vchain(v, v.tensor_scalar(gescrD, srch, midpp, None,
                                          op0=Alu.is_ge, op1=Alu.add,
                                          accum_out=cntp), sVg)
                p5_cnt_ready.append(sVg.n)
                v.wait_ge(sBm.h, it * 2 + 2)
                vchain(v, v.tensor_copy(cnt8s[0:8, :], PBk[3][0:8, 0:1]), sVg)
                p5_cnt_evac.append(sVg.n)
                vchain(v, v.tensor_scalar(pred8[:, 0:1], cnt8s[0:8, :], 2048.0,
                                          None, op0=Alu.is_ge), sVg)
                sVg.inc(v.select(lo8[pb][0:8, :], pred8[:, 0:1], mid8[0:8, :],
                                 lo8[pa][0:8, :], add_drain=True))
                v.wait_ge(sVg.h, sVg.n)
                sVg.inc(v.select(hi8[pb][0:8, :], pred8[:, 0:1], hi8[pa][0:8, :],
                                 mid8[0:8, :], add_drain=True))
                v.wait_ge(sVg.h, sVg.n)
            return f

        def _p5t_iter(it):
            def f(te):
                te.wait_ge(sVg.h, p5_mid_ready[it])
                if it >= 1:
                    te.wait_ge(sVg.h, p5_midpp_evac[it - 1])
                    te.wait_ge(sVg.h, p5_cnt_evac[it - 1])
                sBm.inc(te.matmul(PBk[3][:, 0:1], bdTt[0:8, :], mid8[0:8, 0:1],
                                  start=True, stop=True))
                te.wait_ge(sVg.h, p5_cnt_ready[it])
                sBm.inc(te.matmul(PBk[3][0:8, 0:1], bd8t, cntp,
                                  start=True, stop=True))
            return f

        for _it in range(NIT):
            fe_pairs.append((_p5v_iter(_it), _p5t_iter(_it)))

        def _p5v_fin(v):
            p5_fin.append(sVg.n)
            v.wait_ge(sBm.h, NIT * 2 + 1)
            vchain(v, v.tensor_copy(taur[0:1, 0:8], PBk[3][0:1, 0:8]), sVg)
            p5_fin.append(sVg.n)
            v.wait_ge(sBm.h, NIT * 2 + 2)
            vchain(v, v.tensor_copy(taub, PBk[3][:, 0:8]), sVg)
            vchain(v, v.tensor_scalar(taulb, taub, float(LOOSE), None,
                                      op0=Alu.mult), sVg)
            for e in range(EXP):
                ssl = A3(sgP.offset + e, [[8, 16]])
                sVg.inc(v.scalar_tensor_tensor(
                    A3(gm_all.offset + e, [[8, 16]]),
                    ssl, taub[:, e:e + 1], ssl, op0=Alu.is_ge, op1=Alu.mult))
            v.wait_ge(sVg.h, sVg.n)
            sVg.inc(v.engine_nop())

        def _p5t_fin(te):
            te.wait_ge(sVg.h, p5_fin[0])
            sBm.inc(te.matmul(PBk[3][0:1, 0:8], lo8[NIT % 2][0:8, 0:1],
                              identt[0:8, 0:8],
                              start=True, stop=True, is_transpose=True))
            te.wait_ge(sVg.h, p5_fin[1])
            sBm.inc(te.matmul(PBk[3][:, 0:8], onesrt[0:1, :], taur[0:1, 0:8],
                              start=True, stop=True))

        fe_pairs.append((_p5v_fin, _p5t_fin))
        p6_vg_box = [0]

        def _p6v_mark(v):
            p6_vg_box[0] = sVg.n

        fe_pairs.append((_p6v_mark, None))

        def _p7v(e):
            def f(v):
                ssl = A3(sgP.offset + e, [[8, 16]])
                if e >= 1:
                    v.wait_ge(sBm.h, 2 * NIT + 2 + 2 * e)
                vchain(v, v.scalar_tensor_tensor(
                    selvt, ssl, taulb[:, e:e + 1], iota16t,
                    op0=Alu.is_ge, op1=Alu.mult), sVg)
                vchain(v, v.tensor_scalar(selvt, selvt, -1.0, None,
                                          op0=Alu.add), sVg)
                p7_selvt_done.append(sVg.n)
                v.wait_ge(sBm.h, 2 * NIT + 2 + 2 * e + 1)
                vchain(v, v.tensor_copy(selv_sb, PBk[3][0:16, 0:128]), sVg)
                p7_selv_evac.append(sVg.n)
                v.wait_ge(sGp.h, p7_gp + e + 1)
                vchain(v, v.tensor_copy(nfb[0:1, 0:1], nfu[0:1, 0:1]), sVg)
                p7_nfready.append(sVg.n)
                v.wait_ge(sBm.h, 2 * NIT + 2 + 2 * e + 2)
                vchain(v, v.tensor_copy(nfb[0:16, 1:2], PBk[3][0:16, 0:1]), sVg)
                p7_nfb_evac.append(sVg.n)
                vchain(v, v.tensor_scalar(vmask, ioslt, nfb[0:16, 1:2], None,
                                          op0=Alu.is_lt), sVg)
                sVg.inc(v.select(idxsel[:, e * CAPW:(e + 1) * CAPW], vmask,
                                 idxraw[:, e * CAPW:(e + 1) * CAPW], c2048,
                                 add_drain=True))
                v.wait_ge(sVg.h, sVg.n)
                vchain(v, v.tensor_copy(idx16[0:16, e * CAPW:(e + 1) * CAPW],
                                        idxsel[:, e * CAPW:(e + 1) * CAPW]), sVg)
            return f

        def _p7t(e):
            def f(te):
                te.wait_ge(sVg.h, p7_selvt_done[e])
                if e >= 1:
                    te.wait_ge(sVg.h, p7_selv_evac[e - 1])
                sBm.inc(te.matmul(PBk[3][0:16, 0:128], selvt, identt,
                                  start=True, stop=True, is_transpose=True))
                te.wait_ge(sVg.h, p7_nfready[e])
                if e >= 1:
                    te.wait_ge(sVg.h, p7_nfb_evac[e - 1])
                sBm.inc(te.matmul(PBk[3][0:16, 0:1], onesrt[0:1, 0:16],
                                  nfb[0:1, 0:1], start=True, stop=True))
            return f

        for _e in range(EXP):
            fe_pairs.append((_p7v(_e), _p7t(_e)))
        p7dv_box = [0]

        def _p7v_mark(v):
            p7dv_box[0] = sVg.n

        fe_pairs.append((_p7v_mark, None))


        fe_vi = [0, 0]
        fe_ti = [0, 0]

        def weave_v(v):
            fe_vi[0] += 1
            if fe_vi[0] % 5 == 0 and fe_vi[1] < len(fe_pairs):
                fn = fe_pairs[fe_vi[1]][0]
                fe_vi[1] += 1
                if fn is not None:
                    fn(v)

        def weave_t(te):
            fe_ti[0] += 1
            if fe_ti[0] % 5 == 0 and fe_ti[1] < len(fe_pairs):
                fn = fe_pairs[fe_ti[1]][1]
                fe_ti[1] += 1
                if fn is not None:
                    fn(te)

        p8_mm = sMM.n
        p8_v = sV.n
        p8_a = sA.n
        NJT = 18
        NHT = 2 * NJT
        PE_PER_C = 27 + 3 * NHT
        V_PER_C = 27 + 5 * NHT
        A_PER_C = 2 * NHT
        QP, KP = 4, 5

        def mmv(k):
            return 27 + 3 * k

        p8_ev = {}
        p8_sc = {}
        p8_nm = {}
        p8_qkv = {}
        pe_scores = {}
        pe_T = {}
        pe_attn = {}
        _n = 27
        for _k in range(NHT + 2):
            if _k < NHT:
                _n += 1; pe_scores[_k] = _n
            if 1 <= _k <= NHT:
                _n += 1; pe_T[_k - 1] = _n
            if _k >= 2:
                _n += 1; pe_attn[_k - 2] = _n
        assert _n == PE_PER_C, _n

        if True:
            @blk.vector
            def _(v):
                v.wait_ge(sSt.h, _p2sta[0])
                for c in range(KC):
                    vb = p8_v + c * V_PER_C
                    mb = p8_mm + c * PE_PER_C
                    ab = p8_a + c * A_PER_C
                    for i in range(QP):
                        v.wait_ge(sMM.h, mb + i + 1)
                        sV.inc(v.tensor_scalar(qT[:, i * 512:(i + 1) * 512],
                                               PBk[i % 2][:, 0:512],
                                               float(INV2T), None, op0=Alu.mult))
                        weave_v(v)
                    for i in range(KP):
                        npz = 512 if i < 4 else 128
                        v.wait_ge(sMM.h, mb + QP + i + 1)
                        sV.inc(v.tensor_copy(kT[:, i * 512:i * 512 + npz],
                                             PBk[i % 2][:, 0:npz]))
                        weave_v(v)
                    for b in range(NJT):
                        v.wait_ge(sMM.h, mb + QP + KP + b + 1)
                        sV.inc(v.tensor_copy(vC[:, b * 128:(b + 1) * 128],
                                             PBk[b % 2][:, 0:128]))
                        weave_v(v)
                    p8_qkv[c] = sV.n
                    def qwof(jt):
                        return 120 if jt < NJT - 1 else TOK - 120 * (NJT - 1)
                    for k in range(NHT + 2):
                        if k < NHT:
                            h2, jt = k // NJT, k % NJT
                            qw = qwof(jt)
                            sb = [4, 5, 6][k % 3]
                            mt = mask0t if jt == 0 else maskAt
                            v.wait_ge(sMM.h, mb + pe_scores[k])
                            sV.inc(v.tensor_tensor(PBk[sb][0:qw, 0:128],
                                                   PBk[sb][0:qw, 0:128],
                                                   mt[0:qw, 0:128], Alu.add))
                            v.wait_ge(sV.h, sV.n)
                            sV.inc(v.tensor_reduce(nmx2[k % 2][0:qw, :],
                                                   PBk[sb][0:qw, 0:128], AX,
                                                   Alu.max, negate=True))
                            p8_nm[(c, k)] = sV.n
                        if 1 <= k <= NHT:
                            kp = k - 1
                            qwp = qwof(kp % NJT)
                            v.wait_ge(sA.h, ab + 2 * kp + 1)
                            sV.inc(v.reciprocal(rinv2[kp % 2][0:qwp, :],
                                                rsum2[kp % 2][0:qwp, :]))
                            v.wait_ge(sV.h, sV.n)
                            sV.inc(v.tensor_scalar(wtsb2[kp % 2][0:qwp, :],
                                                   wtsb2[kp % 2][0:qwp, :],
                                                   rinv2[kp % 2][0:qwp, :], None,
                                                   op0=Alu.mult))
                            p8_sc[(c, kp)] = sV.n
                        if k >= 2:
                            kp = k - 2
                            h2p, jtp = kp // NJT, kp % NJT
                            qwp = qwof(jtp)
                            v.wait_ge(sMM.h, mb + pe_attn[kp])
                            sV.inc(v.tensor_copy(
                                attnT[c][64 * h2p:64 * h2p + 64,
                                         jtp * 120:jtp * 120 + qwp],
                                PBk[[7, 2][kp % 2]][64 * h2p:64 * h2p + 64, 0:qwp]))
                            p8_ev[(c, kp)] = sV.n
                        weave_v(v)

            @blk.scalar
            def _(sc):
                for c in range(KC):
                    mb = p8_mm + c * PE_PER_C
                    for k in range(NHT + 1):
                        if k < NHT:
                            jt = k % NJT
                            qw = 120 if jt < NJT - 1 else TOK - 120 * (NJT - 1)
                            sb = [4, 5, 6][k % 3]
                            sc.wait_ge(sV.h, p8_nm[(c, k)])
                            if k >= 2:
                                sc.wait_ge(sMM.h, mb + pe_T[k - 2])
                            elif c >= 1:
                                sc.wait_ge(sMM.h, p8_mm + (c - 1) * PE_PER_C
                                           + pe_T[NHT - 2 + k])
                            sA.inc(sc.activation(wtsb2[k % 2][0:qw, :],
                                                 PBk[sb][0:qw, 0:128],
                                                 Act.Exp, bias=nmx2[k % 2][0:qw, :],
                                                 scale=1.0,
                                                 accum_out=rsum2[k % 2][0:qw, :]))
                        if k >= 1:
                            kp = k - 1
                            qwp = 120 if kp % NJT < NJT - 1 else TOK - 120 * (NJT - 1)
                            sc.wait_ge(sMM.h, mb + pe_T[kp])
                            if kp >= 2:
                                sc.wait_ge(sMM.h, mb + pe_attn[kp - 2])
                            elif c >= 1:
                                sc.wait_ge(sMM.h, p8_mm + (c - 1) * PE_PER_C
                                           + pe_attn[NHT - 2 + kp])
                            sA.inc(sc.copy(wtsTa2[kp % 2][:, 0:qwp],
                                           PBk[kp % 2][:, 0:qwp]))

            @blk.tensor
            def _(te):
                te.wait_ge(sV.h, p4_done_v)
                for c in range(KC):
                    vb = p8_v + c * V_PER_C
                    ab = p8_a + c * A_PER_C
                    mb = p8_mm + c * PE_PER_C
                    sl = c % 2
                    te.wait_ge(SS[sl].h, ssb8[sl] + 48 * (c // 2 + 1))
                    if c == 0:
                        te.wait_ge(sV.h, p3_done_v)
                    for i in range(QP):
                        bk = i % 2
                        if i >= 2:
                            te.wait_ge(sV.h, vb + (i - 2) + 1)
                        elif c >= 1:
                            te.wait_ge(sA.h, p8_a + c * A_PER_C)
                            te.wait_ge(sV.h, p8_v + c * V_PER_C)
                        for kc in range(KC):
                            mi = te.matmul(PBk[bk][:, 0:512],
                                           wqb[sl][:, kc * 128:(kc + 1) * 128],
                                           xnT[:, kc * TA + 8 + i * 512:
                                               kc * TA + 8 + (i + 1) * 512],
                                           start=(kc == 0), stop=(kc == KC - 1))
                        sMM.inc(mi)
                        weave_t(te)
                    for i in range(KP):
                        npz = 512 if i < 4 else 128
                        bk = i % 2
                        te.wait_ge(sV.h, vb + QP + (i - 2) + 1 if i >= 2 else vb + QP)
                        for kc in range(KC):
                            mi = te.matmul(PBk[bk][:, 0:npz],
                                           wkb[sl][:, kc * 128:(kc + 1) * 128],
                                           xnT[:, kc * TA + i * 512: kc * TA + i * 512 + npz],
                                           start=(kc == 0), stop=(kc == KC - 1))
                        sMM.inc(mi)
                        weave_t(te)
                    for b in range(NJT):
                        bk = b % 2
                        te.wait_ge(sV.h, vb + QP + KP + (b - 2) + 1 if b >= 2
                                   else vb + QP + KP)
                        for kc in range(KC):
                            mi = te.matmul(PBk[bk][:, 0:128],
                                           xnT[:, kc * TA + b * 120: kc * TA + b * 120 + 128],
                                           wvb[sl][:, kc * 128:(kc + 1) * 128],
                                           start=(kc == 0), stop=(kc == KC - 1))
                        sMM.inc(mi)
                        weave_t(te)
                    for k in range(NHT + 2):
                        if k < NHT:
                            h2, jt = k // NJT, k % NJT
                            qw = 120 if jt < NJT - 1 else TOK - 120 * (NJT - 1)
                            sb = [4, 5, 6][k % 3]
                            te.wait_ge(sV.h, p8_qkv[c])
                            if k >= 3:
                                te.wait_ge(sA.h, ab + 2 * (k - 3) + 1)
                            elif c >= 1:
                                te.wait_ge(sA.h, p8_a + (c - 1) * A_PER_C
                                           + 2 * (NHT - 3 + k) + 1)
                            mi = te.matmul(PBk[sb][0:qw, 0:128],
                                           qT[64 * h2:64 * h2 + 64,
                                              jt * 120: jt * 120 + qw],
                                           kT[64 * h2:64 * h2 + 64,
                                              jt * 120: jt * 120 + 128],
                                           start=True, stop=True)
                            sMM.inc(mi)
                        if 1 <= k <= NHT:
                            kp = k - 1
                            qwp = 120 if kp % NJT < NJT - 1 else TOK - 120 * (NJT - 1)
                            te.wait_ge(sV.h, p8_sc[(c, kp)])
                            mi = te.matmul(PBk[kp % 2][:, 0:qwp],
                                           wtsb2[kp % 2][0:qwp, 0:128],
                                           identt[0:qwp, 0:qwp],
                                           start=True, stop=True, is_transpose=True)
                            sMM.inc(mi)
                        if k >= 2:
                            kp = k - 2
                            h2p, jtp = kp // NJT, kp % NJT
                            qwp = 120 if jtp < NJT - 1 else TOK - 120 * (NJT - 1)
                            te.wait_ge(sA.h, ab + 2 * kp + 2)
                            if kp >= 2:
                                te.wait_ge(sV.h, p8_ev[(c, kp - 2)])
                            elif c >= 1:
                                te.wait_ge(sV.h, p8_ev[(c - 1, NHT - 2 + kp)])
                            tp = {"tile_position": (0, 64)} if h2p else {}
                            mi = te.matmul(PBk[[7, 2][kp % 2]][64 * h2p:64 * h2p + 64,
                                                               0:qwp],
                                           vC[0:128, jtp * 128 + 64 * h2p:
                                              jtp * 128 + 64 * h2p + 64],
                                           wtsTa2[kp % 2][:, 0:qwp],
                                           start=True, stop=True, **tp)
                            sMM.inc(mi)
                        weave_t(te)

            @blk.sync
            def _(sy):
                for c in range(2, KC):
                    sl = c % 2
                    sy.wait_ge(sMM.h, p8_mm + (c - 1) * PE_PER_C + 27)
                    SS[sl].dma(sy.dma_start(
                        out=A3(wqb[sl].offset, [[128, KC], [1, 128]]),
                        in_=Wq_d[:, c * 128:(c + 1) * 128].rearrange(
                            "(kc p) n -> p kc n", p=128)))
                    SS[sl].dma(sy.dma_start(
                        out=A3(wkb[sl].offset, [[128, KC], [1, 128]]),
                        in_=Wk_d[:, c * 128:(c + 1) * 128].rearrange(
                            "(kc p) n -> p kc n", p=128)))
                    SS[sl].dma(sy.dma_start(
                        out=A3(wvb[sl].offset, [[128, KC], [1, 128]]),
                        in_=Wv_d[:, c * 128:(c + 1) * 128].rearrange(
                            "(kc p) n -> p kc n", p=128)))
        p8_done_v = sV.n
        p8_done_mm = sMM.n
        p8_done_a = sA.n
        p7_done_vg = p7dv_box[0]
        p6_vg = p6_vg_box[0]
        assert fe_vi[1] == len(fe_pairs) and fe_ti[1] == len(fe_pairs)

        if True:
            @blk.gpsimd
            def _(gp):
                gp.load_library(libcfg.sparse_gather)
                for e in range(EXP):
                    gp.wait_ge(sVg.h, p7_selv_evac[e])
                    sGp.inc(gp.sparse_gather(idxraw[:, e * CAPW:(e + 1) * CAPW],
                                             selv_sb, num_found=nfu[0:1, 0:1]))
                gp.wait_ge(sVg.h, p7_done_vg)
                sGd.dma(gp.dma_start(out=idx16[16:32, :], in_=idx16[0:16, :]))

            @blk.sync
            def _(sy):
                sy.wait_ge(sVg.h, p6_vg)
                sSt.dma(sy.dma_start(
                    out=bass.AP(xg_stage.tensor, 1024,
                                [[XGW, 128], [128 * XGW, 16], [1, 8]]),
                    in_=A3(gm_all.offset, [[8, 16], [1, 8]])))
                if "t_tau" in taps:
                    sSt.dma(sy.dma_start(out=taps["t_tau"][:, 0:EXP], in_=taub))
                    sSt.dma(sy.dma_start(out=taps["t_tau"][:, EXP:2 * EXP],
                                         in_=taulb))
                if "t_idx" in taps:
                    sy.wait_ge(sVg.h, p7_done_vg)
                    sSt.dma(sy.dma_start(out=taps["t_idx"], in_=idxsel))


        p9_mm = sMM.n
        p9_v = sV.n
        w0base = sW0.n
        p9_evac = []
        if True:
            @blk.vector
            def _(v):
                S_PER = 32 if "t_s" in taps else 16
                ssb = [sS0.n, sS1.n]
                for tt in range(16):
                    sl = tt % 2
                    for half in range(2):
                        v.wait_ge(sMM.h, p9_mm + tt * 2 + half + 1)
                        if tt >= 2:
                            v.wait_ge(SS[sl].h, ssb[sl] + S_PER * (tt // 2))
                        p9_evac.append(sV.inc(v.tensor_copy(
                            stile[sl][:, half * 512:(half + 1) * 512],
                            PBk[4 + half][:, 0:512])))

            @blk.tensor
            def _(te):
                te.wait_ge(sV.h, p8_done_v)
                te.wait_ge(sA.h, p8_done_a)
                te.wait_ge(sW0.h, w0base + 16 * 8)
                for tt in range(16):
                    for half in range(2):
                        if tt >= 1:
                            te.wait_ge(sV.h, p9_evac[(tt - 1) * 2 + half])
                        for kc in range(KC):
                            mi = te.matmul(PBk[4 + half][:, 0:512],
                                      attnT[kc][:, tt * 128:(tt + 1) * 128],
                                      woap(kc)[:, half * 512:half * 512 + 512],
                                      start=(kc == 0), stop=(kc == KC - 1))
                        sMM.inc(mi)

            @blk.sync
            def _(sy):
                sy.wait_ge(sSt.h, sSt.n)
                for kc in range(2):
                    sW0.dma(sy.dma_start(
                        out=A3(woSM.offset + kc * D, [[1, D]]),
                        in_=Wo_d[kc * 128:(kc + 1) * 128, :]))
                sy.wait_ge(sMM.h, p8_mm + (KC - 1) * PE_PER_C + 27)
                for kc in range(2, KC):
                    sW0.dma(sy.dma_start(
                        out=A3(woWB.offset + (kc - 2) * D, [[1, D]]),
                        in_=Wo_d[kc * 128:(kc + 1) * 128, :]))
                for tt in range(16):
                    sl = tt % 2
                    sy.wait_ge(sV.h, p9_v + tt * 2 + 2)
                    SS[sl].dma(sy.dma_start(
                        out=s_stage[tt * 128:(tt + 1) * 128, :], in_=stile[sl]))
                    if "t_s" in taps:
                        SS[sl].dma(sy.dma_start(
                            out=taps["t_s"][tt * 128:(tt + 1) * 128, :], in_=stile[sl]))
        p9_done_v = sV.n
        p9_done_mm = sMM.n
        s_dma_done = (sS0.n, sS1.n)

        p10_mm = sMM.n
        p10_v = sV.n
        p10_a = sA.n
        p10_gd = sGd.n
        p10_gd1 = sGd1.n
        GD = [sGd, sGd1]
        gdb = [p10_gd, p10_gd1]

        def gdone(e):
            return GD[e % 2].h, gdb[e % 2] + 16 * (e // 2 + 1)
        p10_gs = sGs.n
        inbase = sIn.n
        w1base = [sW0.n, sW1.n]
        w2base = [sV0.n, sV1.n]
        PE_E = 48
        V_E = 38
        A_E = 16

        def mmb(e):
            return p10_mm + e * PE_E

        def vbb(e):
            return p10_v + e * V_E

        def abb(e):
            return p10_a + e * A_E

        if True:
            @blk.vector
            def _(v):
                for e in range(EXP):
                    v.wait_ge(sMM.h, mmb(e) + 1)
                    sV.inc(v.tensor_copy(b1pp, PBk[0][:, 0:32]))
                    v.wait_ge(sMM.h, mmb(e) + 2)
                    sV.inc(v.tensor_copy(b2b[:, 0:512], PBk[2][:, 0:512]))
                    sV.inc(v.tensor_copy(b2b[:, 512:1024], PBk[3][:, 0:512]))
                    for qd in range(6):
                        v.wait_ge(sMM.h, mmb(e) + 3 + qd)
                        blk_i, q = qd // 2, qd % 2
                        sV.inc(v.tensor_copy(
                            A3(toksT.offset + (4 * q) * CAP + blk_i * 128,
                               [[CAP, 4], [1, 128]]),
                            bass.AP(PBk[qd % 2].tensor, 0, [[512, 128], [128, 4], [1, 128]])))
                    for g in range(8):
                        bk = (4 if g % 2 == 0 else 0)
                        v.wait_ge(sMM.h, mmb(e) + 9 + g)
                        for m4 in (0, 2):
                            m = g * 4 + m4
                            sV.inc(v.tensor_scalar(
                                h1t[:, m * CAP:m * CAP + CAPF], PBk[bk + m4][:, 0:CAPF],
                                b1pp[:, m:m + 1], 0.0, op0=Alu.add, op1=Alu.max))
                    v.wait_ge(sMM.h, mmb(e) + PE_E)
                    v.wait_ge(*gdone(e))
                    if e >= 1:
                        v.wait_ge(sGs.h, p10_gs + 16 * e)
                    sV.inc(v.memset(eob[64:128, 2 * 1024:3 * 1024], 0.0))
                    v.wait_ge(sV.h, sV.n)
                    for blk_i in range(3):
                        tw = 128 if blk_i < 2 else CAPF - 256
                        for half in range(2):
                            bk = 2 + blk_i * 2 + half
                            dst = eob[0:tw, blk_i * 1024 + half * 512:
                                      blk_i * 1024 + half * 512 + 512]
                            sV.inc(v.tensor_tensor(dst, PBk[bk][0:tw, 0:512],
                                                   b2b[0:tw, half * 512:(half + 1) * 512],
                                                   Alu.add))
                    v.wait_ge(sV.h, sV.n)
                    for blk_i in range(3):
                        tw = 128 if blk_i < 2 else CAPF - 256
                        for half in range(2):
                            dst = eob[0:tw, blk_i * 1024 + half * 512:
                                      blk_i * 1024 + half * 512 + 512]
                            gmcol = tgbuf(e)[0:tw, blk_i * XGW + 1024 + e:
                                             blk_i * XGW + 1024 + e + 1]
                            sV.inc(v.tensor_scalar(dst, dst, gmcol, None, op0=Alu.mult))
                    v.wait_ge(sV.h, sV.n)

            @blk.scalar
            def _(sc):
                for e in range(EXP):
                    for g in range(8):
                        bk = (4 if g % 2 == 0 else 0)
                        sc.wait_ge(sMM.h, mmb(e) + 9 + g)
                        for m4 in (1, 3):
                            m = g * 4 + m4
                            sA.inc(sc.activation(
                                h1t[:, m * CAP:m * CAP + CAPF], PBk[bk + m4][:, 0:CAPF],
                                Act.Relu, bias=b1pp[:, m:m + 1]))

            @blk.gpsimd
            def _(gp):
                gp.load_library(libcfg.mlp)
                gp.wait_ge(sSt.h, sSt.n)
                gp.wait_ge(sSt2.h, sSt2.n)
                gp.wait_ge(sGd.h, p10_gd)
                for k in range(EXP + 2):
                    if k >= 2:
                        ep = k - 2
                        gp.wait_ge(sV.h, vbb(ep) + V_E)
                        sGs.dma(gp.dma_scatter_add(
                            g_stage, eob.rearrange("p (b w) -> p b w", w=D),
                            idx16[:, ep * CAPW:(ep + 1) * CAPW],
                            num_idxs=CAP, num_idxs_reg=CAP, elem_size=D))
                    if k < EXP:
                        if k == 0:
                            gp.wait_ge(sMM.h, p8_mm + (KC - 1) * PE_PER_C + 27)
                        elif k == 1:
                            gp.wait_ge(sMM.h, p8_done_mm)
                        else:
                            if k == 2:
                                gp.wait_ge(sS0.h, s_dma_done[0])
                                gp.wait_ge(sS1.h, s_dma_done[1])
                            gp.wait_ge(sMM.h, mmb(k - 2) + 8)
                        GD[k % 2].dma(gp.dma_gather(
                            tgbuf(k).rearrange("p (b w) -> p b w", w=XGW),
                            xg_stage, idx16[:, k * CAPW:(k + 1) * CAPW],
                            num_idxs=CAP, num_idxs_reg=CAP, elem_size=XGW))

            @blk.tensor
            def _(te):
                te.wait_ge(sMM.h, p9_done_mm)
                te.wait_ge(sA.h, p8_done_a)
                for e in range(EXP):
                    te.wait_ge(sIn.h, inbase + 32 * e + 32)
                    if e >= 1:
                        te.wait_ge(sV.h, vbb(e - 1) + 25)
                        te.wait_ge(sA.h, abb(e - 1) + A_E)
                    mi = te.matmul(PBk[0][:, 0:32], b1tmp, identt[0:32, 0:32],
                              start=True, stop=True, is_transpose=True)
                    sMM.inc(mi)
                    if e >= 1:
                        te.wait_ge(sV.h, vbb(e - 1) + 28)
                    mi = te.matmul(PBk[2][:, 0:512], onesrt[0:1, :], b2row[0:1, 0:512],
                              start=True, stop=True)
                    mi = te.matmul(PBk[3][:, 0:512], onesrt[0:1, :], b2row[0:1, 512:1024],
                              start=True, stop=True)
                    sMM.inc(mi)
                    te.wait_ge(*gdone(e))
                    te.wait_ge(sV.h, vbb(e) + 1)
                    for qd in range(6):
                        blk_i, q = qd // 2, qd % 2
                        if qd >= 2:
                            te.wait_ge(sV.h, vbb(e) + 3 + (qd - 2) + 1)
                        for k4 in range(4):
                            kc = 4 * q + k4
                            mi = te.matmul(PBk[qd % 2][:, k4 * 128:(k4 + 1) * 128],
                                      tgbuf(e)[:, blk_i * XGW + kc * 128:
                                               blk_i * XGW + (kc + 1) * 128],
                                      identt, start=(k4 == 0), stop=(k4 == 3),
                                      is_transpose=True)
                        sMM.inc(mi)
                    for g in range(8):
                        sl = g % 2
                        bk = (4 if g % 2 == 0 else 0)
                        te.wait_ge(WS[sl].h, w1base[sl] + 16 * (g // 2 + 1 + (4 * e)))
                        if g >= 2:
                            te.wait_ge(sV.h, vbb(e) + 9 + 2 * (g - 2) + 2)
                            te.wait_ge(sA.h, abb(e) + 2 * (g - 2) + 2)
                        elif g == 0 and e >= 1:
                            te.wait_ge(sV.h, vbb(e - 1) + 32)
                        elif g == 0:
                            te.wait_ge(sV.h, p9_done_v)
                        if g < 2:
                            te.wait_ge(sV.h, vbb(e) + 9)
                        for m4 in range(4):
                            for kc in range(KC):
                                mi = te.matmul(PBk[bk + m4][:, 0:CAPF],
                                          w1r[sl][:, kc * 512 + m4 * 128:
                                                  kc * 512 + (m4 + 1) * 128],
                                          toksT[:, kc * CAP:kc * CAP + CAPF],
                                          start=(kc == 0), stop=(kc == KC - 1))
                        sMM.inc(mi)
                    te.wait_ge(sV.h, vbb(e) + 25)
                    te.wait_ge(sA.h, abb(e) + A_E)
                    for m in range(32):
                        sl = (m // 2) % 2
                        te.wait_ge(VS[sl].h, w2base[sl] + 16 * (m // 4 + 1 + 8 * e))
                        for blk_i in range(3):
                            tw = 128 if blk_i < 2 else CAPF - 256
                            for half in range(2):
                                mi = te.matmul(PBk[2 + blk_i * 2 + half][0:tw, 0:512],
                                          h1t[:, m * CAP + blk_i * 128:
                                              m * CAP + blk_i * 128 + tw],
                                          w2r[sl][:, (m % 2) * 1024 + half * 512:
                                                  (m % 2) * 1024 + (half + 1) * 512],
                                          start=(m == 0), stop=(m == 31))
                        sMM.inc(mi)

            @blk.sync
            def _(sy):
                sy.wait_ge(sMM.h, p8_done_mm)
                for e in range(EXP):
                    if e >= 1:
                        sy.wait_ge(sMM.h, mmb(e - 1) + 2)
                    sIn.dma(sy.dma_start(out=b1tmp,
                                         in_=b1_d[e:e + 1, :].rearrange(
                                             "o (m n) -> (o m) n", n=128)))
                    sIn.dma(sy.dma_start(out=b2row, in_=b2_d[e:e + 1, :]))
                    for g in range(8):
                        sl = g % 2
                        gu = e * 8 + g
                        if gu >= 2:
                            ep, gp_ = (gu - 2) // 8, (gu - 2) % 8
                            sy.wait_ge(sMM.h, mmb(ep) + 9 + gp_)
                        sW = WS[sl]
                        sW.dma(sy.dma_start(
                            out=A3(w1r[sl].offset, [[512, KC], [1, 512]]),
                            in_=bass.AP(w1_d.tensor, e * D * DFF + g * 512,
                                        [[DFF, 128], [128 * DFF, KC], [1, 512]])))
                    for mp in range(16):
                        sl = mp % 2
                        mu = e * 16 + mp
                        if mu >= 2:
                            ep, mp_ = (mu - 2) // 16, (mu - 2) % 16
                            te_i = 16 + (2 * mp_ + 1) + 1
                            sy.wait_ge(sMM.h, mmb(ep) + te_i)
                        sy2 = VS[sl]
                        sy2.dma(sy.dma_start(
                            out=A3(w2r[sl].offset, [[1024, 2], [1, 1024]]),
                            in_=bass.AP(w2_d.tensor, e * DFF * D + mp * 2 * 128 * D,
                                        [[D, 128], [128 * D, 2], [1, D]])))
        p10_done_v = sV.n
        p10_done_gd = sGd.n
        p10_done_gs = sGs.n

        p11_v = sV.n
        p11_a = sA.n
        FG3 = [sXa, sXb, sXc]
        FS3 = [sW0, sW1, sWc]
        FX3 = [sV0, sV1, sVc]
        g3base = [sXa.n, sXb.n, sXc.n]
        s3base = [sW0.n, sW1.n, sWc.n]
        x3base = [sV0.n, sV1.n, sVc.n]
        OUT3 = [sOut, sOutB, sOutC]
        outbase3 = [sOut.n, sOutB.n, sOutC.n]
        if True:
            @blk.vector
            def _(v):
                for tt in range(16):
                    sl = tt % 3
                    v.wait_ge(sA.h, p11_a + tt + 1)
                    v.wait_ge(FS3[sl].h, s3base[sl] + 16 * (tt // 3 + 1))
                    v.wait_ge(FX3[sl].h, x3base[sl] + 16 * (tt // 3 + 1))
                    sV.inc(v.tensor_tensor(fs[sl], fs[sl], fsc, Alu.mult))
                    v.wait_ge(sV.h, sV.n)
                    sV.inc(v.tensor_tensor(fx[sl], fx[sl], fs[sl], Alu.add))

            @blk.scalar
            def _(sc):
                sc.wait_ge(sS1.h, s_dma_done[1])
                for tt in range(16):
                    sl = tt % 3
                    sc.wait_ge(FG3[sl].h, g3base[sl] + 16 * (tt // 3 + 1))
                    if tt >= 1:
                        sc.wait_ge(sV.h, p11_v + 2 * (tt - 1) + 1)
                    sA.inc(sc.activation(fsc, fg[sl], Act.Silu))

            @blk.sync
            def _(sy):
                sy.wait_ge(sGd.h, p10_done_gd)
                sy.wait_ge(sGd1.h, sGd1.n)
                sy.wait_ge(sGs.h, p10_done_gs)
                sy.wait_ge(sS0.h, s_dma_done[0])
                sy.wait_ge(sS1.h, s_dma_done[1])
                for tt in range(18):
                    if tt >= 2:
                        tp = tt - 2
                        sy.wait_ge(sV.h, p11_v + 2 * tp + 2)
                        OUT3[tp % 3].dma(sy.dma_start(
                            out=y_out[tp * 128:(tp + 1) * 128, :], in_=fx[tp % 3]))
                    if tt < 16:
                        sl = tt % 3
                        if tt >= 3:
                            sy.wait_ge(sA.h, p11_a + (tt - 3) + 1)
                            sy.wait_ge(sV.h, p11_v + 2 * (tt - 3) + 2)
                            sy.wait_ge(OUT3[(tt - 3) % 3].h,
                                       outbase3[(tt - 3) % 3]
                                       + 16 * ((tt - 3) // 3 + 1))
                        FG3[sl].dma(sy.dma_start(
                            out=fg[sl], in_=g_stage[tt * 128:(tt + 1) * 128, :]))
                        FS3[sl].dma(sy.dma_start(
                            out=fs[sl], in_=s_stage[tt * 128:(tt + 1) * 128, :]))
                        FX3[sl].dma(sy.dma_start(
                            out=fx[sl], in_=x_att[8 + tt * 128: 8 + (tt + 1) * 128, :]))
                sy.wait_ge(sOut.h, sOut.n)
                sy.wait_ge(sOutB.h, sOutB.n)
                sy.wait_ge(sOutC.h, sOutC.n)

    lower_extended_insts(nc)
    return nc


def _consts():
    c = {}
    c["ident"] = np.eye(128, dtype=np.float32)
    band = np.zeros((128, 128), dtype=np.float32)
    for p_ in range(128):
        for cc in range(128):
            if not (0 <= cc - p_ <= 8):
                band[p_, cc] = -1e30
    c["maskA"] = band
    m0 = band.copy()
    m0[:, 0:8] = -1e30
    c["mask0"] = m0
    io = np.zeros((128, 16), dtype=np.float32)
    for p_ in range(128):
        for jj in range(16):
            io[p_, jj] = p_ + 128 * jj + 1
    c["iota16"] = io
    isl = np.zeros((16, CAPW), dtype=np.float32)
    for j in range(CAP):
        isl[j % 16, j // 16] = float(j)
    c["ioslot"] = isl
    bd = np.zeros((128, EXP), dtype=np.float32)
    for q in range(128):
        bd[q, q % EXP] = 1.0
    c["bd8"] = bd
    c["bdT"] = np.ascontiguousarray(bd.T)
    c["onesrow"] = np.ones((1, 128), dtype=np.float32)
    return c


def kernel(L0_attn_x, L0_attn_Wq, L0_attn_Wk, L0_attn_Wv, L0_attn_Wo,
           L0_ffn_router_w, L0_ffn_router_b, L0_ffn_w1, L0_ffn_b1,
           L0_ffn_w2, L0_ffn_b2, _debug_taps=()):
    key = tuple(sorted(_debug_taps))
    if key not in _CACHE:
        _CACHE[key] = build_program(key)
    nc = _CACHE[key]

    x = np.ascontiguousarray(np.asarray(L0_attn_x, dtype=np.float32)).reshape(16384, D)
    consts = _consts()
    shared = {
        "Wq": np.ascontiguousarray(L0_attn_Wq, dtype=np.float32),
        "Wk": np.ascontiguousarray(L0_attn_Wk, dtype=np.float32),
        "Wv": np.ascontiguousarray(L0_attn_Wv, dtype=np.float32),
        "Wo": np.ascontiguousarray(L0_attn_Wo, dtype=np.float32),
        "rw": np.ascontiguousarray(L0_ffn_router_w, dtype=np.float32),
        "rbias": np.ascontiguousarray(L0_ffn_router_b, dtype=np.float32).reshape(1, EXP),
        "w1": np.ascontiguousarray(L0_ffn_w1, dtype=np.float32),
        "b1": np.ascontiguousarray(L0_ffn_b1, dtype=np.float32),
        "w2": np.ascontiguousarray(L0_ffn_w2, dtype=np.float32),
        "b2": np.ascontiguousarray(L0_ffn_b2, dtype=np.float32),
        "ident": consts["ident"], "maskA": consts["maskA"],
        "iota16": consts["iota16"], "ioslot": consts["ioslot"],
        "bd8": consts["bd8"], "bdT": consts["bdT"], "onesrow": consts["onesrow"],
    }
    in_maps = []
    for c in range(8):
        start = c * TOK
        xatt = np.zeros((TA, D), dtype=np.float32)
        s0 = max(0, start - 8)
        nh = start - s0
        xatt[8 - nh:8] = x[s0:start]
        xatt[8:8 + TOK] = x[start:start + TOK]
        m = dict(shared)
        m["x_att"] = xatt
        xr = np.roll(x, -start, axis=0)
        m["x_full_T"] = np.ascontiguousarray(xr.T)
        m["mask0"] = consts["mask0"] if c in (0, 4) else consts["maskA"]
        in_maps.append(m)
    res = run_bass_kernel_spmd(nc, in_maps, core_ids=list(range(8)))
    outs = [res.results[c]["y"] for c in range(8)]
    y = np.concatenate(outs, axis=0).reshape(2, 8192, D).astype(np.float32)
    if _debug_taps:
        kernel._tap_results = res.results
    return y
